# version 06
# baseline (speedup 1.0000x reference)
import numpy as np
from contextlib import ExitStack
import concourse.bass as bass
import concourse.mybir as mybir
from concourse.bass_utils import run_bass_kernel_spmd

F32 = mybir.dt.float32
BF16 = mybir.dt.bfloat16
AF = mybir.ActivationFunctionType
ALU = mybir.AluOpType
AX = mybir.AxisListType
NCORES = 8
D = 1024
KD = 8
TOK = 2048
EPS = 1e-5
CHAINED = ("vector", "gpsimd", "scalar")
PAIR_GROUPS = [[0, 1], [2, 3], [4, 5], [6, 7]]


class Prog:

    def __init__(self, nc):
        self.nc = nc
        self.ops = []
        self.self_sems = {e: nc.alloc_semaphore("self_" + e) for e in CHAINED}
        self.k = {e: 0 for e in CHAINED}
        self.all_sems = list(self.self_sems.values())
        self.pending_pre = {}

    def sem(self, name):
        h = self.nc.alloc_semaphore(name)
        self.all_sems.append(h)
        return h

    def set_phase_pre(self, sig):
        for e in ("vector", "gpsimd", "scalar", "tensor", "sync"):
            self.pending_pre[e] = sig

    def op(self, eng, fn, waits=(), incs=(), setup=False, dma=False):
        waits = [w for w in waits if w is not None]
        if not setup and eng in self.pending_pre:
            waits.append(self.pending_pre.pop(eng))
        sig = None
        chain_wait = None
        if eng in CHAINED and not dma:
            assert not incs, "chained ops cannot carry explicit incs"
            if self.k[eng] > 0:
                chain_wait = (self.self_sems[eng], self.k[eng])
            self.k[eng] += 1
            sig = (self.self_sems[eng], self.k[eng])
            incs = [(self.self_sems[eng], 1)]
        else:
            assert len(incs) <= 1
        self.ops.append((eng, fn, tuple(waits), tuple(incs), setup, chain_wait))
        return sig

    def emit(self, setup_sem=None, setup_val=0):
        nc = self.nc
        with nc.Block() as block:
            for name, meth in (("sync", block.sync), ("gpsimd", block.gpsimd),
                               ("scalar", block.scalar), ("vector", block.vector),
                               ("tensor", block.tensor)):
                def body(eng, name=name):
                    first = True
                    for e, fn, waits, incs, setup, chain_wait in self.ops:
                        if e != name:
                            continue
                        if first and not setup and setup_sem is not None:
                            eng.wait_ge(setup_sem, setup_val)
                        if not setup:
                            first = False
                        if chain_wait is not None:
                            eng.wait_ge(*chain_wait)
                        for (h, v) in waits:
                            if v > 0:
                                eng.wait_ge(h, v)
                        ins = fn(eng)
                        for (h, kk) in incs:
                            ins = ins.then_inc(h, kk)
                meth(body)
        nc.all_engine_barrier()
        nc.clear_and_free_semaphores(self.all_sems)
        nc.all_engine_barrier()


class Consts:

    def __init__(self, nc):
        import ml_dtypes
        self._ident_dram = nc.inline_tensor(
            np.eye(128, dtype=np.float32).astype(ml_dtypes.bfloat16),
            name="ident_bf").ap()
        self.nc = nc

    def materialize(self, p, st, extra_dram=()):
        nc = self.nc
        sbt = lambda name, shp, dt: st.enter_context(nc.sbuf_tensor(name, shp, dt))
        self.ident_bf = sbt("c_ident", [128, 128], BF16)[:]
        self.ident_f32 = sbt("c_identf", [128, 128], F32)[:]
        self.ones_col = sbt("c_ones_col", [128, 1], F32)[:]
        self.ones_row = sbt("c_ones_row", [1, 128], F32)[:]
        self.s_const = p.sem("s_const")
        s_id = p.sem("s_id")
        n = 0
        p.op("sync", lambda e: e.dma_start(out=self.ident_bf, in_=self._ident_dram),
             incs=[(s_id, 16)], setup=True)
        p.op("scalar", lambda e: e.copy(self.ident_f32, self.ident_bf),
             incs=[(self.s_const, 17)], setup=True, dma=True,
             waits=[(s_id, 16)])
        n += 17
        p.op("gpsimd", lambda e: e.memset(self.ones_col, 1.0), setup=True, dma=True,
             incs=[(self.s_const, 1)])
        n += 1
        p.op("gpsimd", lambda e: e.memset(self.ones_row, 1.0), setup=True, dma=True,
             incs=[(self.s_const, 1)])
        n += 1
        self.extra = {}
        for (name, dram_ap, shp, dt) in extra_dram:
            t = sbt("c_" + name, shp, dt)
            self.extra[name] = t
            p.op("sync", lambda e, t=t, dram_ap=dram_ap:
                 e.dma_start(out=t[:], in_=dram_ap),
                 incs=[(self.s_const, 16)], setup=True)
            n += 16
        self.ready_val = n


class Ctx:

    def __init__(self, nc, p, c):
        self.nc, self.p, self.c = nc, p, c
        self.sems = {}
        self.cnt = {}
        self.mmc = {"n": 0}
        self.tqc = {"n": 0}
        self.ldc = {"n": 0}
        self.nstp = [0, 0]
        self.nph = 0

    def S(self, name):
        if name not in self.sems:
            self.sems[name] = self.p.sem(name)
        return self.sems[name]

    def bump(self, name, v=16):
        self.cnt[name] = self.cnt.get(name, 0) + v
        return self.cnt[name]

    def val(self, name):
        return self.cnt.get(name, 0)

    def join(self):
        p = self.p
        w = [(self.S("s_st0"), 16 * self.nstp[0]),
             (self.S("s_st1"), 16 * self.nstp[1]),
             (self.S("s_mm"), self.mmc["n"]),
             (self.S("s_tq"), self.tqc["n"])]
        for e in CHAINED:
            w.append((p.self_sems[e], p.k[e]))
        self.nph += 1
        p.op("sync", lambda e: e.nop(), waits=w, incs=[(self.S("s_ph"), 1)])
        sig = (self.S("s_ph"), self.nph)
        p.set_phase_pre(sig)
        return sig


def emit_ln_fm(p, c, xg, xsq, xln, stats_ps, mb_ps, rb_ps, rowbuf, scr,
               s_mm, mmc, ncols, nch=KD, eps=EPS, pre_waits=(), xln_stride=None,
               xln_off=0):
    W = ncols
    mrow = rowbuf[0:1, 0:W]
    vrow = rowbuf[0:1, W:2 * W]
    rrow = rowbuf[0:1, 2 * W:3 * W]
    sq_sig = p.op("scalar", lambda e: e.activation(xsq[:, 0:nch * W],
                                                   xg[:, 0:nch * W], AF.Square),
                  waits=pre_waits)
    for cc in range(nch):
        p.op("tensor",
             lambda e, cc=cc: e.matmul(stats_ps[0:1, 0:W], c.ones_col,
                                       xg[:, cc * W:(cc + 1) * W],
                                       start=(cc == 0), stop=(cc == nch - 1)),
             waits=(pre_waits if cc == 0 else ()))
    for cc in range(nch):
        p.op("tensor",
             lambda e, cc=cc: e.matmul(stats_ps[32:33, 0:W], c.ones_col,
                                       xsq[:, cc * W:(cc + 1) * W],
                                       start=(cc == 0), stop=(cc == nch - 1)),
             waits=([sq_sig] if cc == 0 else ()),
             incs=([(s_mm, 1)] if cc == nch - 1 else ()))
    mmc["n"] += 1
    stat_w = [(s_mm, mmc["n"])]
    p.op("vector", lambda e: e.tensor_scalar(mrow, stats_ps[0:1, 0:W],
                                             1.0 / (nch * 128.0), None, ALU.mult),
         waits=stat_w)
    p.op("vector", lambda e: e.tensor_mul(vrow, mrow, mrow))
    p.op("vector", lambda e: e.scalar_tensor_tensor(vrow, stats_ps[32:33, 0:W],
                                                    1.0 / (nch * 128.0), vrow,
                                                    ALU.mult, ALU.subtract))
    veps = p.op("vector", lambda e: e.tensor_scalar(vrow, vrow, eps, None, ALU.add))
    p.op("scalar", lambda e: e.activation(rrow, vrow, AF.Ln), waits=[veps])
    rsig = p.op("scalar", lambda e: e.activation(rrow, rrow, AF.Exp, scale=-0.5))
    p.op("tensor", lambda e: e.matmul(mb_ps[:, 0:W], c.ones_row, mrow,
                                      start=True, stop=True),
         waits=[rsig])
    p.op("tensor", lambda e: e.matmul(rb_ps[:, 0:W], c.ones_row, rrow,
                                      start=True, stop=True),
         incs=[(s_mm, 1)])
    mmc["n"] += 1
    bc_w = [(s_mm, mmc["n"])]
    xst = W if xln_stride is None else xln_stride
    last = None
    for cc in range(nch):
        p.op("vector", lambda e, cc=cc: e.tensor_sub(
                scr[:, 0:W], xg[:, cc * W:(cc + 1) * W], mb_ps[:, 0:W]),
             waits=(bc_w if cc == 0 else ()))
        last = p.op("vector", lambda e, cc=cc: e.tensor_mul(
                xln[:, cc * xst + xln_off:cc * xst + xln_off + W], scr[:, 0:W],
                rb_ps[:, 0:W]))
    return last


def emit_wprep_transposed(p, c, specs, stag, sgn, pq, ld_slab, ldc, s_tq, tqc,
                          first_evac_waits=()):
    red_sig = None
    sgn_sig = None
    evac_sig = None
    alpha_sigs = []
    first = True
    for (whbm, nslab, scols, nq4, fdim, wT, alp, use_sign) in specs:
        nqq = nq4 // 4
        for m in range(nslab):
            dw = [red_sig, sgn_sig]
            if first:
                dw += list(first_evac_waits)
            p.op("sync", lambda e, whbm=whbm, m=m, scols=scols:
                 e.dma_start(out=stag[:, 0:scols],
                             in_=whbm[m * 128:(m + 1) * 128, :]),
                 waits=dw, incs=[(ld_slab, 16)])
            ldc["n"] += 16
            ldw = [(ld_slab, ldc["n"])]
            if alp is not None:
                red_sig = p.op("vector", lambda e, m=m, scols=scols, alp=alp:
                               e.tensor_reduce(alp[:, m:m + 1], stag[:, 0:scols],
                                               AX.X, ALU.add,
                                               apply_absolute_value=True),
                               waits=ldw)
                alpha_sigs.append(red_sig)
            tq_w = [(s_tq, tqc["n"])]
            if use_sign:
                sgn_sig = p.op("scalar", lambda e, scols=scols:
                               e.sign(sgn[:, 0:scols], stag[:, 0:scols]),
                               waits=ldw + tq_w)
            else:
                sgn_sig = p.op("scalar", lambda e, scols=scols:
                               e.copy(sgn[:, 0:scols], stag[:, 0:scols]),
                               waits=ldw + tq_w)
            for q in range(nqq):
                for j in range(4):
                    kc = q * 4 + j
                    wts = []
                    if j == 0:
                        wts = [sgn_sig, evac_sig]
                        if first:
                            wts += list(first_evac_waits)
                    p.op("tensor", lambda e, j=j, kc=kc:
                         e.transpose(pq[:, j * 128:(j + 1) * 128],
                                     sgn[:, kc * 128:(kc + 1) * 128], c.ident_bf),
                         waits=wts,
                         incs=([(s_tq, 1)] if j == 3 else []))
                tqc["n"] += 1
                first = False
                evac_sig = p.op("vector", lambda e, q=q, m=m, wT=wT, fdim=fdim:
                                e.tensor_copy(
                                    wT.rearrange("p (kc f) -> p kc f", f=fdim)
                                      [:, q * 4:q * 4 + 4, m * 128:(m + 1) * 128],
                                    pq[:, 0:512].rearrange("p (a f) -> p a f", a=4)),
                                waits=[(s_tq, tqc["n"])])
    return evac_sig, alpha_sigs


EXTL = 2304


def emit_l0attn(X, st, x, wq, wk, wv, wo, kinit, vinit, maskc, y):
    nc, p, c = X.nc, X.p, X.c
    PW = 384
    NPG = 6
    ld_slab = X.S("ld_slab")
    ld_m = X.S("ld_m")
    ld_msw = X.S("ld_msw")
    ldx = X.S("ldx")
    ldr = X.S("ldr")
    s_tq = X.S("s_tq")
    s_mm = X.S("s_mm")
    s_st = [X.S("s_st0"), X.S("s_st1")]
    nstp = X.nstp
    mmc, tqc, ldc = X.mmc, X.tqc, X.ldc
    mfl = c.extra["mflag"]

    sb = lambda name, shp, dt: st.enter_context(nc.sbuf_tensor(name, shp, dt))
    ps = lambda name, shp, dt: st.enter_context(nc.psum_tensor(name, shp, dt))
    xln_ext = sb("a_xln_ext", [128, KD * EXTL], BF16)
    wqT = sb("a_wqT", [128, KD * D], BF16)
    wkT = sb("a_wkT", [128, KD * D], BF16)
    wvT = sb("a_wvT", [128, KD * D], BF16)
    stag = sb("a_stag", [128, 1024], F32)
    sgn = sb("a_sgn", [128, 1024], BF16)
    kin_s = sb("a_kin_s", [128, 8 * 63], BF16)
    vin_s = sb("a_vin_s", [128, 8 * 63], BF16)
    mk = sb("a_mk", [128, 256], F32)
    scr = sb("a_scr", [128, 768], F32)
    rowbuf = sb("a_rowbuf", [1, 3 * PW], F32)
    q_g = sb("a_q_g", [128, 2 * EXTL], BF16)
    k_g = sb("a_k_g", [128, 2 * EXTL], BF16)
    v_g = sb("a_v_g", [128, 2 * EXTL], BF16)
    v_tm = sb("a_v_tm", [128, 17 * 256], BF16)
    scf = sb("a_scf", [128, 512], F32)
    attn_n = sb("a_attn_n", [128, 512], BF16)
    attnT = sb("a_attnT", [128, 512], BF16)
    nmx = sb("a_nmx", [128, 4], F32)
    smr = sb("a_smr", [128, 8], F32)
    yt = sb("a_yt", [128, 2 * 512], F32)
    xr = sb("a_xr", [128, 2 * 512], F32)
    ao_fm = sb("a_ao_fm", [128, KD * TOK], BF16)
    stats_ps = ps("a_stats", [128, 512], F32)
    mb_ps = ps("a_mb_ps", [128, 512], F32)
    rb_ps = ps("a_rb_ps", [128, 512], F32)
    mm = [ps("a_mmA", [128, 512], F32), ps("a_mmB", [128, 512], F32)]
    sc_ps = ps("a_scA", [128, 512], F32)
    pq = ps("a_pqA", [128, 1024], BF16)

    LDMSW = X.val("ld_msw")
    LDM = X.val("ld_m")
    for (eng, sem, nm, dst, src) in (("gpsimd", ld_msw, "ld_msw", kin_s, kinit),
                                     ("gpsimd", ld_msw, "ld_msw", vin_s, vinit),
                                     ("sync", ld_m, "ld_m", mk, maskc)):
        p.op(eng, lambda e, dst=dst, src=src: e.dma_start(out=dst[:], in_=src),
             incs=[(sem, 16)], dma=True)
        X.bump(nm)
    LDMSW += 32
    LDM += 16

    nldx = X.val("ldx")
    ln_sig = None
    with nc.sbuf_tensor("a_xg0", [128, KD * PW], F32, side="right") as xg, \
         nc.sbuf_tensor("a_xsq0", [128, KD * PW], F32, side="right") as xsq:
        for g in range(NPG):
            t0 = g * PW
            p.op("sync", lambda e, t0=t0:
                 e.dma_start(out=xg.rearrange("p (c t) -> p c t", c=KD),
                             in_=x[:, t0:t0 + PW]
                                 .rearrange("(c p) t -> c p t", p=128)
                                 .rearrange("c p t -> p c t")),
                 waits=[ln_sig], incs=[(ldx, 16)])
            nldx = X.bump("ldx")
            ln_sig = emit_ln_fm(p, c, xg, xsq, xln_ext, stats_ps, mb_ps, rb_ps,
                                rowbuf, scr, s_mm, mmc, PW,
                                pre_waits=[(ldx, nldx)],
                                xln_stride=EXTL, xln_off=t0)

    wp_sig, _ = emit_wprep_transposed(
        p, c, [(wq, 8, D, 8, D, wqT, None, False),
               (wk, 8, D, 8, D, wkT, None, False),
               (wv, 8, D, 8, D, wvT, None, False)],
        stag, sgn, pq, ld_slab, ldc, s_tq, tqc)

    SCALE = float(1.0 / np.sqrt(64.0))
    ev_sigs = {}
    aolast = None
    pmm = 0
    for hg in range(4):
        mc0 = 2 * hg
        proj_last = None
        for (wTb, dest) in ((wqT, q_g), (wkT, k_g), (wvT, v_g)):
            for mcl in range(2):
                mc = mc0 + mcl
                for g in range(NPG):
                    par = pmm % 2
                    for kc in range(KD):
                        wts = []
                        if kc == 0:
                            wts = [wp_sig, ln_sig, ev_sigs.get(par)]
                        p.op("tensor", lambda e, par=par, mc=mc, g=g, kc=kc,
                             wTb=wTb:
                             e.matmul(mm[par][:, 0:PW],
                                      wTb.rearrange("p (kc f) -> p kc f", f=D)
                                         [:, kc, mc * 128:(mc + 1) * 128],
                                      xln_ext[:, kc * EXTL + g * PW:
                                              kc * EXTL + (g + 1) * PW],
                                      start=(kc == 0), stop=(kc == KD - 1)),
                             waits=wts,
                             incs=([(s_mm, 1)] if kc == KD - 1 else []))
                    mmc["n"] += 1
                    pmm += 1
                    ev_sigs[par] = p.op(
                        "scalar", lambda e, par=par, mcl=mcl, g=g, dest=dest:
                        e.activation(dest[:, mcl * EXTL + g * PW:
                                          mcl * EXTL + (g + 1) * PW],
                                     mm[par][:, 0:PW], AF.Copy),
                        waits=[(s_mm, mmc["n"])])
                    proj_last = ev_sigs[par]
        bs = None
        for (dst, ini) in ((k_g, kin_s), (v_g, vin_s)):
            for mcl in range(2):
                mc = mc0 + mcl
                p.op("vector", lambda e, ini=ini, mc=mc: e.tensor_scalar(
                        scr[:, 0:63], ini[:, mc * 63:(mc + 1) * 63],
                        mfl[:, 0:1], None, ALU.mult),
                     waits=([proj_last, (ld_msw, LDMSW), (ld_m, LDM)]
                            if bs is None else []))
                bs = p.op("vector", lambda e, dst=dst, mcl=mcl:
                          e.scalar_tensor_tensor(
                              dst[:, mcl * EXTL:mcl * EXTL + 63],
                              dst[:, mcl * EXTL:mcl * EXTL + 63],
                              mfl[:, 1:2], scr[:, 0:63], ALU.mult, ALU.add))
        blend_sig = bs

        vq_sig = None
        for t in range(17):
            for mcl in range(2):
                wts = []
                if mcl == 0:
                    wts = [blend_sig, vq_sig]
                p.op("tensor", lambda e, t=t, mcl=mcl:
                     e.transpose(pq[:, mcl * 128:(mcl + 1) * 128],
                                 v_g[:, mcl * EXTL + 128 * t:
                                      mcl * EXTL + 128 * (t + 1)],
                                 c.ident_bf),
                     waits=wts,
                     incs=([(s_tq, 1)] if mcl == 1 else []))
            tqc["n"] += 1
            vq_sig = p.op("vector", lambda e, t=t:
                          e.tensor_copy(v_tm[:, 256 * t:256 * (t + 1)],
                                        pq[:, 0:256]),
                          waits=[(s_tq, tqc["n"])])

        aolast_sc = [None]
        sc_free = {0: None, 1: None}
        at_free = {0: vq_sig, 1: vq_sig}
        o_free = {0: [None, None], 1: [None, None]}
        sc_mm = {}

        def emit_score(ht):
            odd = ht % 2
            pp2 = ht % 2
            P = 256 * pp2
            t = ht // 2
            kwin = 128 * t + 64 * odd
            q0 = 63 + 128 * t + 64 * odd
            for hl in range(4):
                par = hl % 2
                qq = hl // 2
                mcl = hl // 2
                wts = []
                if hl == 0:
                    wts = [sc_free[pp2], blend_sig]
                p.op("tensor", lambda e, par=par, qq=qq, mcl=mcl,
                     kwin=kwin, q0=q0, P=P:
                     e.matmul(sc_ps[64 * par:64 * par + 64,
                                    P + qq * 128:P + (qq + 1) * 128],
                              q_g[64 * par:64 * par + 64,
                                  mcl * EXTL + q0:mcl * EXTL + q0 + 64],
                              k_g[64 * par:64 * par + 64,
                                  mcl * EXTL + kwin:mcl * EXTL + kwin + 128],
                              start=True, stop=True,
                              tile_position=(64 * par, 64 * par)),
                     waits=wts,
                     incs=([(s_mm, 1)] if hl == 3 else []))
            mmc["n"] += 1
            sc_mm[ht] = mmc["n"]

        def emit_soft(ht):
            nonlocal aolast
            odd = ht % 2
            pp2 = ht % 2
            P = 256 * pp2
            AVP = 128 * pp2
            PS = 256 + 256 * pp2
            t = ht // 2
            ra0 = 128 * t + 64 * odd
            sc_free[pp2] = p.op("vector", lambda e, P=P:
                 e.scalar_tensor_tensor(
                    scf[:, P:P + 256], sc_ps[:, P:P + 256], SCALE, mk[:],
                    ALU.mult, ALU.add),
                 waits=[(s_mm, sc_mm[ht])])
            nm_sig = p.op("vector", lambda e, P=P, pp2=pp2: e.tensor_reduce(
                    nmx[:, 2 * pp2:2 * pp2 + 1],
                    scf[:, P:P + 256],
                    AX.X, ALU.max, negate=True))
            ex_sig = p.op("scalar", lambda e, P=P, pp2=pp2:
                 e.activation(
                    scf[:, P:P + 256],
                    scf[:, P:P + 256],
                    AF.Exp, bias=nmx[:, 2 * pp2:2 * pp2 + 1]),
                 waits=[nm_sig])
            p.op("vector", lambda e, P=P, pp2=pp2: e.tensor_reduce(
                    smr[:, 4 * pp2:4 * pp2 + 2],
                    scf[:, P:P + 256].rearrange("p (a t) -> p a t", a=2),
                    AX.X, ALU.add),
                 waits=[ex_sig])
            p.op("vector", lambda e, pp2=pp2: e.reciprocal(
                    smr[:, 4 * pp2 + 2:4 * pp2 + 4],
                    smr[:, 4 * pp2:4 * pp2 + 2]))
            nops = []
            for qq in range(2):
                if odd == 0:
                    nops.append((qq * 128, qq * 128, 128, qq))
                else:
                    nops.append((qq * 128, qq * 128 + 64, 64, qq))
                    nops.append((qq * 128 + 64, qq * 128, 64, qq))
            an_sig = None
            for (so, do, wd, qq) in nops:
                an_sig = p.op("vector",
                     lambda e, so=so, do=do, wd=wd, qq=qq, P=P, pp2=pp2:
                     e.tensor_scalar(attn_n[:, P + do:P + do + wd],
                                     scf[:, P + so:P + so + wd],
                                     smr[:, 4 * pp2 + 2 + qq:
                                         4 * pp2 + 3 + qq], None, ALU.mult))
            for qq in range(2):
                p.op("tensor", lambda e, qq=qq, P=P: e.transpose(
                        pq[:, P + qq * 128:P + (qq + 1) * 128],
                        attn_n[:, P + qq * 128:P + (qq + 1) * 128],
                        c.ident_bf),
                     waits=([an_sig, at_free[pp2]] if qq == 0 else ()),
                     incs=([(s_tq, 1)] if qq == 1 else []))
            tqc["n"] += 1
            at_free[pp2] = p.op("scalar", lambda e, P=P: e.copy(
                    attnT[:, P:P + 256], pq[:, P:P + 256]),
                 waits=[(s_tq, tqc["n"])])
            for hl in range(4):
                par = hl % 2
                qq = hl // 2
                wts = []
                if hl == 0:
                    wts = [at_free[pp2], o_free[pp2][0], o_free[pp2][1]]
                if odd == 0:
                    p.op("tensor", lambda e, hl=hl, par=par, qq=qq, t=t,
                         P=P, AVP=AVP:
                         e.matmul(mm[0][64 * par:64 * par + 64,
                                        AVP + qq * 64:AVP + (qq + 1) * 64],
                                  v_tm[:, 256 * t + 64 * hl:
                                       256 * t + 64 * (hl + 1)],
                                  attnT[:, P + qq * 128 + 64 * par:
                                        P + qq * 128 + 64 * (par + 1)],
                                  start=True, stop=True),
                         waits=wts,
                         incs=([(s_mm, 1)] if hl == 3 else []))
                else:
                    p.op("tensor", lambda e, hl=hl, par=par, qq=qq, t=t,
                         P=P, AVP=AVP:
                         e.matmul(mm[0][64 * par:64 * par + 64,
                                        AVP + qq * 64:AVP + (qq + 1) * 64],
                                  v_tm[64:128, 256 * t + 64 * hl:
                                       256 * t + 64 * (hl + 1)],
                                  attnT[64:128, P + qq * 128 + 64 * par:
                                        P + qq * 128 + 64 * (par + 1)],
                                  start=True, stop=True),
                         waits=wts)
                    p.op("tensor", lambda e, hl=hl, par=par, qq=qq, t=t,
                         P=P, AVP=AVP:
                         e.matmul(mm[1][64 * par:64 * par + 64,
                                        AVP + qq * 64:AVP + (qq + 1) * 64],
                                  v_tm[0:64, 256 * (t + 1) + 64 * hl:
                                       256 * (t + 1) + 64 * (hl + 1)],
                                  attnT[0:64, P + qq * 128 + 64 * par:
                                        P + qq * 128 + 64 * (par + 1)],
                                  start=True, stop=True),
                         incs=([(s_mm, 1)] if hl == 3 else []))
            mmc["n"] += 1
            cpsig = None
            if odd == 1:
                cpsig = p.op("scalar", lambda e, PS=PS, AVP=AVP: e.copy(
                        scr[:, PS:PS + 256], mm[0][:, AVP:AVP + 256]),
                     waits=[(s_mm, mmc["n"])])
                aolast_sc[0] = cpsig
            for par in range(2):
                if odd == 0:
                    o_free[pp2][par] = p.op("scalar",
                         lambda e, par=par, ra0=ra0, mc0=mc0, AVP=AVP:
                         e.copy(
                             ao_fm.rearrange("p (cc t) -> p cc t", cc=KD)
                                  [64 * par:64 * par + 64, mc0:mc0 + 2,
                                   ra0:ra0 + 64],
                             mm[0][64 * par:64 * par + 64, AVP:AVP + 128]
                             .rearrange("p (a t) -> p a t", a=2)),
                         waits=[(s_mm, mmc["n"])])
                    aolast_sc[0] = o_free[pp2][par]
                else:
                    o_free[pp2][par] = p.op("vector",
                         lambda e, par=par, ra0=ra0, mc0=mc0, PS=PS, AVP=AVP:
                         e.tensor_add(
                             ao_fm.rearrange("p (cc t) -> p cc t", cc=KD)
                                  [64 * par:64 * par + 64, mc0:mc0 + 2,
                                   ra0:ra0 + 64],
                             scr[64 * par:64 * par + 64, PS:PS + 128]
                             .rearrange("p (a t) -> p a t", a=2),
                             mm[1][64 * par:64 * par + 64, AVP:AVP + 128]
                             .rearrange("p (a t) -> p a t", a=2)),
                         waits=[(s_mm, mmc["n"]), cpsig])
                aolast = o_free[pp2][par]

        for ht in range(33):
            if ht < 32:
                emit_score(ht)
            if ht >= 1:
                emit_soft(ht - 1)
        ev_sigs = {0: aolast_sc[0], 1: aolast}

    wo_sig, _ = emit_wprep_transposed(
        p, c, [(wo, 8, D, 8, D, wqT, None, False)], stag, sgn, pq,
        ld_slab, ldc, s_tq, tqc,
        first_evac_waits=[(s_mm, mmc["n"]), at_free[0], at_free[1]])
    EW = 512
    nst = 0
    est = {}
    for g in range(4):
        for mc in range(KD):
            par = pmm % 2
            p.op("sync", lambda e, mc=mc, g=g, par=par:
                 e.dma_start(out=xr[:, par * EW:(par + 1) * EW],
                             in_=x[mc * 128:(mc + 1) * 128,
                                   63 + g * EW:63 + (g + 1) * EW]),
                 waits=[est.get(par)], incs=[(ldr, 16)])
            nldr = X.bump("ldr")
            for kc in range(KD):
                wts = []
                if kc == 0:
                    wts = [wo_sig, aolast, aolast_sc[0], est.get(par)]
                p.op("tensor", lambda e, par=par, mc=mc, g=g, kc=kc:
                     e.matmul(mm[par][:, 0:EW],
                              wqT.rearrange("p (kc f) -> p kc f", f=D)
                                 [:, kc, mc * 128:(mc + 1) * 128],
                              ao_fm[:, kc * TOK + g * EW:
                                    kc * TOK + (g + 1) * EW],
                              start=(kc == 0), stop=(kc == KD - 1)),
                     waits=wts,
                     incs=([(s_mm, 1)] if kc == KD - 1 else []))
            mmc["n"] += 1
            epar = nst % 2
            esig = p.op("vector", lambda e, par=par, epar=epar: e.tensor_add(
                    yt[:, epar * EW:(epar + 1) * EW], mm[par][:, 0:EW],
                    xr[:, par * EW:(par + 1) * EW]),
                 waits=[(s_mm, mmc["n"]), (ldr, nldr),
                        (s_st[epar], 16 * nstp[epar])])
            est[par] = esig
            pmm += 1
            p.op("sync", lambda e, epar=epar, mc=mc, g=g: e.dma_start(
                    out=y[mc * 128:(mc + 1) * 128, g * EW:(g + 1) * EW],
                    in_=yt[:, epar * EW:(epar + 1) * EW]),
                 waits=[esig], incs=[(s_st[epar], 16)])
            nstp[epar] += 1
            nst += 1


def emit_l0ffn(X, st, x, w1, w2, b1t, b2t, a1h, b1c, r1h, a2h, b2c, r2h,
               y, y_off):
    nc, p, c = X.nc, X.p, X.c
    H = 4096
    MH = 32
    NG2 = 8
    W = 256
    ld_w = X.S("ld_w")
    ldx = X.S("ldx")
    s_mm = X.S("s_mm")
    s_st = [X.S("s_st0"), X.S("s_st1")]
    nstp = X.nstp
    mmc = X.mmc
    e24sb = c.extra["e24"]

    sb = lambda name, shp, dt: st.enter_context(nc.sbuf_tensor(name, shp, dt))
    ps = lambda name, shp, dt: st.enter_context(nc.psum_tensor(name, shp, dt))
    w1s = sb("b_w1s", [128, KD * H], BF16)
    w2s = sb("b_w2s", [128, MH * D], BF16)
    a1s = sb("b_a1s", [128, 8 * 24], BF16)
    b1cs = sb("b_b1cs", [24, H], BF16)
    r1s = sb("b_r1s", [128, 8 * 3], BF16)
    a2s = sb("b_a2s", [128, 32 * 24], BF16)
    b2cs = sb("b_b2cs", [24, D], BF16)
    r2s = sb("b_r2s", [128, 32 * 3], BF16)
    b1s = sb("b_b1s", [128, 32], F32)
    b2s = sb("b_b2s", [128, 8], F32)
    xg = sb("b_xg", [128, KD * W], F32)
    xsq = sb("b_xsq", [128, KD * W], F32)
    xln = sb("b_xln", [128, KD * W], BF16)
    hg = sb("b_hg", [128, MH * W], BF16)
    z_sb = sb("b_z_sb", [24, W], BF16)
    zg_sb = sb("b_zg_sb", [24, W], BF16)
    gex = sb("b_gex", [128, 8], F32)
    gtm = sb("b_gtm", [128, 16], BF16)
    gfm = sb("b_gfm", [3, W], BF16)
    negm = sb("b_negm", [128, 2], F32)
    srec = sb("b_srec", [128, 4], F32)
    scr = sb("b_scr", [128, W], F32)
    tep = sb("b_tep", [128, 2 * W], F32)
    yt = sb("b_yt", [128, 2 * W], F32)
    rowbuf = sb("b_rowbuf", [1, 3 * W], F32)
    stats_ps = ps("b_stats", [128, 512], F32)
    mbrb = ps("b_mbrb", [128, 512], F32)
    mm = [ps("b_mmA", [128, 512], F32), ps("b_mmB", [128, 512], F32)]
    z_ps = ps("b_z_ps", [128, 512], F32)
    g_ps = ps("b_g_ps", [128, 512], F32)
    gt_ps = ps("b_gt_ps", [128, 1024], BF16)
    ge_ps = ps("b_ge_ps", [128, 512], F32)

    gz_sig = p.op("vector", lambda e: e.memset(gtm[:], 0.0))

    for (dst, src, nkc, fdim) in ((w1s, w1, KD, H), (w2s, w2, MH, D)):
        for kc in range(nkc):
            p.op("gpsimd", lambda e, dst=dst, src=src, kc=kc, fdim=fdim:
                 e.dma_start(out=dst[:, kc * fdim:(kc + 1) * fdim],
                             in_=src[kc * 128:(kc + 1) * 128, :]),
                 incs=[(ld_w, 16)], dma=True)
            X.bump("ld_w")
    for (dst, src) in ((a1s, a1h), (b1cs, b1c), (r1s, r1h), (a2s, a2h),
                       (b2cs, b2c), (r2s, r2h), (b1s, b1t), (b2s, b2t)):
        p.op("gpsimd", lambda e, dst=dst, src=src:
             e.dma_start(out=dst[:], in_=src),
             incs=[(ld_w, 16)], dma=True)
        X.bump("ld_w")
    WRDY = X.val("ld_w")

    state = {"pmm": 0, "nst": 0}
    ev_sigs = {}
    add_sigs = []

    def mole_layer(tg, src_ready, src_sb, nkc, acat, rcat, bcat, wmain, fdim,
                   nm, bias_sb, is_last, hg_free):
        for kc in range(nkc):
            wts = []
            if kc == 0:
                wts = [src_ready]
                if tg == 0 and src_sb is xln:
                    wts.append((ld_w, WRDY))
            p.op("tensor", lambda e, kc=kc: e.matmul(
                    z_ps[0:24, 0:W], acat[:, kc * 24:(kc + 1) * 24],
                    src_sb[:, kc * W:(kc + 1) * W],
                    start=(kc == 0), stop=(kc == nkc - 1)),
                 waits=wts)
        for i in range(2):
            for kc in range(nkc):
                p.op("tensor", lambda e, i=i, kc=kc: e.matmul(
                        g_ps[:, i * 4:i * 4 + 3],
                        src_sb[:, kc * W + i * 128:kc * W + (i + 1) * 128],
                        rcat[:, kc * 3:(kc + 1) * 3],
                        start=(kc == 0), stop=(kc == nkc - 1)),
                     incs=([(s_mm, 1)] if (i == 1 and kc == nkc - 1) else []))
        mmc["n"] += 1
        zg_w = [(s_mm, mmc["n"])]
        zev = p.op("vector", lambda e: e.tensor_copy(z_sb[:], z_ps[0:24, 0:W]),
                   waits=zg_w)
        p.op("vector", lambda e: e.tensor_reduce(negm[:, 0:1], g_ps[:, 0:3],
                                                 AX.X, ALU.max, negate=True))
        nm_sig = p.op("vector", lambda e: e.tensor_reduce(
                negm[:, 1:2], g_ps[:, 4:7], AX.X, ALU.max, negate=True))
        p.op("scalar", lambda e: e.activation(gex[:, 0:3], g_ps[:, 0:3],
                                              AF.Exp, bias=negm[:, 0:1]),
             waits=[nm_sig])
        ex_sig = p.op("scalar", lambda e: e.activation(
                gex[:, 4:7], g_ps[:, 4:7], AF.Exp, bias=negm[:, 1:2]))
        p.op("vector", lambda e: e.tensor_reduce(
                srec[:, 0:1], gex[:, 0:3], AX.X, ALU.add), waits=[ex_sig])
        p.op("vector", lambda e: e.tensor_reduce(
                srec[:, 1:2], gex[:, 4:7], AX.X, ALU.add))
        p.op("vector", lambda e: e.reciprocal(srec[:, 2:4], srec[:, 0:2]))
        gt_sig = None
        for i in range(2):
            gt_sig = p.op("vector", lambda e, i=i: e.tensor_scalar(
                    gtm[:, i * 8:i * 8 + 3], gex[:, i * 4:i * 4 + 3],
                    srec[:, 2 + i:3 + i], None, ALU.mult),
                 waits=([gz_sig] if (tg == 0 and i == 0) else ()))
        for i in range(2):
            p.op("tensor", lambda e, i=i: e.transpose(
                    gt_ps[0:8, i * 128:(i + 1) * 128],
                    gtm[:, i * 8:(i + 1) * 8], c.ident_bf),
                 waits=([gt_sig] if i == 0 else ()),
                 incs=([(s_mm, 1)] if i == 1 else []))
        mmc["n"] += 1
        gf_sig = p.op("vector", lambda e: e.tensor_copy(
                gfm[:], gt_ps[0:3, 0:W]), waits=[(s_mm, mmc["n"])])
        p.op("tensor", lambda e: e.matmul(ge_ps[0:24, 0:W], e24sb[:], gfm[:],
                                          start=True, stop=True),
             waits=[gf_sig], incs=[(s_mm, 1)])
        mmc["n"] += 1
        zg_sig = p.op("vector", lambda e: e.scalar_tensor_tensor(
                zg_sb[:], ge_ps[0:24, 0:W], 0.125, z_sb[:],
                ALU.mult, ALU.mult),
             waits=[(s_mm, mmc["n"]), zev])
        last_ep = None
        for m in range(nm):
            par = state["pmm"] % 2
            for kc in range(nkc):
                wts = []
                if kc == 0:
                    wts = [zg_sig, ev_sigs.get(par)]
                    if m == 0:
                        wts.append(hg_free)
                p.op("tensor", lambda e, par=par, m=m, kc=kc: e.matmul(
                        mm[par][:, 0:W],
                        wmain.rearrange("p (kc f) -> p kc f", f=fdim)
                             [:, kc, m * 128:(m + 1) * 128],
                        src_sb[:, kc * W:(kc + 1) * W],
                        start=(kc == 0), stop=False),
                     waits=wts)
            p.op("tensor", lambda e, par=par, m=m: e.matmul(
                    mm[par][:, 0:W], bcat[:, m * 128:(m + 1) * 128], zg_sb[:],
                    start=False, stop=True),
                 incs=[(s_mm, 1)])
            mmc["n"] += 1
            state["pmm"] += 1
            if not is_last:
                ev_sigs[par] = p.op("scalar", lambda e, par=par, m=m:
                     e.activation(hg[:, m * W:(m + 1) * W], mm[par][:, 0:W],
                                  AF.Relu, bias=bias_sb[:, m:m + 1]),
                     waits=[(s_mm, mmc["n"])])
                last_ep = ev_sigs[par]
            else:
                nst = state["nst"]
                epar = nst % 2
                tep_guard = [add_sigs[nst - 2]] if nst >= 2 else []
                ev_sigs[par] = p.op("scalar", lambda e, par=par, epar=epar,
                                    m=m:
                     e.activation(tep[:, epar * W:(epar + 1) * W],
                                  mm[par][:, 0:W], AF.Identity,
                                  bias=bias_sb[:, m:m + 1]),
                     waits=[(s_mm, mmc["n"])] + tep_guard)
                asig = p.op("vector", lambda e, epar=epar, m=m: e.tensor_add(
                        yt[:, epar * W:(epar + 1) * W],
                        tep[:, epar * W:(epar + 1) * W],
                        xg[:, m * W:(m + 1) * W]),
                     waits=[ev_sigs[par], (s_st[epar], 16 * nstp[epar])])
                add_sigs.append(asig)
                p.op("sync", lambda e, epar=epar, m=m, tg=tg: e.dma_start(
                        out=y[m * 128:(m + 1) * 128,
                              y_off + tg * W:y_off + (tg + 1) * W],
                        in_=yt[:, epar * W:(epar + 1) * W]),
                     waits=[asig], incs=[(s_st[epar], 16)])
                nstp[epar] += 1
                state["nst"] += 1
                last_ep = asig
        return last_ep

    ln_sig = None
    hg_free = None
    for tg in range(NG2):
        t0 = tg * W
        xg_guard = add_sigs[-1] if add_sigs else None
        p.op("sync", lambda e, t0=t0:
             e.dma_start(out=xg.rearrange("p (c t) -> p c t", c=KD),
                         in_=x[:, t0:t0 + W]
                             .rearrange("(c p) t -> c p t", p=128)
                             .rearrange("c p t -> p c t")),
             waits=[xg_guard, ln_sig], incs=[(ldx, 16)])
        nldx = X.bump("ldx")
        ln_sig = emit_ln_fm(p, c, xg, xsq, xln, stats_ps, mbrb[:, 0:W],
                            mbrb[:, W:2 * W], rowbuf, scr, s_mm, mmc, W,
                            pre_waits=[(ldx, nldx)])
        evA = mole_layer(tg, ln_sig, xln, KD, a1s, r1s, b1cs, w1s, H, MH,
                         b1s, False, hg_free)
        hg_free = mole_layer(tg, evA, hg, MH, a2s, r2s, b2cs, w2s, D, KD,
                             b2s, True, None)


def emit_l1ffn(X, st, x, w1, b1t, w2, b2t, y):
    nc, p, c = X.nc, X.p, X.c
    H = 4096
    MH = 32
    NG2 = 8
    W = 256
    ld_slab = X.S("ld_slab")
    ld_b = X.S("ld_b")
    ldx = X.S("ldx")
    s_tq = X.S("s_tq")
    s_mm = X.S("s_mm")
    s_st = [X.S("s_st0"), X.S("s_st1")]
    nstp = X.nstp
    mmc, tqc, ldc = X.mmc, X.tqc, X.ldc

    sb = lambda name, shp, dt: st.enter_context(nc.sbuf_tensor(name, shp, dt))
    ps = lambda name, shp, dt: st.enter_context(nc.psum_tensor(name, shp, dt))
    w1T = sb("f_w1T", [128, KD * H], BF16)
    w2T = sb("f_w2T", [128, MH * D], BF16)
    stag = sb("f_stag", [128, 4096], F32)
    sgn = sb("f_sgn", [128, 4096], BF16)
    alpha1 = sb("f_alpha1", [128, 32], F32)
    alpha2 = sb("f_alpha2", [128, 8], F32)
    b1s = sb("f_b1s", [128, 32], F32)
    b2s = sb("f_b2s", [128, 8], F32)
    xg = sb("f_xg", [128, KD * W], F32)
    xsq = sb("f_xsq", [128, KD * W], F32)
    xln = sb("f_xln", [128, KD * W], BF16)
    hg = sb("f_hg", [128, MH * W], BF16)
    scr = sb("f_scr", [128, W], F32)
    tep = sb("f_tep", [128, 2 * W], F32)
    yt = sb("f_yt", [128, 2 * W], F32)
    rowbuf = sb("f_rowbuf", [1, 3 * W], F32)
    pq = ps("f_pq0", [128, 1024], BF16)
    stats_ps = ps("f_stats", [128, 512], F32)
    mbrb = ps("f_mbrb", [128, 512], F32)
    mm = [ps("f_mmA", [128, 512], F32), ps("f_mmB", [128, 512], F32)]

    LDB = X.val("ld_b")
    p.op("sync", lambda e: e.dma_start(out=b1s[:], in_=b1t), incs=[(ld_b, 16)])
    p.op("sync", lambda e: e.dma_start(out=b2s[:], in_=b2t), incs=[(ld_b, 16)])
    X.bump("ld_b", 32)
    LDB += 32

    specs = [(w1, 32, D, 8, H, w1T, alpha1, True),
             (w2, 8, 4096, 32, D, w2T, alpha2, True)]
    wp_sig, alpha_sigs = emit_wprep_transposed(
        p, c, specs, stag, sgn, pq, ld_slab, ldc, s_tq, tqc)
    p.op("vector", lambda e: e.tensor_scalar(alpha1[:], alpha1[:], 1.0 / D,
                                             None, ALU.mult),
         waits=[(ld_b, LDB), alpha_sigs[-1]])
    al_sig = p.op("vector", lambda e: e.tensor_scalar(
            alpha2[:], alpha2[:], 1.0 / 4096.0, None, ALU.mult))

    ev_sigs = {}
    add_sigs = []
    ln_sig = None
    hg_free = None
    pmm = 0
    nst = 0
    for tg in range(NG2):
        t0 = tg * W
        xg_guard = add_sigs[-1] if add_sigs else None
        p.op("sync", lambda e, t0=t0:
             e.dma_start(out=xg.rearrange("p (c t) -> p c t", c=KD),
                         in_=x[:, t0:t0 + W]
                             .rearrange("(c p) t -> c p t", p=128)
                             .rearrange("c p t -> p c t")),
             waits=[xg_guard, ln_sig], incs=[(ldx, 16)])
        nldx = X.bump("ldx")
        ln_sig = emit_ln_fm(p, c, xg, xsq, xln, stats_ps, mbrb[:, 0:W],
                            mbrb[:, W:2 * W], rowbuf, scr, s_mm, mmc, W,
                            pre_waits=[(ldx, nldx)])
        last_gelu = None
        for m in range(MH):
            par = pmm % 2
            for kc in range(KD):
                wts = []
                if kc == 0:
                    wts = [ln_sig, ev_sigs.get(par)]
                    if tg == 0 and m == 0:
                        wts.append(wp_sig)
                    if m == 0:
                        wts.append(hg_free)
                p.op("tensor", lambda e, par=par, m=m, kc=kc:
                     e.matmul(mm[par][:, 0:W],
                              w1T.rearrange("p (kc f) -> p kc f", f=H)
                                 [:, kc, m * 128:(m + 1) * 128],
                              xln[:, kc * W:(kc + 1) * W],
                              start=(kc == 0), stop=(kc == KD - 1)),
                     waits=wts, incs=([(s_mm, 1)] if kc == KD - 1 else []))
            mmc["n"] += 1
            pmm += 1
            ev_sigs[par] = p.op("scalar", lambda e, par=par, m=m:
                 e.activation(hg[:, m * W:(m + 1) * W], mm[par][:, 0:W],
                              AF.Gelu, bias=b1s[:, m:m + 1],
                              scale=alpha1[:, m:m + 1]),
                 waits=[(s_mm, mmc["n"])] +
                       ([al_sig] if (tg == 0 and m == 0) else []))
            last_gelu = ev_sigs[par]
        for md in range(KD):
            par = pmm % 2
            for kc in range(MH):
                wts = []
                if kc == 0:
                    wts = [ev_sigs.get(par)]
                    if md == 0:
                        wts.append(last_gelu)
                p.op("tensor", lambda e, par=par, md=md, kc=kc:
                     e.matmul(mm[par][:, 0:W],
                              w2T.rearrange("p (kc f) -> p kc f", f=D)
                                 [:, kc, md * 128:(md + 1) * 128],
                              hg[:, kc * W:(kc + 1) * W],
                              start=(kc == 0), stop=(kc == MH - 1)),
                     waits=wts, incs=([(s_mm, 1)] if kc == MH - 1 else []))
            mmc["n"] += 1
            pmm += 1
            epar = nst % 2
            tep_guard = [add_sigs[nst - 2]] if nst >= 2 else []
            ev_sigs[par] = p.op("scalar", lambda e, par=par, epar=epar, md=md:
                 e.activation(tep[:, epar * W:(epar + 1) * W],
                              mm[par][:, 0:W], AF.Identity,
                              bias=b2s[:, md:md + 1],
                              scale=alpha2[:, md:md + 1]),
                 waits=[(s_mm, mmc["n"])] + tep_guard)
            asig = p.op("vector", lambda e, epar=epar, md=md: e.tensor_add(
                    yt[:, epar * W:(epar + 1) * W],
                    tep[:, epar * W:(epar + 1) * W],
                    xg[:, md * W:(md + 1) * W]),
                 waits=[ev_sigs[par], (s_st[epar], 16 * nstp[epar])])
            add_sigs.append(asig)
            p.op("sync", lambda e, epar=epar, md=md, t0=t0: e.dma_start(
                    out=y[md * 128:(md + 1) * 128, t0:t0 + W],
                    in_=yt[:, epar * W:(epar + 1) * W]),
                 waits=[asig], incs=[(s_st[epar], 16)])
            nstp[epar] += 1
            nst += 1
            hg_free = asig


EXT3 = 2304


def emit_l1front(X, st, x, wqkv, convs, wsm, ccq_in, ccs2_in):
    nc, p, c = X.nc, X.p, X.c
    PW = 384
    NPG = 6
    ld_w = X.S("ld_w")
    ld_w2 = X.S("ld_w2")
    ldx = X.S("ldx")
    s_mm = X.S("s_mm")
    s_tq = X.S("s_tq")
    s_st = X.S("s_st0")
    nstp = X.nstp
    mmc, tqc = X.mmc, X.tqc
    cbase = nstp[0]

    sb = lambda name, shp, dt: st.enter_context(nc.sbuf_tensor(name, shp, dt))
    ps = lambda name, shp, dt: st.enter_context(nc.psum_tensor(name, shp, dt))
    xln_ext = sb("c_xln_ext", [128, KD * EXT3], BF16)
    w_sb = sb("c_w_sb", [128, KD * D], BF16)
    conv_s = sb("c_conv_s", [128, 8 * 12], F32)
    wsm_s = sb("c_wsm_s", [128, 8 * 96], BF16)
    praw = sb("c_praw", [128, 2 * EXT3], BF16)
    cacc_v = sb("c_cacc_v", [128, TOK], F32)
    pout1 = sb("c_pout1", [128, KD * TOK], BF16)
    smq_b = sb("c_smq_b", [8, 512], F32)
    smq_g = sb("c_smq_g", [4, 512], F32)
    smq_h = sb("c_smq_h", [4, 512], F32)
    smtm = sb("c_smtm", [128, 64], F32)
    xg = st.enter_context(nc.sbuf_tensor("c_xg", [128, KD * PW], F32,
                                         side="right"))
    xsq = st.enter_context(nc.sbuf_tensor("c_xsq", [128, KD * PW], F32,
                                          side="right"))
    scr = sb("c_scr", [128, 512], F32)
    rowbuf = sb("c_rowbuf", [1, 3 * PW], F32)
    stats_ps = ps("c_stats", [128, 512], F32)
    mb_ps = ps("c_mb_ps", [128, 512], F32)
    rb_ps = ps("c_rb_ps", [128, 512], F32)
    mm = [ps("c_mmA", [128, 512], F32), ps("c_mmB", [128, 512], F32)]
    sm_ps = ps("c_sm_ps", [128, 512], F32)
    sm_tp = ps("c_sm_tp", [128, 64], F32)

    p.op("gpsimd", lambda e: e.dma_start(out=conv_s[:], in_=convs),
         incs=[(ld_w, 16)], dma=True)
    X.bump("ld_w")
    for kcx in range(KD):
        p.op("gpsimd", lambda e, kcx=kcx: e.dma_start(
                out=wsm_s[:, kcx * 96:(kcx + 1) * 96],
                in_=wsm[kcx * 128:(kcx + 1) * 128, :]),
             incs=[(ld_w, 16)], dma=True)
        X.bump("ld_w")
    WRDY0 = X.val("ld_w")

    ln_sig = None
    for g in range(NPG):
        t0 = g * PW
        p.op("sync", lambda e, t0=t0:
             e.dma_start(out=xg.rearrange("p (c t) -> p c t", c=KD),
                         in_=x[:, t0:t0 + PW]
                             .rearrange("(c p) t -> c p t", p=128)
                             .rearrange("c p t -> p c t")),
             waits=[ln_sig], incs=[(ldx, 16)])
        nldx = X.bump("ldx")
        ln_sig = emit_ln_fm(p, c, xg, xsq, xln_ext, stats_ps, mb_ps, rb_ps,
                            rowbuf, scr, s_mm, mmc, PW,
                            pre_waits=[(ldx, nldx)],
                            xln_stride=EXT3, xln_off=t0)

    smfree = None
    smev = None
    for g4 in range(4):
        t0 = 3 + g4 * 512
        for kcx in range(KD):
            wts = []
            if kcx == 0:
                wts = [ln_sig, (ld_w, WRDY0), smfree]
            p.op("tensor", lambda e, kcx=kcx, t0=t0: e.matmul(
                    sm_ps[0:96, 0:512],
                    wsm_s[:, kcx * 96:(kcx + 1) * 96],
                    xln_ext[:, kcx * EXT3 + t0:kcx * EXT3 + t0 + 512],
                    start=(kcx == 0), stop=(kcx == KD - 1)),
                 waits=wts, incs=([(s_mm, 1)] if kcx == KD - 1 else []))
        mmc["n"] += 1
        p.op("scalar", lambda e: e.activation(
                smq_b[0:8, 0:512], sm_ps[0:8, 0:512], AF.Sigmoid),
             waits=[(s_mm, mmc["n"])])
        p.op("scalar", lambda e: e.copy(smq_g[0:4, 0:512],
                                        sm_ps[32:36, 0:512]))
        smfree = p.op("scalar", lambda e: e.copy(smq_h[0:4, 0:512],
                                                 sm_ps[64:68, 0:512]))
        nt = 0
        for tt in range(4):
            for (src, nq, qo) in ((smq_b, 8, 0), (smq_g, 4, 8), (smq_h, 4, 12)):
                nt += 1
                p.op("tensor", lambda e, tt=tt, src=src, nq=nq, qo=qo:
                     e.transpose(
                        sm_tp[:, tt * 16 + qo:tt * 16 + qo + nq],
                        src[0:nq, tt * 128:(tt + 1) * 128],
                        c.ident_f32[0:nq, 0:nq]),
                     waits=([smfree, smev] if nt == 1 else ()),
                     incs=([(s_tq, 1)] if nt == 12 else []))
        tqc["n"] += 1
        smev = p.op("vector", lambda e: e.tensor_copy(smtm[:, 0:64],
                                                      sm_tp[:, 0:64]),
                    waits=[(s_tq, tqc["n"]), (s_st, 16 * nstp[0])])
        p.op("sync", lambda e, g4=g4: e.dma_start(
                out=ccs2_in[g4 * 512:(g4 + 1) * 512, :]
                    .rearrange("(tt pp) q -> pp tt q", pp=128),
                in_=smtm[:, 0:64].rearrange("p (tt q) -> p tt q", q=16)),
             waits=[smev], incs=[(s_st, 16)])
        nstp[0] += 1

    pmm = 0
    ev_sigs = {}
    praw_free = {}
    sl_last = {}
    for ti, whbm in enumerate(wqkv):
        wguard = ev_sigs.get("wlast")
        for kcx in range(KD):
            p.op("gpsimd", lambda e, kcx=kcx, whbm=whbm: e.dma_start(
                    out=w_sb[:, kcx * D:(kcx + 1) * D],
                    in_=whbm[kcx * 128:(kcx + 1) * 128, :]),
                 waits=([wguard] if (kcx == 0 and wguard is not None) else []),
                 incs=[(ld_w2, 16)], dma=True)
            X.bump("ld_w2")
        wld = X.val("ld_w2")
        for mc in range(KD):
            par = pmm % 2
            for g in range(NPG):
                for kcx in range(KD):
                    wts = []
                    if kcx == 0:
                        wts = [(ld_w2, wld), ln_sig, ev_sigs.get(par)]
                    p.op("tensor", lambda e, par=par, mc=mc, g=g, kcx=kcx:
                         e.matmul(mm[par][:, 0:PW],
                                  w_sb.rearrange("p (kc f) -> p kc f", f=D)
                                      [:, kcx, mc * 128:(mc + 1) * 128],
                                  xln_ext[:, kcx * EXT3 + g * PW:
                                          kcx * EXT3 + (g + 1) * PW],
                                  start=(kcx == 0), stop=(kcx == KD - 1)),
                         waits=wts,
                         incs=([(s_mm, 1)] if kcx == KD - 1 else []))
                mmc["n"] += 1
                ppar = mc % 2
                ev_sigs[par] = p.op("scalar",
                     lambda e, par=par, ppar=ppar, g=g:
                     e.activation(praw[:, ppar * EXT3 + g * PW:
                                       ppar * EXT3 + (g + 1) * PW],
                                  mm[par][:, 0:PW], AF.Copy),
                     waits=[(s_mm, mmc["n"])] +
                           ([praw_free.get(mc % 2)] if g == 0 else []))
                pmm += 1
                par = pmm % 2
            ev_sigs["wlast"] = ev_sigs[(pmm - 1) % 2]
            cacc = cacc_v
            ppar = mc % 2
            pr_sig = ev_sigs[(pmm - 1) % 2]
            p.op("vector", lambda e, ppar=ppar, mc=mc, ti=ti, cacc=cacc:
                 e.tensor_scalar(
                    cacc[:], praw[:, ppar * EXT3 + 0:ppar * EXT3 + TOK],
                    conv_s[:, mc * 12 + ti * 4:mc * 12 + ti * 4 + 1],
                    None, ALU.mult),
                 waits=[pr_sig, sl_last.get("vector")])
            csig = None
            for j in (1, 2, 3):
                csig = p.op("vector", lambda e, ppar=ppar, mc=mc, ti=ti, j=j,
                            cacc=cacc:
                     e.scalar_tensor_tensor(
                         cacc[:], praw[:, ppar * EXT3 + j:ppar * EXT3 + j + TOK],
                         conv_s[:, mc * 12 + ti * 4 + j:
                                mc * 12 + ti * 4 + j + 1],
                         cacc[:], ALU.mult, ALU.add))
            stguard = (s_st, 16 * (cbase + 4 + ti * 8)) if ti > 0 else None
            sl_sig = p.op("scalar", lambda e, mc=mc, cacc=cacc:
                 e.activation(
                    pout1[:, mc * TOK:(mc + 1) * TOK], cacc[:], AF.Silu),
                 waits=[csig, stguard])
            sl_last["vector"] = sl_sig
            praw_free[mc % 2] = csig
            p.op("sync", lambda e, mc=mc, ti=ti: e.dma_start(
                    out=ccq_in[ti * 1024 + mc * 128:
                               ti * 1024 + (mc + 1) * 128, :],
                    in_=pout1[:, mc * TOK:(mc + 1) * TOK]),
                 waits=[sl_sig], incs=[(s_st, 16)])
            nstp[0] += 1


def emit_l1comb(X, st, cc2_out, ccq_in, ccs2_in, bgh, nw_in, wo, x, x_off, y):
    nc, p, c = X.nc, X.p, X.c
    ld_w = X.S("ld_w")
    ld_wh = X.S("ld_wh")
    ld_t = X.S("ld_t")
    ldr = X.S("ldr")
    s_tq = X.S("s_tq")
    s_mm = X.S("s_mm")
    s_cc = X.S("s_cc")
    s_st = [X.S("s_st0"), X.S("s_st1")]
    nstp = X.nstp
    mmc, tqc = X.mmc, X.tqc
    mfl = c.extra["mflag"]

    sb = lambda name, shp, dt: st.enter_context(nc.sbuf_tensor(name, shp, dt))
    ps = lambda name, shp, dt: st.enter_context(nc.psum_tensor(name, shp, dt))
    wo_s = sb("e_wo_s", [128, KD * D], BF16)
    nw_s = sb("e_nw_s", [128, 1024], F32)
    bgh_s = sb("e_bgh_s", [128, 8], F32)
    otA = sb("e_otA", [128, 1024], BF16)
    otB = sb("e_otB", [128, 1024], BF16)
    ot = sb("e_ot", [128, 1024], F32)
    ot2 = sb("e_ot2", [128, 1024], F32)
    vstg = sb("e_vstg", [128, 1024], BF16)
    vt = sb("e_vt", [128, 1024], BF16)
    ght = sb("e_ght", [128, 8], F32)
    wrk = sb("e_wrk", [128, 24], F32)
    o2 = sb("e_o2", [128, 1024], F32)
    ssq = sb("e_ssq", [128, 8], F32)
    o3 = sb("e_o3", [128, 1024], BF16)
    ofm = sb("e_ofm", [128, KD * TOK], BF16)
    yt = sb("e_yt", [128, 2 * 512], F32)
    xr = sb("e_xr", [128, 2 * 512], F32)
    mm = [ps("e_mmA", [128, 512], F32), ps("e_mmB", [128, 512], F32)]
    pq = ps("e_pqA", [128, 1024], BF16)

    LW0 = X.val("ld_w")
    for kcx in range(KD):
        p.op("gpsimd", lambda e, kcx=kcx: e.dma_start(
                out=wo_s[:, kcx * D:(kcx + 1) * D],
                in_=wo[kcx * 128:(kcx + 1) * 128, :]),
             incs=[(ld_w, 16)], dma=True)
        X.bump("ld_w")
    WRDY = X.val("ld_w")
    LWH = X.val("ld_wh")
    p.op("sync", lambda e: e.dma_start(out=nw_s[:], in_=nw_in),
         incs=[(ld_wh, 16)])
    p.op("sync", lambda e: e.dma_start(out=bgh_s[:], in_=bgh),
         incs=[(ld_wh, 16)])
    X.bump("ld_wh", 32)
    LWH += 32

    ev_sig = None
    tile_sigs = {}
    for i in range(16):
        tguard = tile_sigs.get(i - 1)
        first = (i == 0)
        for r in range(2):
            for h2 in range(2):
                for (buf, goff) in ((otA, 0), (otB, 16)):
                    wts = [tguard] if (r == 0 and h2 == 0 and goff == 0) else []
                    if first and r == 0 and h2 == 0 and goff == 0:
                        wts.append((s_cc, 4))
                    p.op("sync", lambda e, r=r, h2=h2, buf=buf, goff=goff, i=i:
                         e.dma_start(
                            out=buf[:, (2 * r + h2) * 256:
                                    (2 * r + h2 + 1) * 256],
                            in_=cc2_out[r * 8192 + h2 * 4096 + (goff + i) * 128:
                                        r * 8192 + h2 * 4096 + (goff + i + 1) * 128,
                                        :]),
                         waits=wts, incs=[(ld_t, 16)])
                    X.bump("ld_t")
        p.op("sync", lambda e, i=i: e.dma_start(
                out=vstg.rearrange("p (cx t) -> p cx t", cx=KD),
                in_=ccq_in[2048:3072, i * 128:(i + 1) * 128]
                    .rearrange("(cx p) t -> cx p t", p=128)
                    .rearrange("cx p t -> p cx t")),
             incs=[(ld_t, 16)])
        X.bump("ld_t")
        p.op("sync", lambda e, i=i: e.dma_start(
                out=ght[:],
                in_=ccs2_in[i * 128:(i + 1) * 128, 8:16]),
             incs=[(ld_t, 16)])
        X.bump("ld_t")
        nldt = X.val("ld_t")
        ldw = [(ld_t, nldt), (ld_w, WRDY), (ld_wh, LWH)]
        p.op("vector", lambda e: e.tensor_scalar(ot[:], otB[:], mfl[:, 1:2],
                                                 None, ALU.mult), waits=ldw)
        blend_sig = p.op("vector", lambda e: e.scalar_tensor_tensor(
                ot[:], otA[:], mfl[:, 0:1], ot[:], ALU.mult, ALU.add))
        for jj in range(8):
            wts = []
            if jj == 0:
                wts = [ev_sig, (ld_t, nldt)]
            p.op("tensor", lambda e, jj=jj: e.transpose(
                    pq[:, jj * 128:(jj + 1) * 128],
                    vstg[:, jj * 128:(jj + 1) * 128], c.ident_bf),
                 waits=wts, incs=([(s_tq, 1)] if jj == 7 else []))
        tqc["n"] += 1
        vt_sig = p.op("scalar", lambda e: e.activation(
                vt[:], pq[:, 0:1024], AF.Copy),
             waits=[(s_tq, tqc["n"])])
        p.op("vector", lambda e: e.tensor_add(wrk[:, 0:4], ght[:, 0:4],
                                              bgh_s[:, 0:4]), waits=ldw)
        p.op("vector", lambda e: e.tensor_add(wrk[:, 4:8], ght[:, 4:8],
                                              bgh_s[:, 4:8]))
        p.op("vector", lambda e: e.tensor_max(wrk[:, 8:12], wrk[:, 0:4],
                                              wrk[:, 4:8]))
        p.op("vector", lambda e: e.tensor_scalar(wrk[:, 8:12], wrk[:, 8:12],
                                                 0.0, None, ALU.max))
        p.op("vector", lambda e: e.tensor_sub(wrk[:, 0:4], wrk[:, 0:4],
                                              wrk[:, 8:12]))
        vsb = p.op("vector", lambda e: e.tensor_sub(wrk[:, 4:8], wrk[:, 4:8],
                                                    wrk[:, 8:12]))
        p.op("scalar", lambda e: e.activation(wrk[:, 12:16], wrk[:, 0:4],
                                              AF.Exp), waits=[vsb])
        p.op("scalar", lambda e: e.activation(wrk[:, 16:20], wrk[:, 4:8],
                                              AF.Exp))
        es = p.op("scalar", lambda e: e.activation(wrk[:, 20:24], wrk[:, 8:12],
                                                   AF.Exp, scale=-1.0))
        p.op("vector", lambda e: e.tensor_add(wrk[:, 12:16], wrk[:, 12:16],
                                              wrk[:, 16:20]), waits=[es])
        p.op("vector", lambda e: e.tensor_add(wrk[:, 12:16], wrk[:, 12:16],
                                              wrk[:, 20:24]))
        rcp = p.op("vector", lambda e: e.reciprocal(wrk[:, 16:20],
                                                    wrk[:, 12:16]))
        wr = p.op("vector", lambda e: e.tensor_mul(wrk[:, 20:24],
                                                   wrk[:, 20:24],
                                                   wrk[:, 16:20]),
                  waits=[rcp])
        o2s = None
        for h in range(4):
            o2s = p.op("vector", lambda e, h=h: e.scalar_tensor_tensor(
                    o2[:, h * 256:(h + 1) * 256],
                    vt[:, h * 256:(h + 1) * 256],
                    wrk[:, 20 + h:21 + h],
                    ot[:, h * 256:(h + 1) * 256], ALU.mult, ALU.add),
                 waits=([wr, blend_sig, vt_sig] if h == 0 else ()))
        sqs = p.op("vector", lambda e: e.tensor_mul(yt[:, 0:1024],
                                                    o2[:, 0:1024],
                                                    o2[:, 0:1024]),
                   waits=[o2s])
        p.op("vector", lambda e: e.tensor_reduce(
                ssq[:, 0:4], yt[:, 0:1024].rearrange("p (a t) -> p a t", a=4),
                AX.X, ALU.add))
        tss = p.op("vector", lambda e: e.tensor_scalar(
                ssq[:, 0:4], ssq[:, 0:4], 1.0 / 256.0,
                EPS, ALU.mult, ALU.add))
        lns = p.op("scalar", lambda e: e.activation(ssq[:, 4:8], ssq[:, 0:4],
                                                    AF.Ln), waits=[tss])
        rst = p.op("scalar", lambda e: e.activation(ssq[:, 4:8], ssq[:, 4:8],
                                                    AF.Exp, scale=-0.5),
                   waits=[lns])
        for h in range(4):
            p.op("vector", lambda e, h=h: e.tensor_scalar(
                    o2[:, h * 256:(h + 1) * 256], o2[:, h * 256:(h + 1) * 256],
                    ssq[:, 4 + h:5 + h], None, ALU.mult),
                 waits=([rst] if h == 0 else ()))
        o3s = p.op("vector", lambda e: e.tensor_mul(o3[:], o2[:, 0:1024],
                                                    nw_s[:]))
        for quad in range(2):
            for jj in range(4):
                cx = quad * 4 + jj
                wts = []
                if jj == 0:
                    wts = [o3s, ev_sig, vt_sig]
                p.op("tensor", lambda e, cx=cx, jj=jj: e.transpose(
                        pq[:, jj * 128:(jj + 1) * 128],
                        o3[:, cx * 128:(cx + 1) * 128], c.ident_bf),
                     waits=wts, incs=([(s_tq, 1)] if jj == 3 else []))
            tqc["n"] += 1
            ev_sig = p.op("vector", lambda e, quad=quad, i=i: e.tensor_copy(
                    ofm.rearrange("p (cx t) -> p cx t", cx=KD)
                       [:, quad * 4:(quad + 1) * 4, i * 128:(i + 1) * 128],
                    pq[:, 0:512].rearrange("p (a f) -> p a f", a=4)),
                 waits=[(s_tq, tqc["n"])])
        tile_sigs[i] = ev_sig

    EW = 512
    nst = 0
    est = {}
    pmm = 0
    for g in range(4):
        for mc in range(KD):
            par = pmm % 2
            p.op("sync", lambda e, mc=mc, g=g, par=par:
                 e.dma_start(out=xr[:, par * EW:(par + 1) * EW],
                             in_=x[mc * 128:(mc + 1) * 128,
                                   x_off + g * EW:x_off + (g + 1) * EW]),
                 waits=[est.get(par)], incs=[(ldr, 16)])
            nldr = X.bump("ldr")
            for kcx in range(KD):
                wts = []
                if kcx == 0:
                    wts = [ev_sig, est.get(par), (ld_w, WRDY), (ld_wh, LWH)]
                p.op("tensor", lambda e, par=par, mc=mc, g=g, kcx=kcx:
                     e.matmul(mm[par][:, 0:EW],
                              wo_s.rearrange("p (kc f) -> p kc f", f=D)
                                  [:, kcx, mc * 128:(mc + 1) * 128],
                              ofm[:, kcx * TOK + g * EW:
                                  kcx * TOK + (g + 1) * EW],
                              start=(kcx == 0), stop=(kcx == KD - 1)),
                     waits=wts,
                     incs=([(s_mm, 1)] if kcx == KD - 1 else []))
            mmc["n"] += 1
            epar = nst % 2
            esig = p.op("vector", lambda e, par=par, epar=epar: e.tensor_add(
                    yt[:, epar * EW:(epar + 1) * EW], mm[par][:, 0:EW],
                    xr[:, par * EW:(par + 1) * EW]),
                 waits=[(s_mm, mmc["n"]), (ldr, nldr),
                        (s_st[epar], 16 * nstp[epar])])
            est[par] = esig
            pmm += 1
            p.op("sync", lambda e, epar=epar, mc=mc, g=g: e.dma_start(
                    out=y[mc * 128:(mc + 1) * 128, g * EW:(g + 1) * EW],
                    in_=yt[:, epar * EW:(epar + 1) * EW]),
                 waits=[esig], incs=[(s_st[epar], 16)])
            nstp[epar] += 1
            nst += 1


def emit_l1scan(X, st, ccq_out, ccs_out, msk_d, bgh_d, cc2_in, scr1, scr2):
    nc, p, c = X.nc, X.p, X.c
    L4L = 4096
    NG = 32
    CCq = 3072
    ld_in = X.S("ld_in")
    ld_sm = X.S("ld_sm")
    ld_sh = X.S("ld_sh")
    s_tq = X.S("s_tq")
    s_mm = X.S("s_mm")
    s_cc = X.S("s_cc")
    s_st = [X.S("s_st0"), X.S("s_st1")]
    nstp = X.nstp
    mmc, tqc = X.mmc, X.tqc
    mflc = c.extra["mflag"]
    selm_sb = c.extra["selm"]

    sb = lambda name, shp, dt: st.enter_context(nc.sbuf_tensor(name, shp, dt))
    qtm = sb("d_qtm", [128, NG * 256], BF16)
    ktm = sb("d_ktm", [128, NG * 256], BF16)
    vtm = sb("d_vtm", [128, NG * 256], BF16)
    q_fm = sb("d_q_fm", [128, 2 * L4L], BF16)
    k_fm = sb("d_k_fm", [128, 2 * L4L], BF16)
    TB = [sb("d_TB_l", [128, NG * 128], BF16), sb("d_TB_g", [128, NG * 128], BF16)]
    T0F = [sb("d_T0F_l", [128, 4096], F32), sb("d_T0F_g", [128, 4096], F32)]
    T_vec = sb("d_T_vec", [128, 1024], F32)
    T_vt = sb("d_T_vt", [128, 1024], F32)
    T_bfs = sb("d_T_bfs", [128, 1024], BF16)
    kkm_sb = sb("d_kkm_sb", [128, 128], F32)
    acc32 = sb("d_acc32", [128, 32], F32)
    blg = [sb("d_bl_s", [128, NG], F32), sb("d_bg_s", [128, NG], F32)]
    nblg = [sb("d_nbl_s", [128, NG], F32), sb("d_nbg_s", [128, NG], F32)]
    ggs = sb("d_ggs", [128, NG], F32)
    hhs = sb("d_hhs", [128, NG], F32)
    wlg = [sb("d_wl_s", [128, NG], F32), sb("d_wg_s", [128, NG], F32)]
    gwk = sb("d_gwk", [128, 8 * NG], F32)
    bghs = sb("d_bghs", [128, 4], F32)
    msk = sb("d_msk_s", [128, 416], F32)
    nrm = sb("d_nrm", [128, 2 * NG], F32)
    vs_sb = sb("d_vs_sb", [128, 512], BF16)
    kb_sb = sb("d_kb_sb", [128, 512], BF16)
    wm_sb = sb("d_wm_sb", [128, 512], BF16)
    wkm_sb = [sb("d_wkm_l", [128, 128], BF16), sb("d_wkm_g", [128, 128], BF16)]
    wkm3 = [sb("d_wkm3_l", [128, 128], BF16), sb("d_wkm3_g", [128, 128], BF16)]
    grm_sb = sb("d_grm_sb", [128, 128], BF16)
    Usb = [sb("d_U_l", [128, 256], BF16), sb("d_U_g", [128, 256], BF16)]
    S_bf = [sb("d_Sbf_l", [128, 512], BF16), sb("d_Sbf_g", [128, 512], BF16)]
    S_sb = [sb("d_Ssb_l", [128, 512], F32), sb("d_Ssb_g", [128, 512], F32)]
    o_stage = sb("d_o_stage", [128, 2 * 256], BF16)
    stgA = sb("d_stgA", [128, 2 * TOK], BF16)
    stgB = sb("d_stgB", [128, 2 * TOK], BF16)
    smfull = sb("d_smfull", [128, 512], F32)
    smT = sb("d_smT", [128, 32], F32)
    nld = {"n": 0}
    nsh = {"n": 0}

    MS_BD = msk[:, 32:160]
    MS_BL = msk[:, 160:288]
    MS_CA = msk[:, 288:416]
    nsm = {"n": 0}
    p.op("sync", lambda e: e.dma_start(out=msk[:], in_=msk_d),
         incs=[(ld_sm, 16)])
    nsm["n"] += 16
    p.op("sync", lambda e: e.dma_start(out=bghs[:], in_=bgh_d),
         incs=[(ld_sm, 16)])
    nsm["n"] += 16
    mz_sig = None
    for t in (Usb[0], Usb[1], S_bf[0], S_bf[1], wkm3[0], wkm3[1],
              TB[0], TB[1]):
        mz_sig = p.op("vector", lambda e, t=t: e.memset(t[:], 0.0))

    for r in range(2):
        for gl in range(16):
            wts = []
            if r == 0 and gl == 0:
                wts = [(s_cc, 3)]
            p.op("sync", lambda e, r=r, gl=gl: e.dma_start(
                    out=smfull[:, (r * 16 + gl) * 16:(r * 16 + gl + 1) * 16],
                    in_=ccs_out[r * TOK + gl * 128:r * TOK + (gl + 1) * 128,
                                0:16]),
                 waits=wts, incs=[(ld_sm, 16)])
            nsm["n"] += 16
    LDSM = nsm["n"]
    smv = smfull.rearrange("p (g q) -> p g q", q=16)

    head_free = None
    shuf_done = None
    conv_ev = None
    with nc.psum_tensor("d_pqc", [128, 1024], F32) as pqc:
        for hh in range(2):
            stg_free = None
            for ti, dst in enumerate((qtm, ktm, vtm)):
                for cc2 in range(2):
                    for (stg, pbase) in ((stgA, 0), (stgB, 512)):
                        for r in range(2):
                            row0 = (r * CCq + ti * 1024 + pbase + hh * 256 +
                                    cc2 * 128)
                            wts = [stg_free]
                            if (hh == 0 and ti == 0 and cc2 == 0 and
                                    pbase == 0 and r == 0):
                                wts.append((s_cc, 3))
                            p.op("sync", lambda e, stg=stg, r=r, row0=row0:
                                 e.dma_start(
                                    out=stg[:, r * TOK:(r + 1) * TOK],
                                    in_=ccq_out[row0:row0 + 128, :]),
                                 waits=wts, incs=[(ld_in, 16)])
                            nld["n"] += 16
                    LDS = nld["n"]
                    for gb in range(4):
                        for gi in range(8):
                            g = gb * 8 + gi
                            r = g // 16
                            t0 = (g % 16) * 128
                            wts = []
                            if gi == 0:
                                wts = [conv_ev]
                                if gb == 0:
                                    wts.append((ld_in, LDS))
                            p.op("tensor", lambda e, gi=gi, r=r, t0=t0:
                                 e.matmul(pqc[:, gi * 128:(gi + 1) * 128],
                                          stgA[:, r * TOK + t0:
                                               r * TOK + t0 + 128],
                                          selm_sb[:, 0:128],
                                          start=True, stop=False),
                                 waits=wts)
                            p.op("tensor", lambda e, gi=gi, r=r, t0=t0:
                                 e.matmul(pqc[:, gi * 128:(gi + 1) * 128],
                                          stgB[:, r * TOK + t0:
                                               r * TOK + t0 + 128],
                                          selm_sb[:, 128:256],
                                          start=False, stop=True),
                                 incs=([(s_mm, 1)] if gi == 7 else []))
                        mmc["n"] += 1
                        cwts = [(s_mm, mmc["n"])]
                        if gb == 0 and cc2 == 0 and ti == 0:
                            cwts.append(head_free)
                        conv_ev = p.op("scalar",
                             lambda e, dst=dst, gb=gb, cc2=cc2:
                             e.copy(
                                dst.rearrange("p (g cx) -> p g cx", cx=256)
                                   [:, gb * 8:(gb + 1) * 8,
                                    cc2 * 128:(cc2 + 1) * 128],
                                pqc[:, 0:1024]
                                   .rearrange("p (g cx) -> p g cx", cx=128)),
                             waits=cwts)
                    stg_free = (s_mm, mmc["n"])
            sm_sig = None
            t3 = smT[:].rearrange("p (g o) -> p g o", o=1)
            for (dstb, base) in ((blg[0], 0), (blg[1], 4), (ggs, 8),
                                 (hhs, 12)):
                d3 = dstb[:].rearrange("p (g o) -> p g o", o=1)
                a3 = smv[:, :, base + hh:base + hh + 1]
                b3 = smv[:, :, base + 2 + hh:base + 2 + hh + 1]
                p.op("gpsimd", lambda e, d3=d3, a3=a3: e.tensor_scalar(
                        d3, a3, mflc[:, 0:1], None, ALU.mult),
                     waits=[(ld_sm, LDSM)] + ([head_free] if base == 0 else []))
                p.op("gpsimd", lambda e, b3=b3: e.tensor_scalar(
                        t3, b3, mflc[:, 1:2], None, ALU.mult))
                sm_sig = p.op("gpsimd", lambda e, d3=d3: e.tensor_add(
                        d3, d3, t3))

            for (ti, tsr) in ((0, qtm), (1, ktm)):
                for g in range(NG):
                    p.op("vector", lambda e, tsr=tsr, g=g: e.tensor_mul(
                            T_vec[:, 0:256], tsr[:, 256 * g:256 * (g + 1)],
                            tsr[:, 256 * g:256 * (g + 1)]),
                         waits=([sm_sig, conv_ev, mz_sig, (ld_sm, LDSM),
                                 (s_st[0], 16 * nstp[0]),
                                 (s_st[1], 16 * nstp[1])]
                                if (ti == 0 and g == 0) else []))
                    p.op("vector", lambda e, ti=ti, g=g: e.tensor_reduce(
                            nrm[:, ti * NG + g:ti * NG + g + 1],
                            T_vec[:, 0:256], AX.X, ALU.add))
            ns = p.op("vector", lambda e: e.tensor_scalar(
                    nrm[:], nrm[:], 1e-12, None, ALU.add))
            p.op("scalar", lambda e: e.activation(nrm[:], nrm[:], AF.Ln),
                 waits=[ns])
            rs = p.op("scalar", lambda e: e.activation(nrm[:], nrm[:], AF.Exp,
                                                       scale=-0.5))
            sc_sig = None
            for (ti, tsr) in ((0, qtm), (1, ktm)):
                for g in range(NG):
                    sc_sig = p.op("scalar", lambda e, ti=ti, tsr=tsr, g=g:
                         e.activation(
                            tsr[:, 256 * g:256 * (g + 1)],
                            tsr[:, 256 * g:256 * (g + 1)], AF.Identity,
                            scale=nrm[:, ti * NG + g:ti * NG + g + 1]),
                         waits=([rs] if (ti == 0 and g == 0) else ()))
            p.op("vector", lambda e: e.tensor_scalar(nblg[0][:], blg[0][:],
                                                     -1.0, None, ALU.mult),
                 waits=[sm_sig])
            nb_sig = p.op("vector", lambda e: e.tensor_scalar(
                    nblg[1][:], blg[1][:], -1.0, None, ALU.mult))
            W0 = gwk
            p.op("vector", lambda e, hh=hh: e.tensor_scalar(
                    W0[:, 0:NG], ggs[:], bghs[:, 2 * hh:2 * hh + 1], None,
                    ALU.add),
                 waits=[sm_sig])
            p.op("vector", lambda e, hh=hh: e.tensor_scalar(
                    W0[:, NG:2 * NG], hhs[:], bghs[:, 2 * hh + 1:2 * hh + 2],
                    None, ALU.add))
            p.op("vector", lambda e: e.tensor_max(W0[:, 2 * NG:3 * NG],
                                                  W0[:, 0:NG],
                                                  W0[:, NG:2 * NG]))
            p.op("vector", lambda e: e.tensor_scalar(
                    W0[:, 2 * NG:3 * NG], W0[:, 2 * NG:3 * NG], 0.0, None,
                    ALU.max))
            p.op("vector", lambda e: e.tensor_sub(W0[:, 0:NG], W0[:, 0:NG],
                                                  W0[:, 2 * NG:3 * NG]))
            sb_sig = p.op("vector", lambda e: e.tensor_sub(
                    W0[:, NG:2 * NG], W0[:, NG:2 * NG], W0[:, 2 * NG:3 * NG]))
            p.op("scalar", lambda e: e.activation(W0[:, 3 * NG:4 * NG],
                                                  W0[:, 0:NG], AF.Exp),
                 waits=[sb_sig])
            p.op("scalar", lambda e: e.activation(W0[:, 4 * NG:5 * NG],
                                                  W0[:, NG:2 * NG], AF.Exp))
            ex_sig = p.op("scalar", lambda e: e.activation(
                    W0[:, 5 * NG:6 * NG], W0[:, 2 * NG:3 * NG], AF.Exp,
                    scale=-1.0))
            p.op("vector", lambda e: e.tensor_add(W0[:, 6 * NG:7 * NG],
                                                  W0[:, 3 * NG:4 * NG],
                                                  W0[:, 4 * NG:5 * NG]),
                 waits=[ex_sig])
            p.op("vector", lambda e: e.tensor_add(W0[:, 6 * NG:7 * NG],
                                                  W0[:, 6 * NG:7 * NG],
                                                  W0[:, 5 * NG:6 * NG]))
            p.op("vector", lambda e: e.reciprocal(W0[:, 7 * NG:8 * NG],
                                                  W0[:, 6 * NG:7 * NG]))
            p.op("vector", lambda e: e.tensor_mul(wlg[0][:],
                                                  W0[:, 3 * NG:4 * NG],
                                                  W0[:, 7 * NG:8 * NG]))
            gate_sig = p.op("vector", lambda e: e.tensor_mul(
                    wlg[1][:], W0[:, 4 * NG:5 * NG], W0[:, 7 * NG:8 * NG]))

            with nc.psum_tensor(f"d_pq{hh}", [128, 1024], BF16) as pq, \
                 nc.psum_tensor(f"d_kk_ps{hh}", [128, 512], F32) as kk_ps:
                ev_sig = None
                for (tsr, fmt) in ((qtm, q_fm), (ktm, k_fm)):
                    for g in range(NG):
                        par = g % 2
                        for cc in range(2):
                            wts = []
                            if cc == 0:
                                wts = [sc_sig, ev_sig]
                            p.op("tensor", lambda e, tsr=tsr, g=g, cc=cc,
                                 par=par:
                                 e.transpose(pq[:, par * 256 + cc * 128:
                                                par * 256 + (cc + 1) * 128],
                                             tsr[:, 256 * g + 128 * cc:
                                                 256 * g + 128 * (cc + 1)],
                                             c.ident_bf),
                                 waits=wts,
                                 incs=([(s_tq, 1)] if cc == 1 else []))
                        tqc["n"] += 1
                        for cc in range(2):
                            ev_sig = p.op("scalar",
                                 lambda e, fmt=fmt, g=g, cc=cc, par=par:
                                 e.copy(
                                     fmt[:, cc * L4L + 128 * g:
                                         cc * L4L + 128 * (g + 1)],
                                     pq[:, par * 256 + cc * 128:
                                        par * 256 + (cc + 1) * 128]),
                                 waits=[(s_tq, tqc["n"])])
                t0_sig = None
                for g in range(NG):
                    for cc in range(2):
                        wts = []
                        if cc == 0:
                            wts = [ev_sig, nb_sig, t0_sig]
                        p.op("tensor", lambda e, g=g, cc=cc: e.matmul(
                                kk_ps[:, 0:128],
                                k_fm[:, cc * L4L + 128 * g:
                                     cc * L4L + 128 * (g + 1)],
                                k_fm[:, cc * L4L + 128 * g:
                                     cc * L4L + 128 * (g + 1)],
                                start=(cc == 0), stop=(cc == 1)),
                             waits=wts,
                             incs=([(s_mm, 1)] if cc == 1 else []))
                    mmc["n"] += 1
                    p.op("vector", lambda e: e.tensor_mul(
                            kkm_sb[:], kk_ps[:, 0:128], MS_BD),
                         waits=[(s_mm, mmc["n"])])
                    for s in range(2):
                        t0_sig = p.op("vector", lambda e, g=g, s=s:
                             e.tensor_scalar(
                                T0F[s][:, 128 * g:128 * (g + 1)], kkm_sb[:],
                                nblg[s][:, g:g + 1], None, ALU.mult))
                prev_tv_done = None
                for s in range(2):
                    p.op("sync", lambda e, s=s: e.dma_start(out=scr1,
                                                            in_=T0F[s][:]),
                         waits=[t0_sig, shuf_done], incs=[(ld_sh, 16)])
                    nsh["n"] += 16
                    WSC = nsh["n"]
                    for a in range(4):
                        p.op("sync", lambda e, a=a: e.dma_start(
                                out=T_vec[32 * a:32 * (a + 1), :]
                                    .rearrange("p (r cc) -> p r cc", r=32),
                                in_=scr1.rearrange(
                                    "(A r) (g blk cc) -> A blk g r cc",
                                    A=4, blk=4, cc=32)[a, a]),
                             waits=([prev_tv_done, (ld_sh, WSC)] if a == 0
                                    else [(ld_sh, WSC)]),
                             incs=[(ld_sh, 16)])
                        nsh["n"] += 16
                    rec_sig = None
                    first_rec = [(ld_sh, nsh["n"])]
                    for i in range(1, 32):
                        for k2 in range(i):
                            if i == 1:
                                src_out = T_vec[:, 32:64]
                                in1 = T_vec[:, 32:64]
                            elif k2 == 0:
                                src_out = acc32[:]
                                in1 = T_vec[:, 32 * i:32 * (i + 1)]
                            elif k2 == i - 1:
                                src_out = T_vec[:, 32 * i:32 * (i + 1)]
                                in1 = acc32[:]
                            else:
                                src_out = acc32[:]
                                in1 = acc32[:]
                            rec_sig = p.op("vector",
                                 lambda e, i=i, k2=k2, src_out=src_out, in1=in1:
                                 e.scalar_tensor_tensor(
                                     src_out, T_vec[:, 32 * k2:32 * (k2 + 1)],
                                     T_vec[:, 32 * i + k2:32 * i + k2 + 1],
                                     in1, ALU.mult, ALU.add),
                                 waits=(first_rec if (i == 1 and k2 == 0)
                                        else ()))
                            first_rec = []
                    p.op("vector", lambda e: e.tensor_scalar(
                            T_vec[:, 0:1024:33], T_vec[:, 0:1024:33], 1.0,
                            None, ALU.add))
                    prev_tv_done = p.op("vector", lambda e: e.tensor_copy(
                            T_vt[:].rearrange("p (cc r) -> p cc r", cc=32),
                            T_vec[:].rearrange("p (r cc) -> p r cc", r=32)
                                    .rearrange("p r cc -> p cc r")))
                    tb_sig = p.op("vector", lambda e: e.tensor_copy(
                            T_bfs[:], T_vt[:]))
                    p.op("sync", lambda e: e.dma_start(out=scr2, in_=T_bfs[:]),
                         waits=[tb_sig], incs=[(ld_sh, 16)])
                    nsh["n"] += 16
                    WSC2 = nsh["n"]
                    for a in range(4):
                        p.op("sync", lambda e, s=s, a=a: e.dma_start(
                                out=TB[s][32 * a:32 * (a + 1), :]
                                    .rearrange("k (g blk r) -> k g blk r",
                                               blk=4, r=32)[:, :, a, :],
                                in_=scr2.rearrange("(A g) (k r) -> A g k r",
                                                   A=4, r=32)[a]
                                        .rearrange("g k r -> k g r")),
                             waits=[(ld_sh, WSC2), mz_sig],
                             incs=[(ld_sh, 16)])
                        nsh["n"] += 16
                    shuf_done = (ld_sh, nsh["n"])
            SHUF = (ld_sh, nsh["n"])

            p.op("vector", lambda e: e.memset(S_bf[0][:], 0.0),
                 waits=[head_free])
            p.op("vector", lambda e: e.memset(S_bf[1][:], 0.0))
            p.op("vector", lambda e: e.memset(S_sb[0][:], 0.0))
            sbz = p.op("vector", lambda e: e.memset(S_sb[1][:], 0.0))
            with nc.psum_tensor(f"d_sd_l{hh}", [128, 512], F32) as sd_l, \
                 nc.psum_tensor(f"d_sd_g{hh}", [128, 512], F32) as sd_g, \
                 nc.psum_tensor(f"d_u_l{hh}", [128, 512], F32) as u_l, \
                 nc.psum_tensor(f"d_u_g{hh}", [128, 512], F32) as u_g, \
                 nc.psum_tensor(f"d_o_ps{hh}", [128, 512], F32) as o_ps, \
                 nc.psum_tensor(f"d_gram{hh}", [128, 512], F32) as gram:
                sd_ps = [sd_l, sd_g]
                u_ps = [u_l, u_g]
                vk_free = None
                wm_free = None
                gram_free = None
                sbf_sig = {0: sbz, 1: sbz}
                ext_sigs = {}
                o_ev = {0: None, 1: None}
                for g in range(NG):
                    p.op("scalar", lambda e, g=g: e.activation(
                            vs_sb[:, 0:256], vtm[:, 256 * g:256 * (g + 1)],
                            AF.Identity, scale=blg[0][:, g:g + 1]),
                         waits=[vk_free, gate_sig, SHUF])
                    p.op("scalar", lambda e, g=g: e.activation(
                            vs_sb[:, 256:512], vtm[:, 256 * g:256 * (g + 1)],
                            AF.Identity, scale=blg[1][:, g:g + 1]))
                    p.op("scalar", lambda e, g=g: e.activation(
                            kb_sb[:, 0:256], ktm[:, 256 * g:256 * (g + 1)],
                            AF.Identity, scale=nblg[0][:, g:g + 1]))
                    vk_sig = p.op("scalar", lambda e, g=g: e.activation(
                            kb_sb[:, 256:512], ktm[:, 256 * g:256 * (g + 1)],
                            AF.Identity, scale=nblg[1][:, g:g + 1]))
                    for s in range(2):
                        p.op("tensor", lambda e, s=s, g=g: e.matmul(
                                u_ps[s][:, 0:256], TB[s][:, 128 * g:128 * (g + 1)],
                                vs_sb[:, s * 256:(s + 1) * 256],
                                start=True, stop=False),
                             waits=([vk_sig, SHUF, sbf_sig[0], sbf_sig[1]]
                                    if s == 0 else []))
                    for s in range(2):
                        for cc in range(2):
                            p.op("tensor", lambda e, s=s, cc=cc, g=g: e.matmul(
                                    gram[:, s * 256 + cc * 128:
                                         s * 256 + (cc + 1) * 128],
                                    kb_sb[:, s * 256 + cc * 128:
                                          s * 256 + (cc + 1) * 128],
                                    TB[s][:, 128 * g:128 * (g + 1)],
                                    start=(s == 0 and cc == 0),
                                    stop=(s == 1 and cc == 1)),
                                 waits=([gram_free] if (s == 0 and cc == 0)
                                        else []),
                                 incs=([(s_mm, 1)] if (s == 1 and cc == 1)
                                       else []))
                    mmc["n"] += 1
                    wm_sig = p.op("scalar", lambda e: e.copy(
                            wm_sb[:], gram[:, 0:512]),
                         waits=[(s_mm, mmc["n"])])
                    for s in range(2):
                        for cc in range(2):
                            p.op("tensor", lambda e, s=s, cc=cc: e.matmul(
                                    u_ps[s][:, 0:256],
                                    wm_sb[:, s * 256 + cc * 128:
                                          s * 256 + (cc + 1) * 128],
                                    S_bf[s][:, cc * 256:(cc + 1) * 256],
                                    start=False, stop=(cc == 1)),
                                 waits=([wm_sig] if (s == 0 and cc == 0)
                                        else []))
                    for s in range(2):
                        for cc in range(2):
                            p.op("tensor", lambda e, s=s, cc=cc, g=g: e.matmul(
                                    gram[:, s * 128:(s + 1) * 128],
                                    k_fm[:, cc * L4L + 128 * g:
                                         cc * L4L + 128 * (g + 1)],
                                    wm_sb[:, s * 256 + cc * 128:
                                          s * 256 + (cc + 1) * 128],
                                    start=(s == 0 and cc == 0),
                                    stop=(s == 1 and cc == 1)),
                                 waits=([wm_sig] if (s == 0 and cc == 0)
                                        else []),
                                 incs=([(s_mm, 1)] if (s == 1 and cc == 1)
                                       else []))
                    mmc["n"] += 1
                    wk3 = None
                    for s in range(2):
                        p.op("vector", lambda e, s=s: e.tensor_mul(
                                wkm_sb[s][:], gram[:, s * 128:(s + 1) * 128],
                                MS_BL),
                             waits=([(s_mm, mmc["n"])] if s == 0 else ()))
                        wk3 = p.op("vector", lambda e, s=s: e.tensor_copy(
                                wkm3[s][:, 96:128], wkm_sb[s][:, 96:128]))
                    wm_free = wk3
                    EXT_SLICES = [(0, 32), (32, 64), (64, 96), (64, 128)]
                    for s in range(2):
                        ext_sigs[(0, s)] = p.op("scalar",
                             lambda e, s=s, g=g: e.activation(
                                Usb[s][0:32, :], u_ps[s][0:32, 0:256],
                                AF.Identity, scale=wlg[s][0:32, g:g + 1]),
                             waits=[wk3] if s == 0 else [])
                    for a in range(1, 4):
                        for s in range(2):
                            if a < 3:
                                p.op("tensor", lambda e, s=s, a=a: e.matmul(
                                        u_ps[s][32 * a:32 * (a + 1), 0:256],
                                        wkm_sb[s][:, 32 * a:32 * (a + 1)],
                                        Usb[s][:],
                                        start=False, stop=True,
                                        skip_group_check=True),
                                     waits=[ext_sigs[(a - 1, s)]],
                                     incs=([(s_mm, 1)] if s == 1 else []))
                            else:
                                p.op("tensor", lambda e, s=s: e.matmul(
                                        u_ps[s][:, 0:256], wkm3[s][:],
                                        Usb[s][:],
                                        start=False, stop=True,
                                        skip_group_check=True),
                                     waits=[ext_sigs[(2, s)]],
                                     incs=([(s_mm, 1)] if s == 1 else []))
                        mmc["n"] += 1
                        lo, hi = EXT_SLICES[a]
                        for s in range(2):
                            ext_sigs[(a, s)] = p.op("scalar",
                                 lambda e, s=s, g=g, lo=lo, hi=hi:
                                 e.activation(
                                    Usb[s][lo:hi, :], u_ps[s][lo:hi, 0:256],
                                    AF.Identity, scale=wlg[s][lo:hi, g:g + 1]),
                                 waits=([(s_mm, mmc["n"])] if s == 0 else []))
                    uprime_done = ext_sigs[(3, 1)]
                    for s in range(2):
                        for cc in range(2):
                            p.op("tensor", lambda e, s=s, cc=cc, g=g: e.matmul(
                                    o_ps[:, 0:256],
                                    q_fm[:, cc * L4L + 128 * g:
                                         cc * L4L + 128 * (g + 1)],
                                    S_bf[s][:, cc * 256:(cc + 1) * 256],
                                    start=(s == 0 and cc == 0), stop=False),
                                 waits=([o_ev[0], o_ev[1]]
                                        if (s == 0 and cc == 0) else []))
                    for cc in range(2):
                        p.op("tensor", lambda e, cc=cc, g=g: e.matmul(
                                gram[:, 0:128],
                                k_fm[:, cc * L4L + 128 * g:
                                     cc * L4L + 128 * (g + 1)],
                                q_fm[:, cc * L4L + 128 * g:
                                     cc * L4L + 128 * (g + 1)],
                                start=(cc == 0), stop=(cc == 1)),
                             waits=([wm_free] if cc == 0 else []),
                             incs=([(s_mm, 1)] if cc == 1 else []))
                    mmc["n"] += 1
                    gr_sig = p.op("vector", lambda e: e.tensor_mul(
                            grm_sb[:], gram[:, 0:128], MS_CA),
                         waits=[(s_mm, mmc["n"])])
                    gram_free = gr_sig
                    for s in range(2):
                        p.op("tensor", lambda e, s=s: e.matmul(
                                o_ps[:, 0:256], grm_sb[:], Usb[s][:],
                                start=False, stop=(s == 1)),
                             waits=([gr_sig, uprime_done] if s == 0 else []),
                             incs=([(s_mm, 1)] if s == 1 else []))
                    mmc["n"] += 1
                    OMM = mmc["n"]
                    for s in range(2):
                        for cc in range(2):
                            p.op("tensor", lambda e, s=s, cc=cc, g=g: e.matmul(
                                    sd_ps[s][:, cc * 256:(cc + 1) * 256],
                                    ktm[:, 256 * g + 128 * cc:
                                        256 * g + 128 * (cc + 1)],
                                    Usb[s][:],
                                    start=(cc == 0), stop=(cc == 1)),
                                 waits=([sbf_sig[s]] if cc == 0 else []),
                                 incs=([(s_mm, 1)] if (s == 1 and cc == 1)
                                       else []))
                    mmc["n"] += 1
                    SMM = mmc["n"]
                    par = g % 2
                    oev = p.op("scalar", lambda e, par=par: e.copy(
                            o_stage[:, par * 256:(par + 1) * 256],
                            o_ps[:, 0:256]),
                         waits=[(s_mm, OMM), (s_st[par], 16 * nstp[par])])
                    o_ev[par] = oev
                    p.op("sync", lambda e, hh=hh, g=g, par=par: e.dma_start(
                            out=cc2_in[(hh * NG + g) * 128:
                                       (hh * NG + g + 1) * 128, :],
                            in_=o_stage[:, par * 256:(par + 1) * 256]),
                         waits=[oev], incs=[(s_st[par], 16)])
                    nstp[par] += 1
                    for s in range(2):
                        sadd = p.op("vector", lambda e, s=s: e.tensor_add(
                                S_sb[s][:], S_sb[s][:], sd_ps[s][:, 0:512]),
                             waits=([(s_mm, SMM)] if s == 0 else ()))
                        sbf_sig[s] = p.op("scalar", lambda e, s=s: e.copy(
                                S_bf[s][:], S_sb[s][:]), waits=[sadd])
                    vk_free = (s_mm, SMM)
                head_free = sbf_sig[1]


def build_fused():
    nc = bass.Bass("TRN2", target_bir_lowering=False, debug=False)
    import ml_dtypes
    dri = lambda n, shp, dt=F32: nc.dram_tensor(n, shp, dt,
                                                kind="ExternalInput").ap()
    a_x = dri("a_x", [D, EXTL])
    a_wq = dri("a_wq", [D, D])
    a_wk = dri("a_wk", [D, D])
    a_wv = dri("a_wv", [D, D])
    a_wo = dri("a_wo", [D, D])
    a_kinit = dri("a_kinit", [128, 8 * 63])
    a_vinit = dri("a_vinit", [128, 8 * 63])
    a_maskc = dri("a_maskc", [128, 256])
    b_w1 = dri("b_w1", [D, 4096])
    b_w2 = dri("b_w2", [4096, D])
    b_b1t = dri("b_b1t", [128, 32])
    b_b2t = dri("b_b2t", [128, 8])
    b_a1h = dri("b_a1h", [128, 8 * 24])
    b_b1c = dri("b_b1c", [24, 4096])
    b_r1h = dri("b_r1h", [128, 8 * 3])
    b_a2h = dri("b_a2h", [128, 32 * 24])
    b_b2c = dri("b_b2c", [24, D])
    b_r2h = dri("b_r2h", [128, 32 * 3])
    c_wq = dri("c_wq", [D, D])
    c_wk = dri("c_wk", [D, D])
    c_wv = dri("c_wv", [D, D])
    c_convs = dri("c_convs", [128, 8 * 12])
    c_wsm = dri("c_wsm", [D, 96])
    d_msk = dri("d_msk", [128, 416])
    d_bgh = dri("d_bgh", [128, 4])
    e_bgh = dri("e_bgh", [128, 8])
    e_nw = dri("e_nw", [128, 1024])
    e_wo = dri("e_wo", [D, D])
    f_w1 = dri("f_w1", [4096, D])
    f_b1t = dri("f_b1t", [128, 32])
    f_w2 = dri("f_w2", [D, 4096])
    f_b2t = dri("f_b2t", [128, 8])
    mflag = dri("mflag", [128, 2])
    selm = dri("selm", [128, 256], BF16)
    y = nc.dram_tensor("y", [D, TOK], F32, kind="ExternalOutput").ap()
    drt = lambda n, shp, dt: nc.dram_tensor(n, shp, dt, kind="Internal").ap()
    xb1 = drt("xb1", [D, TOK], F32)
    xb2e = drt("xb2e", [D, EXT3], F32)
    xb3 = drt("xb3", [D, TOK], F32)
    sv_in = drt("sv_in", [D, 3], F32)
    sv_out = drt("sv_out", [2 * D, 3], F32)
    ccq_in = drt("ccq_in", [3 * D, TOK], BF16)
    ccq_out = drt("ccq_out", [6 * D, TOK], BF16)
    ccs_in = drt("ccs_in", [TOK, 16], F32)
    ccs_out = drt("ccs_out", [2 * TOK, 16], F32)
    cc2_in = drt("cc2_in", [8192, 256], BF16)
    cc2_out = drt("cc2_out", [16384, 256], BF16)
    scr1 = drt("d_scr1", [128, 4096], F32)
    scr2 = drt("d_scr2", [128, 1024], BF16)

    c = Consts(nc)
    e24_np = np.zeros((3, 24), np.float32)
    for e_ in range(3):
        e24_np[e_, e_ * 8:(e_ + 1) * 8] = 1.0
    e24_dram = nc.inline_tensor(e24_np.astype(ml_dtypes.bfloat16),
                                name="e24").ap()

    with ExitStack() as st:
        p = Prog(nc)
        c.materialize(p, st, extra_dram=[
            ("e24", e24_dram, [3, 24], BF16),
            ("mflag", mflag, [128, 2], F32),
            ("selm", selm, [128, 256], BF16)])
        X = Ctx(nc, p, c)
        s_cc = X.S("s_cc")
        s_z = X.S("s_z")
        ld_m = X.S("ld_m")
        svb = st.enter_context(nc.sbuf_tensor("svb", [128, 24], F32))
        zb = st.enter_context(nc.sbuf_tensor("zb", [128, 256], F32))
        p.op("gpsimd", lambda e: e.memset(zb[:], 0.0), setup=True, dma=True,
             incs=[(s_z, 1)])
        for mc in range(KD):
            p.op("sync", lambda e, mc=mc: e.dma_start(
                    out=xb2e[mc * 128:(mc + 1) * 128, 2051:2304],
                    in_=zb[:, 0:253]),
                 waits=[(s_z, 1)], incs=[(c.s_const, 16)], setup=True)
            c.ready_val += 16

        with ExitStack() as stA:
            emit_l0attn(X, stA, a_x, a_wq, a_wk, a_wv, a_wo, a_kinit, a_vinit,
                        a_maskc, xb1)
        X.join()
        with ExitStack() as stB:
            emit_l0ffn(X, stB, xb1, b_w1, b_w2, b_b1t, b_b2t, b_a1h, b_b1c,
                       b_r1h, b_a2h, b_b2c, b_r2h, xb2e, 3)
        X.join()
        p.op("sync", lambda e: e.dma_start(
                out=svb.rearrange("p (cx t) -> p cx t", cx=KD)[:, :, 0:3],
                in_=xb2e[:, 2048:2051]
                    .rearrange("(cx p) t -> cx p t", p=128)
                    .rearrange("cx p t -> p cx t")),
             incs=[(ld_m, 16)])
        X.bump("ld_m")
        p.op("sync", lambda e: e.dma_start(
                out=sv_in.rearrange("(cx p) t -> cx p t", p=128)
                         .rearrange("cx p t -> p cx t"),
                in_=svb.rearrange("p (cx t) -> p cx t", cx=KD)[:, :, 0:3]),
             waits=[(ld_m, X.val("ld_m"))], incs=[(ld_m, 16)])
        X.bump("ld_m")
        p.op("gpsimd", lambda e: e.collective_compute(
                "AllGather", mybir.AluOpType.bypass,
                replica_groups=PAIR_GROUPS,
                ins=[sv_in.opt()], outs=[sv_out.opt()]),
             waits=[(ld_m, X.val("ld_m"))], incs=[(s_cc, 1)], dma=True)
        p.op("sync", lambda e: e.dma_start(
                out=svb.rearrange("p (cx t) -> p cx t", cx=KD)[:, :, 0:3],
                in_=sv_out[0:D, :]
                    .rearrange("(cx p) t -> cx p t", p=128)
                    .rearrange("cx p t -> p cx t")),
             waits=[(s_cc, 1)], incs=[(ld_m, 16)])
        X.bump("ld_m")
        hb = p.op("vector", lambda e: e.tensor_scalar(
                svb[:, 0:24], svb[:, 0:24], c.extra["mflag"][:, 1:2],
                None, ALU.mult),
             waits=[(ld_m, X.val("ld_m"))])
        p.op("sync", lambda e: e.dma_start(
                out=xb2e[:, 0:3]
                    .rearrange("(cx p) t -> cx p t", p=128)
                    .rearrange("cx p t -> p cx t"),
                in_=svb.rearrange("p (cx t) -> p cx t", cx=KD)[:, :, 0:3]),
             waits=[hb], incs=[(ld_m, 16)])
        X.bump("ld_m")
        HALO = (ld_m, X.val("ld_m"))
        with ExitStack() as stC:
            p.op("sync", lambda e: e.nop(), waits=[HALO])
            emit_l1front(X, stC, xb2e, [c_wq, c_wk, c_wv], c_convs, c_wsm,
                         ccq_in, ccs_in)
        jC = X.join()
        p.op("gpsimd", lambda e: e.collective_compute(
                "AllGather", mybir.AluOpType.bypass,
                replica_groups=PAIR_GROUPS,
                ins=[ccq_in.opt()], outs=[ccq_out.opt()]),
             waits=[jC], incs=[(s_cc, 1)], dma=True)
        p.op("gpsimd", lambda e: e.collective_compute(
                "AllGather", mybir.AluOpType.bypass,
                replica_groups=PAIR_GROUPS,
                ins=[ccs_in.opt()], outs=[ccs_out.opt()]),
             incs=[(s_cc, 1)], dma=True)
        with ExitStack() as stD:
            emit_l1scan(X, stD, ccq_out, ccs_out, d_msk, d_bgh, cc2_in,
                        scr1, scr2)
        jD = X.join()
        p.op("gpsimd", lambda e: e.collective_compute(
                "AllGather", mybir.AluOpType.bypass,
                replica_groups=PAIR_GROUPS,
                ins=[cc2_in.opt()], outs=[cc2_out.opt()]),
             waits=[jD], incs=[(s_cc, 1)], dma=True)
        with ExitStack() as stE:
            emit_l1comb(X, stE, cc2_out, ccq_in, ccs_in, e_bgh, e_nw, e_wo,
                        xb2e, 3, xb3)
        X.join()
        with ExitStack() as stF:
            emit_l1ffn(X, stF, xb3, f_w1, f_b1t, f_w2, f_b2t, y)
        X.join()
        p.emit(c.s_const, c.ready_val)
    return nc


def kernel(L0_attn_x, L0_attn_Wq, L0_attn_Wk, L0_attn_Wv, L0_attn_Wo,
           L0_attn_last_k_init, L0_attn_last_v_init,
           L0_ffn_W1, L0_ffn_b1, L0_ffn_A1, L0_ffn_B1, L0_ffn_R1,
           L0_ffn_W2, L0_ffn_b2, L0_ffn_A2, L0_ffn_B2, L0_ffn_R2,
           L1_attn_Wq, L1_attn_Wk, L1_attn_Wv, L1_attn_conv_q, L1_attn_conv_k,
           L1_attn_conv_v, L1_attn_Wb, L1_attn_Wg, L1_attn_bg, L1_attn_Wh,
           L1_attn_bh, L1_attn_norm_w, L1_attn_Wo,
           L1_ffn_w1, L1_ffn_b1, L1_ffn_w2, L1_ffn_b2):
    import ml_dtypes
    f32 = lambda a: np.asarray(a, np.float32)
    nc = build_fused()

    x0 = f32(L0_attn_x)
    xin = np.zeros((NCORES, EXTL, D), np.float32)
    for i in range(NCORES):
        j, s = i // 2, i % 2
        xin[i, 63:63 + TOK] = x0[j, s * TOK:(s + 1) * TOK]
        if s == 1:
            xin[i, 0:63] = x0[j, TOK - 63:TOK]
    xin_t = np.ascontiguousarray(xin.transpose(0, 2, 1))
    ki = np.ascontiguousarray(f32(L0_attn_last_k_init).reshape(63, D).T
                              .reshape(8, 128, 63).transpose(1, 0, 2)
                              .reshape(128, 504))
    vi = np.ascontiguousarray(f32(L0_attn_last_v_init).reshape(63, D).T
                              .reshape(8, 128, 63).transpose(1, 0, 2)
                              .reshape(128, 504))
    maskc = np.full((128, 256), -30000.0, np.float32)
    for r in range(64):
        for qq in range(2):
            maskc[r, qq * 128 + r:qq * 128 + r + 64] = 0.0
            maskc[64 + r, qq * 128 + r:qq * 128 + r + 64] = 0.0
    a1hh = np.ascontiguousarray(
        f32(L0_ffn_A1).transpose(1, 0, 2).reshape(8, 128, 24).transpose(1, 0, 2)
        .reshape(128, 8 * 24))
    a2hh = np.ascontiguousarray(
        f32(L0_ffn_A2).transpose(1, 0, 2).reshape(32, 128, 24).transpose(1, 0, 2)
        .reshape(128, 32 * 24))
    b1cc = np.ascontiguousarray(f32(L0_ffn_B1).reshape(24, 4096))
    b2cc = np.ascontiguousarray(f32(L0_ffn_B2).reshape(24, D))
    r1hh = np.ascontiguousarray(
        f32(L0_ffn_R1).reshape(8, 128, 3).transpose(1, 0, 2).reshape(128, 24))
    r2hh = np.ascontiguousarray(
        f32(L0_ffn_R2).reshape(32, 128, 3).transpose(1, 0, 2).reshape(128, 96))
    b1tt = np.ascontiguousarray(f32(L0_ffn_b1).reshape(32, 128).T)
    b2tt = np.ascontiguousarray(f32(L0_ffn_b2).reshape(8, 128).T)
    cv = np.zeros((128, 8 * 12), np.float32)
    for kcx in range(8):
        for ti, cw in enumerate((L1_attn_conv_q, L1_attn_conv_k,
                                 L1_attn_conv_v)):
            cv[:, kcx * 12 + ti * 4:kcx * 12 + (ti + 1) * 4] = \
                f32(cw)[kcx * 128:(kcx + 1) * 128, :]
    wsm_np = np.zeros((D, 96), np.float32)
    wsm_np[:, 0:8] = f32(L1_attn_Wb)
    wsm_np[:, 32:36] = f32(L1_attn_Wg)
    wsm_np[:, 64:68] = f32(L1_attn_Wh)
    msk = np.zeros((128, 416), np.float32)
    rr = np.arange(128)
    blk = rr // 32
    for i in range(128):
        for jj in range(128):
            if blk[i] == blk[jj] and (i % 32) > (jj % 32):
                msk[i, 32 + jj] = 1.0
            if blk[i] < blk[jj]:
                msk[i, 160 + jj] = 1.0
            if blk[i] < blk[jj] or (blk[i] == blk[jj] and i <= jj):
                msk[i, 288 + jj] = 1.0
    bghv8 = np.zeros((128, 8), np.float32)
    bghv8[:, 0:4] = f32(L1_attn_bg)[None, :]
    bghv8[:, 4:8] = f32(L1_attn_bh)[None, :]
    nwv = np.broadcast_to(f32(L1_attn_norm_w)[None, :], (128, 256))
    nwt = np.ascontiguousarray(np.tile(nwv, (1, 4)))
    fb1tt = np.ascontiguousarray(f32(L1_ffn_b1).reshape(32, 128).T)
    fb2tt = np.ascontiguousarray(f32(L1_ffn_b2).reshape(8, 128).T)

    eye = np.eye(128, dtype=np.float32)
    in_maps = []
    for i in range(NCORES):
        j, s = i // 2, i % 2
        mfl = np.zeros((128, 2), np.float32)
        mfl[:, 0] = 1.0 if s == 0 else 0.0
        mfl[:, 1] = 0.0 if s == 0 else 1.0
        selm = np.zeros((128, 256), np.float32)
        selm[:, 0:128] = eye * (1.0 if s == 0 else 0.0)
        selm[:, 128:256] = eye * (0.0 if s == 0 else 1.0)
        bghv4 = np.zeros((128, 4), np.float32)
        for hh in range(2):
            h = 2 * s + hh
            bghv4[:, 2 * hh] = float(f32(L1_attn_bg)[h])
            bghv4[:, 2 * hh + 1] = float(f32(L1_attn_bh)[h])
        in_maps.append({
            "a_x": xin_t[i], "a_wq": f32(L0_attn_Wq), "a_wk": f32(L0_attn_Wk),
            "a_wv": f32(L0_attn_Wv), "a_wo": f32(L0_attn_Wo),
            "a_kinit": ki, "a_vinit": vi, "a_maskc": maskc,
            "b_w1": f32(L0_ffn_W1), "b_w2": f32(L0_ffn_W2),
            "b_b1t": b1tt, "b_b2t": b2tt, "b_a1h": a1hh, "b_b1c": b1cc,
            "b_r1h": r1hh, "b_a2h": a2hh, "b_b2c": b2cc, "b_r2h": r2hh,
            "c_wq": f32(L1_attn_Wq), "c_wk": f32(L1_attn_Wk),
            "c_wv": f32(L1_attn_Wv), "c_convs": cv, "c_wsm": wsm_np,
            "d_msk": msk, "d_bgh": bghv4,
            "e_bgh": bghv8, "e_nw": nwt, "e_wo": f32(L1_attn_Wo),
            "f_w1": f32(L1_ffn_w1), "f_b1t": fb1tt,
            "f_w2": f32(L1_ffn_w2), "f_b2t": fb2tt,
            "mflag": mfl,
            "selm": selm.astype(ml_dtypes.bfloat16),
        })
    res = run_bass_kernel_spmd(nc, in_maps, core_ids=list(range(NCORES)))
    yv = np.stack([r["y"] for r in res.results])
    return np.ascontiguousarray(yv.transpose(0, 2, 1)).reshape(4, 4096, D)


# version 08
# speedup vs baseline: 1.0041x; 1.0041x over previous
import numpy as np
from contextlib import ExitStack
import concourse.bass as bass
import concourse.mybir as mybir
from concourse.bass_utils import run_bass_kernel_spmd

F32 = mybir.dt.float32
BF16 = mybir.dt.bfloat16
AF = mybir.ActivationFunctionType
ALU = mybir.AluOpType
AX = mybir.AxisListType
NCORES = 8
D = 1024
KD = 8
TOK = 2048
EPS = 1e-5
CHAINED = ("vector", "gpsimd", "scalar")
PAIR_GROUPS = [[0, 1], [2, 3], [4, 5], [6, 7]]


class Prog:

    def __init__(self, nc):
        self.nc = nc
        self.ops = []
        self.self_sems = {e: nc.alloc_semaphore("self_" + e) for e in CHAINED}
        self.k = {e: 0 for e in CHAINED}
        self.all_sems = list(self.self_sems.values())
        self.pending_pre = {}

    def sem(self, name):
        h = self.nc.alloc_semaphore(name)
        self.all_sems.append(h)
        return h

    def set_phase_pre(self, sig):
        for e in ("vector", "gpsimd", "scalar", "tensor", "sync"):
            self.pending_pre[e] = sig

    def op(self, eng, fn, waits=(), incs=(), setup=False, dma=False):
        waits = [w for w in waits if w is not None]
        if not setup and eng in self.pending_pre:
            waits.append(self.pending_pre.pop(eng))
        sig = None
        chain_wait = None
        if eng in CHAINED and not dma:
            assert not incs, "chained ops cannot carry explicit incs"
            if self.k[eng] > 0:
                chain_wait = (self.self_sems[eng], self.k[eng])
            self.k[eng] += 1
            sig = (self.self_sems[eng], self.k[eng])
            incs = [(self.self_sems[eng], 1)]
        else:
            assert len(incs) <= 1
        self.ops.append((eng, fn, tuple(waits), tuple(incs), setup, chain_wait))
        return sig

    def emit(self, setup_sem=None, setup_val=0):
        nc = self.nc
        with nc.Block() as block:
            for name, meth in (("sync", block.sync), ("gpsimd", block.gpsimd),
                               ("scalar", block.scalar), ("vector", block.vector),
                               ("tensor", block.tensor)):
                def body(eng, name=name):
                    first = True
                    for e, fn, waits, incs, setup, chain_wait in self.ops:
                        if e != name:
                            continue
                        if first and not setup and setup_sem is not None:
                            eng.wait_ge(setup_sem, setup_val)
                        if not setup:
                            first = False
                        if chain_wait is not None:
                            eng.wait_ge(*chain_wait)
                        for (h, v) in waits:
                            if v > 0:
                                eng.wait_ge(h, v)
                        ins = fn(eng)
                        for (h, kk) in incs:
                            ins = ins.then_inc(h, kk)
                meth(body)
        nc.all_engine_barrier()
        nc.clear_and_free_semaphores(self.all_sems)
        nc.all_engine_barrier()


class Consts:

    def __init__(self, nc):
        import ml_dtypes
        self._ident_dram = nc.inline_tensor(
            np.eye(128, dtype=np.float32).astype(ml_dtypes.bfloat16),
            name="ident_bf").ap()
        self.nc = nc

    def materialize(self, p, st, extra_dram=()):
        nc = self.nc
        sbt = lambda name, shp, dt: st.enter_context(nc.sbuf_tensor(name, shp, dt))
        self.ident_bf = sbt("c_ident", [128, 128], BF16)[:]
        self.ident_f32 = sbt("c_identf", [128, 128], F32)[:]
        self.ones_col = sbt("c_ones_col", [128, 1], F32)[:]
        self.ones_row = sbt("c_ones_row", [1, 128], F32)[:]
        self.s_const = p.sem("s_const")
        s_id = p.sem("s_id")
        n = 0
        p.op("sync", lambda e: e.dma_start(out=self.ident_bf, in_=self._ident_dram),
             incs=[(s_id, 16)], setup=True)
        p.op("scalar", lambda e: e.copy(self.ident_f32, self.ident_bf),
             incs=[(self.s_const, 17)], setup=True, dma=True,
             waits=[(s_id, 16)])
        n += 17
        p.op("gpsimd", lambda e: e.memset(self.ones_col, 1.0), setup=True, dma=True,
             incs=[(self.s_const, 1)])
        n += 1
        p.op("gpsimd", lambda e: e.memset(self.ones_row, 1.0), setup=True, dma=True,
             incs=[(self.s_const, 1)])
        n += 1
        self.extra = {}
        for (name, dram_ap, shp, dt) in extra_dram:
            t = sbt("c_" + name, shp, dt)
            self.extra[name] = t
            p.op("sync", lambda e, t=t, dram_ap=dram_ap:
                 e.dma_start(out=t[:], in_=dram_ap),
                 incs=[(self.s_const, 16)], setup=True)
            n += 16
        self.ready_val = n


class Ctx:

    def __init__(self, nc, p, c):
        self.nc, self.p, self.c = nc, p, c
        self.sems = {}
        self.cnt = {}
        self.mmc = {"n": 0}
        self.tqc = {"n": 0}
        self.ldc = {"n": 0}
        self.nstp = [0, 0]
        self.nph = 0

    def S(self, name):
        if name not in self.sems:
            self.sems[name] = self.p.sem(name)
        return self.sems[name]

    def bump(self, name, v=16):
        self.cnt[name] = self.cnt.get(name, 0) + v
        return self.cnt[name]

    def val(self, name):
        return self.cnt.get(name, 0)

    def join(self):
        p = self.p
        w = [(self.S("s_st0"), 16 * self.nstp[0]),
             (self.S("s_st1"), 16 * self.nstp[1]),
             (self.S("s_mm"), self.mmc["n"]),
             (self.S("s_tq"), self.tqc["n"])]
        for e in CHAINED:
            w.append((p.self_sems[e], p.k[e]))
        self.nph += 1
        p.op("sync", lambda e: e.nop(), waits=w, incs=[(self.S("s_ph"), 1)])
        sig = (self.S("s_ph"), self.nph)
        p.set_phase_pre(sig)
        return sig


def emit_ln_fm(p, c, xg, xsq, xln, stats_ps, mb_ps, rb_ps, rowbuf, scr,
               s_mm, mmc, ncols, nch=KD, eps=EPS, pre_waits=(), xln_stride=None,
               xln_off=0):
    W = ncols
    mrow = rowbuf[0:1, 0:W]
    vrow = rowbuf[0:1, W:2 * W]
    rrow = rowbuf[0:1, 2 * W:3 * W]
    sq_sig = p.op("scalar", lambda e: e.activation(xsq[:, 0:nch * W],
                                                   xg[:, 0:nch * W], AF.Square),
                  waits=pre_waits)
    for cc in range(nch):
        p.op("tensor",
             lambda e, cc=cc: e.matmul(stats_ps[0:1, 0:W], c.ones_col,
                                       xg[:, cc * W:(cc + 1) * W],
                                       start=(cc == 0), stop=(cc == nch - 1)),
             waits=(pre_waits if cc == 0 else ()))
    for cc in range(nch):
        p.op("tensor",
             lambda e, cc=cc: e.matmul(stats_ps[32:33, 0:W], c.ones_col,
                                       xsq[:, cc * W:(cc + 1) * W],
                                       start=(cc == 0), stop=(cc == nch - 1)),
             waits=([sq_sig] if cc == 0 else ()),
             incs=([(s_mm, 1)] if cc == nch - 1 else ()))
    mmc["n"] += 1
    stat_w = [(s_mm, mmc["n"])]
    p.op("vector", lambda e: e.tensor_scalar(mrow, stats_ps[0:1, 0:W],
                                             1.0 / (nch * 128.0), None, ALU.mult),
         waits=stat_w)
    p.op("vector", lambda e: e.tensor_mul(vrow, mrow, mrow))
    p.op("vector", lambda e: e.scalar_tensor_tensor(vrow, stats_ps[32:33, 0:W],
                                                    1.0 / (nch * 128.0), vrow,
                                                    ALU.mult, ALU.subtract))
    veps = p.op("vector", lambda e: e.tensor_scalar(vrow, vrow, eps, None, ALU.add))
    p.op("scalar", lambda e: e.activation(rrow, vrow, AF.Ln), waits=[veps])
    rsig = p.op("scalar", lambda e: e.activation(rrow, rrow, AF.Exp, scale=-0.5))
    p.op("tensor", lambda e: e.matmul(mb_ps[:, 0:W], c.ones_row, mrow,
                                      start=True, stop=True),
         waits=[rsig])
    p.op("tensor", lambda e: e.matmul(rb_ps[:, 0:W], c.ones_row, rrow,
                                      start=True, stop=True),
         incs=[(s_mm, 1)])
    mmc["n"] += 1
    bc_w = [(s_mm, mmc["n"])]
    xst = W if xln_stride is None else xln_stride
    last = None
    for cc in range(nch):
        p.op("vector", lambda e, cc=cc: e.tensor_sub(
                scr[:, 0:W], xg[:, cc * W:(cc + 1) * W], mb_ps[:, 0:W]),
             waits=(bc_w if cc == 0 else ()))
        last = p.op("vector", lambda e, cc=cc: e.tensor_mul(
                xln[:, cc * xst + xln_off:cc * xst + xln_off + W], scr[:, 0:W],
                rb_ps[:, 0:W]))
    return last


def emit_wprep_transposed(p, c, specs, stag, sgn, pq, ld_slab, ldc, s_tq, tqc,
                          first_evac_waits=()):
    red_sig = None
    sgn_sig = None
    evac_sig = None
    alpha_sigs = []
    first = True
    for (whbm, nslab, scols, nq4, fdim, wT, alp, use_sign) in specs:
        nqq = nq4 // 4
        for m in range(nslab):
            dw = [red_sig, sgn_sig]
            if first:
                dw += list(first_evac_waits)
            p.op("sync", lambda e, whbm=whbm, m=m, scols=scols:
                 e.dma_start(out=stag[:, 0:scols],
                             in_=whbm[m * 128:(m + 1) * 128, :]),
                 waits=dw, incs=[(ld_slab, 16)])
            ldc["n"] += 16
            ldw = [(ld_slab, ldc["n"])]
            if alp is not None:
                red_sig = p.op("vector", lambda e, m=m, scols=scols, alp=alp:
                               e.tensor_reduce(alp[:, m:m + 1], stag[:, 0:scols],
                                               AX.X, ALU.add,
                                               apply_absolute_value=True),
                               waits=ldw)
                alpha_sigs.append(red_sig)
            tq_w = [(s_tq, tqc["n"])]
            if use_sign:
                sgn_sig = p.op("scalar", lambda e, scols=scols:
                               e.sign(sgn[:, 0:scols], stag[:, 0:scols]),
                               waits=ldw + tq_w)
            else:
                sgn_sig = p.op("scalar", lambda e, scols=scols:
                               e.copy(sgn[:, 0:scols], stag[:, 0:scols]),
                               waits=ldw + tq_w)
            for q in range(nqq):
                for j in range(4):
                    kc = q * 4 + j
                    wts = []
                    if j == 0:
                        wts = [sgn_sig, evac_sig]
                        if first:
                            wts += list(first_evac_waits)
                    p.op("tensor", lambda e, j=j, kc=kc:
                         e.transpose(pq[:, j * 128:(j + 1) * 128],
                                     sgn[:, kc * 128:(kc + 1) * 128], c.ident_bf),
                         waits=wts,
                         incs=([(s_tq, 1)] if j == 3 else []))
                tqc["n"] += 1
                first = False
                evac_sig = p.op("vector", lambda e, q=q, m=m, wT=wT, fdim=fdim:
                                e.tensor_copy(
                                    wT.rearrange("p (kc f) -> p kc f", f=fdim)
                                      [:, q * 4:q * 4 + 4, m * 128:(m + 1) * 128],
                                    pq[:, 0:512].rearrange("p (a f) -> p a f", a=4)),
                                waits=[(s_tq, tqc["n"])])
    return evac_sig, alpha_sigs


EXTL = 2304


def emit_l0attn(X, st, x, wq, wk, wv, wo, kinit, vinit, maskc, y):
    nc, p, c = X.nc, X.p, X.c
    PW = 384
    NPG = 6
    ld_slab = X.S("ld_slab")
    ld_m = X.S("ld_m")
    ld_msw = X.S("ld_msw")
    ldx = X.S("ldx")
    ldr = X.S("ldr")
    s_tq = X.S("s_tq")
    s_mm = X.S("s_mm")
    s_st = [X.S("s_st0"), X.S("s_st1")]
    nstp = X.nstp
    mmc, tqc, ldc = X.mmc, X.tqc, X.ldc
    mfl = c.extra["mflag"]

    sb = lambda name, shp, dt: st.enter_context(nc.sbuf_tensor(name, shp, dt))
    ps = lambda name, shp, dt: st.enter_context(nc.psum_tensor(name, shp, dt))
    xln_ext = sb("a_xln_ext", [128, KD * EXTL], BF16)
    wqT = sb("a_wqT", [128, KD * D], BF16)
    wkT = sb("a_wkT", [128, KD * D], BF16)
    wvT = sb("a_wvT", [128, KD * D], BF16)
    stag = sb("a_stag", [128, 1024], F32)
    sgn = sb("a_sgn", [128, 1024], BF16)
    kin_s = sb("a_kin_s", [128, 8 * 63], BF16)
    vin_s = sb("a_vin_s", [128, 8 * 63], BF16)
    mk = sb("a_mk", [128, 256], F32)
    scr = sb("a_scr", [128, 768], F32)
    rowbuf = sb("a_rowbuf", [1, 3 * PW], F32)
    q_g = sb("a_q_g", [128, 2 * EXTL], BF16)
    k_g = sb("a_k_g", [128, 2 * EXTL], BF16)
    v_g = sb("a_v_g", [128, 2 * EXTL], BF16)
    v_tm = sb("a_v_tm", [128, 17 * 256], BF16)
    scf = sb("a_scf", [128, 512], F32)
    attn_n = sb("a_attn_n", [128, 512], BF16)
    attnT = sb("a_attnT", [128, 512], BF16)
    nmx = sb("a_nmx", [128, 4], F32)
    smr = sb("a_smr", [128, 8], F32)
    yt = sb("a_yt", [128, 2 * 512], F32)
    xr = sb("a_xr", [128, 2 * 512], F32)
    ao_fm = sb("a_ao_fm", [128, KD * TOK], BF16)
    stats_ps = ps("a_stats", [128, 512], F32)
    mb_ps = ps("a_mb_ps", [128, 512], F32)
    rb_ps = ps("a_rb_ps", [128, 512], F32)
    mm = [ps("a_mmA", [128, 512], F32), ps("a_mmB", [128, 512], F32)]
    sc_ps = ps("a_scA", [128, 512], F32)
    pq = ps("a_pqA", [128, 512], BF16)
    pq2 = [pq, ps("a_pqB", [128, 512], BF16)]

    LDMSW = X.val("ld_msw")
    LDM = X.val("ld_m")
    for (eng, sem, nm, dst, src) in (("gpsimd", ld_msw, "ld_msw", kin_s, kinit),
                                     ("gpsimd", ld_msw, "ld_msw", vin_s, vinit),
                                     ("sync", ld_m, "ld_m", mk, maskc)):
        p.op(eng, lambda e, dst=dst, src=src: e.dma_start(out=dst[:], in_=src),
             incs=[(sem, 16)], dma=True)
        X.bump(nm)
    LDMSW += 32
    LDM += 16

    nldx = X.val("ldx")
    ln_sig = None
    with nc.sbuf_tensor("a_xg0", [128, KD * PW], F32, side="right") as xg, \
         nc.sbuf_tensor("a_xsq0", [128, KD * PW], F32, side="right") as xsq:
        for g in range(NPG):
            t0 = g * PW
            p.op("sync", lambda e, t0=t0:
                 e.dma_start(out=xg.rearrange("p (c t) -> p c t", c=KD),
                             in_=x[:, t0:t0 + PW]
                                 .rearrange("(c p) t -> c p t", p=128)
                                 .rearrange("c p t -> p c t")),
                 waits=[ln_sig], incs=[(ldx, 16)])
            nldx = X.bump("ldx")
            ln_sig = emit_ln_fm(p, c, xg, xsq, xln_ext, stats_ps, mb_ps, rb_ps,
                                rowbuf, scr, s_mm, mmc, PW,
                                pre_waits=[(ldx, nldx)],
                                xln_stride=EXTL, xln_off=t0)

    wp_sig, _ = emit_wprep_transposed(
        p, c, [(wq, 8, D, 8, D, wqT, None, False),
               (wk, 8, D, 8, D, wkT, None, False),
               (wv, 8, D, 8, D, wvT, None, False)],
        stag, sgn, pq, ld_slab, ldc, s_tq, tqc)

    SCALE = float(1.0 / np.sqrt(64.0))
    ev_sigs = {}
    aolast = None
    pmm = 0
    for hg in range(4):
        mc0 = 2 * hg
        proj_last = None
        for (wTb, dest) in ((wqT, q_g), (wkT, k_g), (wvT, v_g)):
            for mcl in range(2):
                mc = mc0 + mcl
                for g in range(NPG):
                    par = pmm % 2
                    for kc in range(KD):
                        wts = []
                        if kc == 0:
                            wts = [wp_sig, ln_sig, ev_sigs.get(par)]
                        p.op("tensor", lambda e, par=par, mc=mc, g=g, kc=kc,
                             wTb=wTb:
                             e.matmul(mm[par][:, 0:PW],
                                      wTb.rearrange("p (kc f) -> p kc f", f=D)
                                         [:, kc, mc * 128:(mc + 1) * 128],
                                      xln_ext[:, kc * EXTL + g * PW:
                                              kc * EXTL + (g + 1) * PW],
                                      start=(kc == 0), stop=(kc == KD - 1)),
                             waits=wts,
                             incs=([(s_mm, 1)] if kc == KD - 1 else []))
                    mmc["n"] += 1
                    pmm += 1
                    ev_sigs[par] = p.op(
                        "scalar", lambda e, par=par, mcl=mcl, g=g, dest=dest:
                        e.activation(dest[:, mcl * EXTL + g * PW:
                                          mcl * EXTL + (g + 1) * PW],
                                     mm[par][:, 0:PW], AF.Copy),
                        waits=[(s_mm, mmc["n"])])
                    proj_last = ev_sigs[par]
        bs = None
        for (dst, ini) in ((k_g, kin_s), (v_g, vin_s)):
            for mcl in range(2):
                mc = mc0 + mcl
                p.op("vector", lambda e, ini=ini, mc=mc: e.tensor_scalar(
                        scr[:, 0:63], ini[:, mc * 63:(mc + 1) * 63],
                        mfl[:, 0:1], None, ALU.mult),
                     waits=([proj_last, (ld_msw, LDMSW), (ld_m, LDM)]
                            if bs is None else []))
                bs = p.op("vector", lambda e, dst=dst, mcl=mcl:
                          e.scalar_tensor_tensor(
                              dst[:, mcl * EXTL:mcl * EXTL + 63],
                              dst[:, mcl * EXTL:mcl * EXTL + 63],
                              mfl[:, 1:2], scr[:, 0:63], ALU.mult, ALU.add))
        blend_sig = bs

        vq_sig = None
        for t in range(17):
            for mcl in range(2):
                wts = []
                if mcl == 0:
                    wts = [blend_sig, vq_sig]
                p.op("tensor", lambda e, t=t, mcl=mcl:
                     e.transpose(pq[:, mcl * 128:(mcl + 1) * 128],
                                 v_g[:, mcl * EXTL + 128 * t:
                                      mcl * EXTL + 128 * (t + 1)],
                                 c.ident_bf),
                     waits=wts,
                     incs=([(s_tq, 1)] if mcl == 1 else []))
            tqc["n"] += 1
            vq_sig = p.op("vector", lambda e, t=t:
                          e.tensor_copy(v_tm[:, 256 * t:256 * (t + 1)],
                                        pq[:, 0:256]),
                          waits=[(s_tq, tqc["n"])])

        sc_free = {0: None, 1: None}
        at_free = {0: vq_sig, 1: vq_sig}
        o_free = {0: [None, None], 1: [None, None]}
        sc_mm = {}

        def emit_score(ht):
            odd = ht % 2
            pp2 = ht % 2
            P = 256 * pp2
            t = ht // 2
            kwin = 128 * t + 64 * odd
            q0 = 63 + 128 * t + 64 * odd
            for hl in range(4):
                par = hl % 2
                qq = hl // 2
                mcl = hl // 2
                wts = []
                if hl == 0:
                    wts = [sc_free[pp2], blend_sig]
                p.op("tensor", lambda e, par=par, qq=qq, mcl=mcl,
                     kwin=kwin, q0=q0, P=P:
                     e.matmul(sc_ps[64 * par:64 * par + 64,
                                    P + qq * 128:P + (qq + 1) * 128],
                              q_g[64 * par:64 * par + 64,
                                  mcl * EXTL + q0:mcl * EXTL + q0 + 64],
                              k_g[64 * par:64 * par + 64,
                                  mcl * EXTL + kwin:mcl * EXTL + kwin + 128],
                              start=True, stop=True,
                              tile_position=(64 * par, 64 * par)),
                     waits=wts,
                     incs=([(s_mm, 1)] if hl == 3 else []))
            mmc["n"] += 1
            sc_mm[ht] = mmc["n"]

        pet_tq = {}

        def emit_soft(ht):
            odd = ht % 2
            pp2 = ht % 2
            P = 256 * pp2
            t = ht // 2
            sc_free[pp2] = p.op("vector", lambda e, P=P:
                 e.scalar_tensor_tensor(
                    scf[:, P:P + 256], sc_ps[:, P:P + 256], SCALE, mk[:],
                    ALU.mult, ALU.add),
                 waits=[(s_mm, sc_mm[ht])])
            nm_sig = p.op("vector", lambda e, P=P, pp2=pp2: e.tensor_reduce(
                    nmx[:, 2 * pp2:2 * pp2 + 2],
                    scf[:, P:P + 256].rearrange("p (a t) -> p a t", a=2),
                    AX.X, ALU.max, negate=True))
            ex_sig = None
            for qq in range(2):
                ex_sig = p.op("scalar", lambda e, qq=qq, P=P, pp2=pp2:
                     e.activation(
                        scf[:, P + qq * 128:P + (qq + 1) * 128],
                        scf[:, P + qq * 128:P + (qq + 1) * 128],
                        AF.Exp, bias=nmx[:, 2 * pp2 + qq:2 * pp2 + qq + 1]),
                     waits=([nm_sig] if qq == 0 else ()))
            p.op("vector", lambda e, P=P, pp2=pp2: e.tensor_reduce(
                    smr[:, 4 * pp2:4 * pp2 + 2],
                    scf[:, P:P + 256].rearrange("p (a t) -> p a t", a=2),
                    AX.X, ALU.add),
                 waits=[ex_sig])
            p.op("vector", lambda e, pp2=pp2: e.reciprocal(
                    smr[:, 4 * pp2 + 2:4 * pp2 + 4],
                    smr[:, 4 * pp2:4 * pp2 + 2]))
            nops = []
            for qq in range(2):
                if odd == 0:
                    nops.append((qq * 128, qq * 128, 128, qq))
                else:
                    nops.append((qq * 128, qq * 128 + 64, 64, qq))
                    nops.append((qq * 128 + 64, qq * 128, 64, qq))
            an_sig = None
            first_an = [(s_tq, pet_tq.get(ht - 2, 0))]
            for (so, do, wd, qq) in nops:
                an_sig = p.op("vector",
                     lambda e, so=so, do=do, wd=wd, qq=qq, P=P, pp2=pp2:
                     e.tensor_scalar(attn_n[:, P + do:P + do + wd],
                                     scf[:, P + so:P + so + wd],
                                     smr[:, 4 * pp2 + 2 + qq:
                                         4 * pp2 + 3 + qq], None, ALU.mult),
                     waits=first_an)
                first_an = []
            pqb = pq2[pp2]
            for qq in range(2):
                p.op("tensor", lambda e, qq=qq, P=P, pqb=pqb: e.transpose(
                        pqb[:, qq * 128:(qq + 1) * 128],
                        attn_n[:, P + qq * 128:P + (qq + 1) * 128],
                        c.ident_bf),
                     waits=([an_sig, at_free[pp2]] if qq == 0 else ()),
                     incs=([(s_tq, 1)] if qq == 1 else []))
            tqc["n"] += 1
            pet_tq[ht] = tqc["n"]

        def emit_out(ht):
            nonlocal aolast
            odd = ht % 2
            pp2 = ht % 2
            P = 256 * pp2
            AVP = 128 * pp2
            PS = 256 + 256 * pp2
            t = ht // 2
            ra0 = 128 * t + 64 * odd
            pqb = pq2[pp2]
            at_free[pp2] = p.op("vector", lambda e, P=P, ht=ht, pqb=pqb:
                 e.tensor_copy(
                    attnT[:, P:P + 256], pqb[:, 0:256]),
                 waits=[(s_tq, pet_tq[ht])])
            for hl in range(4):
                par = hl % 2
                qq = hl // 2
                wts = []
                if hl == 0:
                    wts = [at_free[pp2], o_free[pp2][0], o_free[pp2][1]]
                if odd == 0:
                    p.op("tensor", lambda e, hl=hl, par=par, qq=qq, t=t,
                         P=P, AVP=AVP:
                         e.matmul(mm[0][64 * par:64 * par + 64,
                                        AVP + qq * 64:AVP + (qq + 1) * 64],
                                  v_tm[:, 256 * t + 64 * hl:
                                       256 * t + 64 * (hl + 1)],
                                  attnT[:, P + qq * 128 + 64 * par:
                                        P + qq * 128 + 64 * (par + 1)],
                                  start=True, stop=True),
                         waits=wts,
                         incs=([(s_mm, 1)] if hl == 3 else []))
                else:
                    p.op("tensor", lambda e, hl=hl, par=par, qq=qq, t=t,
                         P=P, AVP=AVP:
                         e.matmul(mm[0][64 * par:64 * par + 64,
                                        AVP + qq * 64:AVP + (qq + 1) * 64],
                                  v_tm[64:128, 256 * t + 64 * hl:
                                       256 * t + 64 * (hl + 1)],
                                  attnT[64:128, P + qq * 128 + 64 * par:
                                        P + qq * 128 + 64 * (par + 1)],
                                  start=True, stop=True),
                         waits=wts)
                    p.op("tensor", lambda e, hl=hl, par=par, qq=qq, t=t,
                         P=P, AVP=AVP:
                         e.matmul(mm[1][64 * par:64 * par + 64,
                                        AVP + qq * 64:AVP + (qq + 1) * 64],
                                  v_tm[0:64, 256 * (t + 1) + 64 * hl:
                                       256 * (t + 1) + 64 * (hl + 1)],
                                  attnT[0:64, P + qq * 128 + 64 * par:
                                        P + qq * 128 + 64 * (par + 1)],
                                  start=True, stop=True),
                         incs=([(s_mm, 1)] if hl == 3 else []))
            mmc["n"] += 1
            if odd == 1:
                cpsig = p.op("vector", lambda e, PS=PS, AVP=AVP: e.tensor_copy(
                        scr[:, PS:PS + 256], mm[0][:, AVP:AVP + 256]),
                     waits=[(s_mm, mmc["n"])])
            for par in range(2):
                if odd == 0:
                    o_free[pp2][par] = p.op("vector",
                         lambda e, par=par, ra0=ra0, mc0=mc0, AVP=AVP:
                         e.tensor_copy(
                             ao_fm.rearrange("p (cc t) -> p cc t", cc=KD)
                                  [64 * par:64 * par + 64, mc0:mc0 + 2,
                                   ra0:ra0 + 64],
                             mm[0][64 * par:64 * par + 64, AVP:AVP + 128]
                             .rearrange("p (a t) -> p a t", a=2)),
                         waits=[(s_mm, mmc["n"])])
                else:
                    o_free[pp2][par] = p.op("vector",
                         lambda e, par=par, ra0=ra0, mc0=mc0, PS=PS, AVP=AVP:
                         e.tensor_add(
                             ao_fm.rearrange("p (cc t) -> p cc t", cc=KD)
                                  [64 * par:64 * par + 64, mc0:mc0 + 2,
                                   ra0:ra0 + 64],
                             scr[64 * par:64 * par + 64, PS:PS + 128]
                             .rearrange("p (a t) -> p a t", a=2),
                             mm[1][64 * par:64 * par + 64, AVP:AVP + 128]
                             .rearrange("p (a t) -> p a t", a=2)),
                         waits=[(s_mm, mmc["n"])])
                aolast = o_free[pp2][par]

        for k in range(34):
            if k < 32:
                emit_score(k)
            if 1 <= k <= 32:
                emit_soft(k - 1)
            if k >= 2:
                emit_out(k - 2)
        ev_sigs = {0: aolast, 1: aolast}

    wo_sig, _ = emit_wprep_transposed(
        p, c, [(wo, 8, D, 8, D, wqT, None, False)], stag, sgn, pq,
        ld_slab, ldc, s_tq, tqc,
        first_evac_waits=[(s_mm, mmc["n"]), at_free[0], at_free[1]])
    EW = 512
    nst = 0
    est = {}
    for g in range(4):
        for mc in range(KD):
            par = pmm % 2
            p.op("sync", lambda e, mc=mc, g=g, par=par:
                 e.dma_start(out=xr[:, par * EW:(par + 1) * EW],
                             in_=x[mc * 128:(mc + 1) * 128,
                                   63 + g * EW:63 + (g + 1) * EW]),
                 waits=[est.get(par)], incs=[(ldr, 16)])
            nldr = X.bump("ldr")
            for kc in range(KD):
                wts = []
                if kc == 0:
                    wts = [wo_sig, aolast, est.get(par)]
                p.op("tensor", lambda e, par=par, mc=mc, g=g, kc=kc:
                     e.matmul(mm[par][:, 0:EW],
                              wqT.rearrange("p (kc f) -> p kc f", f=D)
                                 [:, kc, mc * 128:(mc + 1) * 128],
                              ao_fm[:, kc * TOK + g * EW:
                                    kc * TOK + (g + 1) * EW],
                              start=(kc == 0), stop=(kc == KD - 1)),
                     waits=wts,
                     incs=([(s_mm, 1)] if kc == KD - 1 else []))
            mmc["n"] += 1
            epar = nst % 2
            esig = p.op("vector", lambda e, par=par, epar=epar: e.tensor_add(
                    yt[:, epar * EW:(epar + 1) * EW], mm[par][:, 0:EW],
                    xr[:, par * EW:(par + 1) * EW]),
                 waits=[(s_mm, mmc["n"]), (ldr, nldr),
                        (s_st[epar], 16 * nstp[epar])])
            est[par] = esig
            pmm += 1
            p.op("sync", lambda e, epar=epar, mc=mc, g=g: e.dma_start(
                    out=y[mc * 128:(mc + 1) * 128, g * EW:(g + 1) * EW],
                    in_=yt[:, epar * EW:(epar + 1) * EW]),
                 waits=[esig], incs=[(s_st[epar], 16)])
            nstp[epar] += 1
            nst += 1


def emit_l0ffn(X, st, x, w1, w2, b1t, b2t, a1h, b1c, r1h, a2h, b2c, r2h,
               y, y_off):
    nc, p, c = X.nc, X.p, X.c
    H = 4096
    MH = 32
    NG2 = 8
    W = 256
    ld_w = X.S("ld_w")
    ldx = X.S("ldx")
    s_mm = X.S("s_mm")
    s_st = [X.S("s_st0"), X.S("s_st1")]
    nstp = X.nstp
    mmc = X.mmc
    e24sb = c.extra["e24"]

    sb = lambda name, shp, dt: st.enter_context(nc.sbuf_tensor(name, shp, dt))
    ps = lambda name, shp, dt: st.enter_context(nc.psum_tensor(name, shp, dt))
    w1s = sb("b_w1s", [128, KD * H], BF16)
    w2s = sb("b_w2s", [128, MH * D], BF16)
    a1s = sb("b_a1s", [128, 8 * 24], BF16)
    b1cs = sb("b_b1cs", [24, H], BF16)
    r1s = sb("b_r1s", [128, 8 * 3], BF16)
    a2s = sb("b_a2s", [128, 32 * 24], BF16)
    b2cs = sb("b_b2cs", [24, D], BF16)
    r2s = sb("b_r2s", [128, 32 * 3], BF16)
    b1s = sb("b_b1s", [128, 32], F32)
    b2s = sb("b_b2s", [128, 8], F32)
    xg = sb("b_xg", [128, KD * W], F32)
    xsq = sb("b_xsq", [128, KD * W], F32)
    xln = sb("b_xln", [128, KD * W], BF16)
    hg = sb("b_hg", [128, MH * W], BF16)
    z_sb = sb("b_z_sb", [24, W], BF16)
    zg_sb = sb("b_zg_sb", [24, W], BF16)
    gex = sb("b_gex", [128, 8], F32)
    gtm = sb("b_gtm", [128, 16], BF16)
    gfm = sb("b_gfm", [3, W], BF16)
    negm = sb("b_negm", [128, 2], F32)
    srec = sb("b_srec", [128, 4], F32)
    scr = sb("b_scr", [128, W], F32)
    tep = sb("b_tep", [128, 2 * W], F32)
    yt = sb("b_yt", [128, 2 * W], F32)
    rowbuf = sb("b_rowbuf", [1, 3 * W], F32)
    stats_ps = ps("b_stats", [128, 512], F32)
    mbrb = ps("b_mbrb", [128, 512], F32)
    mm = [ps("b_mmA", [128, 512], F32), ps("b_mmB", [128, 512], F32)]
    z_ps = ps("b_z_ps", [128, 512], F32)
    g_ps = ps("b_g_ps", [128, 512], F32)
    gt_ps = ps("b_gt_ps", [128, 1024], BF16)
    ge_ps = ps("b_ge_ps", [128, 512], F32)

    gz_sig = p.op("vector", lambda e: e.memset(gtm[:], 0.0))

    for (dst, src, nkc, fdim) in ((w1s, w1, KD, H), (w2s, w2, MH, D)):
        for kc in range(nkc):
            p.op("gpsimd", lambda e, dst=dst, src=src, kc=kc, fdim=fdim:
                 e.dma_start(out=dst[:, kc * fdim:(kc + 1) * fdim],
                             in_=src[kc * 128:(kc + 1) * 128, :]),
                 incs=[(ld_w, 16)], dma=True)
            X.bump("ld_w")
    for (dst, src) in ((a1s, a1h), (b1cs, b1c), (r1s, r1h), (a2s, a2h),
                       (b2cs, b2c), (r2s, r2h), (b1s, b1t), (b2s, b2t)):
        p.op("gpsimd", lambda e, dst=dst, src=src:
             e.dma_start(out=dst[:], in_=src),
             incs=[(ld_w, 16)], dma=True)
        X.bump("ld_w")
    WRDY = X.val("ld_w")

    state = {"pmm": 0, "nst": 0}
    ev_sigs = {}
    add_sigs = []

    def mole_layer(tg, src_ready, src_sb, nkc, acat, rcat, bcat, wmain, fdim,
                   nm, bias_sb, is_last, hg_free):
        for kc in range(nkc):
            wts = []
            if kc == 0:
                wts = [src_ready]
                if tg == 0 and src_sb is xln:
                    wts.append((ld_w, WRDY))
            p.op("tensor", lambda e, kc=kc: e.matmul(
                    z_ps[0:24, 0:W], acat[:, kc * 24:(kc + 1) * 24],
                    src_sb[:, kc * W:(kc + 1) * W],
                    start=(kc == 0), stop=(kc == nkc - 1)),
                 waits=wts)
        for i in range(2):
            for kc in range(nkc):
                p.op("tensor", lambda e, i=i, kc=kc: e.matmul(
                        g_ps[:, i * 4:i * 4 + 3],
                        src_sb[:, kc * W + i * 128:kc * W + (i + 1) * 128],
                        rcat[:, kc * 3:(kc + 1) * 3],
                        start=(kc == 0), stop=(kc == nkc - 1)),
                     incs=([(s_mm, 1)] if (i == 1 and kc == nkc - 1) else []))
        mmc["n"] += 1
        zg_w = [(s_mm, mmc["n"])]
        zev = p.op("vector", lambda e: e.tensor_copy(z_sb[:], z_ps[0:24, 0:W]),
                   waits=zg_w)
        p.op("vector", lambda e: e.tensor_reduce(negm[:, 0:1], g_ps[:, 0:3],
                                                 AX.X, ALU.max, negate=True))
        nm_sig = p.op("vector", lambda e: e.tensor_reduce(
                negm[:, 1:2], g_ps[:, 4:7], AX.X, ALU.max, negate=True))
        p.op("scalar", lambda e: e.activation(gex[:, 0:3], g_ps[:, 0:3],
                                              AF.Exp, bias=negm[:, 0:1]),
             waits=[nm_sig])
        ex_sig = p.op("scalar", lambda e: e.activation(
                gex[:, 4:7], g_ps[:, 4:7], AF.Exp, bias=negm[:, 1:2]))
        p.op("vector", lambda e: e.tensor_reduce(
                srec[:, 0:1], gex[:, 0:3], AX.X, ALU.add), waits=[ex_sig])
        p.op("vector", lambda e: e.tensor_reduce(
                srec[:, 1:2], gex[:, 4:7], AX.X, ALU.add))
        p.op("vector", lambda e: e.reciprocal(srec[:, 2:4], srec[:, 0:2]))
        gt_sig = None
        for i in range(2):
            gt_sig = p.op("vector", lambda e, i=i: e.tensor_scalar(
                    gtm[:, i * 8:i * 8 + 3], gex[:, i * 4:i * 4 + 3],
                    srec[:, 2 + i:3 + i], None, ALU.mult),
                 waits=([gz_sig] if (tg == 0 and i == 0) else ()))
        for i in range(2):
            p.op("tensor", lambda e, i=i: e.transpose(
                    gt_ps[0:8, i * 128:(i + 1) * 128],
                    gtm[:, i * 8:(i + 1) * 8], c.ident_bf),
                 waits=([gt_sig] if i == 0 else ()),
                 incs=([(s_mm, 1)] if i == 1 else []))
        mmc["n"] += 1
        gf_sig = p.op("vector", lambda e: e.tensor_copy(
                gfm[:], gt_ps[0:3, 0:W]), waits=[(s_mm, mmc["n"])])
        p.op("tensor", lambda e: e.matmul(ge_ps[0:24, 0:W], e24sb[:], gfm[:],
                                          start=True, stop=True),
             waits=[gf_sig], incs=[(s_mm, 1)])
        mmc["n"] += 1
        zg_sig = p.op("vector", lambda e: e.scalar_tensor_tensor(
                zg_sb[:], ge_ps[0:24, 0:W], 0.125, z_sb[:],
                ALU.mult, ALU.mult),
             waits=[(s_mm, mmc["n"]), zev])
        last_ep = None
        for m in range(nm):
            par = state["pmm"] % 2
            for kc in range(nkc):
                wts = []
                if kc == 0:
                    wts = [zg_sig, ev_sigs.get(par)]
                    if m == 0:
                        wts.append(hg_free)
                p.op("tensor", lambda e, par=par, m=m, kc=kc: e.matmul(
                        mm[par][:, 0:W],
                        wmain.rearrange("p (kc f) -> p kc f", f=fdim)
                             [:, kc, m * 128:(m + 1) * 128],
                        src_sb[:, kc * W:(kc + 1) * W],
                        start=(kc == 0), stop=False),
                     waits=wts)
            p.op("tensor", lambda e, par=par, m=m: e.matmul(
                    mm[par][:, 0:W], bcat[:, m * 128:(m + 1) * 128], zg_sb[:],
                    start=False, stop=True),
                 incs=[(s_mm, 1)])
            mmc["n"] += 1
            state["pmm"] += 1
            if not is_last:
                ev_sigs[par] = p.op("scalar", lambda e, par=par, m=m:
                     e.activation(hg[:, m * W:(m + 1) * W], mm[par][:, 0:W],
                                  AF.Relu, bias=bias_sb[:, m:m + 1]),
                     waits=[(s_mm, mmc["n"])])
                last_ep = ev_sigs[par]
            else:
                nst = state["nst"]
                epar = nst % 2
                tep_guard = [add_sigs[nst - 2]] if nst >= 2 else []
                ev_sigs[par] = p.op("scalar", lambda e, par=par, epar=epar,
                                    m=m:
                     e.activation(tep[:, epar * W:(epar + 1) * W],
                                  mm[par][:, 0:W], AF.Identity,
                                  bias=bias_sb[:, m:m + 1]),
                     waits=[(s_mm, mmc["n"])] + tep_guard)
                asig = p.op("vector", lambda e, epar=epar, m=m: e.tensor_add(
                        yt[:, epar * W:(epar + 1) * W],
                        tep[:, epar * W:(epar + 1) * W],
                        xg[:, m * W:(m + 1) * W]),
                     waits=[ev_sigs[par], (s_st[epar], 16 * nstp[epar])])
                add_sigs.append(asig)
                p.op("sync", lambda e, epar=epar, m=m, tg=tg: e.dma_start(
                        out=y[m * 128:(m + 1) * 128,
                              y_off + tg * W:y_off + (tg + 1) * W],
                        in_=yt[:, epar * W:(epar + 1) * W]),
                     waits=[asig], incs=[(s_st[epar], 16)])
                nstp[epar] += 1
                state["nst"] += 1
                last_ep = asig
        return last_ep

    ln_sig = None
    hg_free = None
    for tg in range(NG2):
        t0 = tg * W
        xg_guard = add_sigs[-1] if add_sigs else None
        p.op("sync", lambda e, t0=t0:
             e.dma_start(out=xg.rearrange("p (c t) -> p c t", c=KD),
                         in_=x[:, t0:t0 + W]
                             .rearrange("(c p) t -> c p t", p=128)
                             .rearrange("c p t -> p c t")),
             waits=[xg_guard, ln_sig], incs=[(ldx, 16)])
        nldx = X.bump("ldx")
        ln_sig = emit_ln_fm(p, c, xg, xsq, xln, stats_ps, mbrb[:, 0:W],
                            mbrb[:, W:2 * W], rowbuf, scr, s_mm, mmc, W,
                            pre_waits=[(ldx, nldx)])
        evA = mole_layer(tg, ln_sig, xln, KD, a1s, r1s, b1cs, w1s, H, MH,
                         b1s, False, hg_free)
        hg_free = mole_layer(tg, evA, hg, MH, a2s, r2s, b2cs, w2s, D, KD,
                             b2s, True, None)


def emit_l1ffn(X, st, x, w1, b1t, w2, b2t, y):
    nc, p, c = X.nc, X.p, X.c
    H = 4096
    MH = 32
    NG2 = 8
    W = 256
    ld_slab = X.S("ld_slab")
    ld_b = X.S("ld_b")
    ldx = X.S("ldx")
    s_tq = X.S("s_tq")
    s_mm = X.S("s_mm")
    s_st = [X.S("s_st0"), X.S("s_st1")]
    nstp = X.nstp
    mmc, tqc, ldc = X.mmc, X.tqc, X.ldc

    sb = lambda name, shp, dt: st.enter_context(nc.sbuf_tensor(name, shp, dt))
    ps = lambda name, shp, dt: st.enter_context(nc.psum_tensor(name, shp, dt))
    w1T = sb("f_w1T", [128, KD * H], BF16)
    w2T = sb("f_w2T", [128, MH * D], BF16)
    stag = sb("f_stag", [128, 4096], F32)
    sgn = sb("f_sgn", [128, 4096], BF16)
    alpha1 = sb("f_alpha1", [128, 32], F32)
    alpha2 = sb("f_alpha2", [128, 8], F32)
    b1s = sb("f_b1s", [128, 32], F32)
    b2s = sb("f_b2s", [128, 8], F32)
    xg = sb("f_xg", [128, KD * W], F32)
    xsq = sb("f_xsq", [128, KD * W], F32)
    xln = sb("f_xln", [128, KD * W], BF16)
    hg = sb("f_hg", [128, MH * W], BF16)
    scr = sb("f_scr", [128, W], F32)
    tep = sb("f_tep", [128, 2 * W], F32)
    yt = sb("f_yt", [128, 2 * W], F32)
    rowbuf = sb("f_rowbuf", [1, 3 * W], F32)
    pq = ps("f_pq0", [128, 1024], BF16)
    stats_ps = ps("f_stats", [128, 512], F32)
    mbrb = ps("f_mbrb", [128, 512], F32)
    mm = [ps("f_mmA", [128, 512], F32), ps("f_mmB", [128, 512], F32)]

    LDB = X.val("ld_b")
    p.op("sync", lambda e: e.dma_start(out=b1s[:], in_=b1t), incs=[(ld_b, 16)])
    p.op("sync", lambda e: e.dma_start(out=b2s[:], in_=b2t), incs=[(ld_b, 16)])
    X.bump("ld_b", 32)
    LDB += 32

    specs = [(w1, 32, D, 8, H, w1T, alpha1, True),
             (w2, 8, 4096, 32, D, w2T, alpha2, True)]
    wp_sig, alpha_sigs = emit_wprep_transposed(
        p, c, specs, stag, sgn, pq, ld_slab, ldc, s_tq, tqc)
    p.op("vector", lambda e: e.tensor_scalar(alpha1[:], alpha1[:], 1.0 / D,
                                             None, ALU.mult),
         waits=[(ld_b, LDB), alpha_sigs[-1]])
    al_sig = p.op("vector", lambda e: e.tensor_scalar(
            alpha2[:], alpha2[:], 1.0 / 4096.0, None, ALU.mult))

    ev_sigs = {}
    add_sigs = []
    ln_sig = None
    hg_free = None
    pmm = 0
    nst = 0
    for tg in range(NG2):
        t0 = tg * W
        xg_guard = add_sigs[-1] if add_sigs else None
        p.op("sync", lambda e, t0=t0:
             e.dma_start(out=xg.rearrange("p (c t) -> p c t", c=KD),
                         in_=x[:, t0:t0 + W]
                             .rearrange("(c p) t -> c p t", p=128)
                             .rearrange("c p t -> p c t")),
             waits=[xg_guard, ln_sig], incs=[(ldx, 16)])
        nldx = X.bump("ldx")
        ln_sig = emit_ln_fm(p, c, xg, xsq, xln, stats_ps, mbrb[:, 0:W],
                            mbrb[:, W:2 * W], rowbuf, scr, s_mm, mmc, W,
                            pre_waits=[(ldx, nldx)])
        last_gelu = None
        for m in range(MH):
            par = pmm % 2
            for kc in range(KD):
                wts = []
                if kc == 0:
                    wts = [ln_sig, ev_sigs.get(par)]
                    if tg == 0 and m == 0:
                        wts.append(wp_sig)
                    if m == 0:
                        wts.append(hg_free)
                p.op("tensor", lambda e, par=par, m=m, kc=kc:
                     e.matmul(mm[par][:, 0:W],
                              w1T.rearrange("p (kc f) -> p kc f", f=H)
                                 [:, kc, m * 128:(m + 1) * 128],
                              xln[:, kc * W:(kc + 1) * W],
                              start=(kc == 0), stop=(kc == KD - 1)),
                     waits=wts, incs=([(s_mm, 1)] if kc == KD - 1 else []))
            mmc["n"] += 1
            pmm += 1
            ev_sigs[par] = p.op("scalar", lambda e, par=par, m=m:
                 e.activation(hg[:, m * W:(m + 1) * W], mm[par][:, 0:W],
                              AF.Gelu, bias=b1s[:, m:m + 1],
                              scale=alpha1[:, m:m + 1]),
                 waits=[(s_mm, mmc["n"])] +
                       ([al_sig] if (tg == 0 and m == 0) else []))
            last_gelu = ev_sigs[par]
        for md in range(KD):
            par = pmm % 2
            for kc in range(MH):
                wts = []
                if kc == 0:
                    wts = [ev_sigs.get(par)]
                    if md == 0:
                        wts.append(last_gelu)
                p.op("tensor", lambda e, par=par, md=md, kc=kc:
                     e.matmul(mm[par][:, 0:W],
                              w2T.rearrange("p (kc f) -> p kc f", f=D)
                                 [:, kc, md * 128:(md + 1) * 128],
                              hg[:, kc * W:(kc + 1) * W],
                              start=(kc == 0), stop=(kc == MH - 1)),
                     waits=wts, incs=([(s_mm, 1)] if kc == MH - 1 else []))
            mmc["n"] += 1
            pmm += 1
            epar = nst % 2
            tep_guard = [add_sigs[nst - 2]] if nst >= 2 else []
            ev_sigs[par] = p.op("scalar", lambda e, par=par, epar=epar, md=md:
                 e.activation(tep[:, epar * W:(epar + 1) * W],
                              mm[par][:, 0:W], AF.Identity,
                              bias=b2s[:, md:md + 1],
                              scale=alpha2[:, md:md + 1]),
                 waits=[(s_mm, mmc["n"])] + tep_guard)
            asig = p.op("vector", lambda e, epar=epar, md=md: e.tensor_add(
                    yt[:, epar * W:(epar + 1) * W],
                    tep[:, epar * W:(epar + 1) * W],
                    xg[:, md * W:(md + 1) * W]),
                 waits=[ev_sigs[par], (s_st[epar], 16 * nstp[epar])])
            add_sigs.append(asig)
            p.op("sync", lambda e, epar=epar, md=md, t0=t0: e.dma_start(
                    out=y[md * 128:(md + 1) * 128, t0:t0 + W],
                    in_=yt[:, epar * W:(epar + 1) * W]),
                 waits=[asig], incs=[(s_st[epar], 16)])
            nstp[epar] += 1
            nst += 1
            hg_free = asig


EXT3 = 2304


def emit_l1front(X, st, x, wqkv, convs, wsm, ccq_in, ccs2_in):
    nc, p, c = X.nc, X.p, X.c
    PW = 384
    NPG = 6
    ld_w = X.S("ld_w")
    ld_w2 = X.S("ld_w2")
    ldx = X.S("ldx")
    s_mm = X.S("s_mm")
    s_tq = X.S("s_tq")
    s_st = X.S("s_st0")
    nstp = X.nstp
    mmc, tqc = X.mmc, X.tqc
    cbase = nstp[0]

    sb = lambda name, shp, dt: st.enter_context(nc.sbuf_tensor(name, shp, dt))
    ps = lambda name, shp, dt: st.enter_context(nc.psum_tensor(name, shp, dt))
    xln_ext = sb("c_xln_ext", [128, KD * EXT3], BF16)
    w_sb = sb("c_w_sb", [128, KD * D], BF16)
    conv_s = sb("c_conv_s", [128, 8 * 12], F32)
    wsm_s = sb("c_wsm_s", [128, 8 * 96], BF16)
    praw = sb("c_praw", [128, 2 * EXT3], BF16)
    cacc_v = sb("c_cacc_v", [128, TOK], F32)
    pout1 = sb("c_pout1", [128, KD * TOK], BF16)
    smq_b = sb("c_smq_b", [8, 512], F32)
    smq_g = sb("c_smq_g", [4, 512], F32)
    smq_h = sb("c_smq_h", [4, 512], F32)
    smtm = sb("c_smtm", [128, 64], F32)
    xg = st.enter_context(nc.sbuf_tensor("c_xg", [128, KD * PW], F32,
                                         side="right"))
    xsq = st.enter_context(nc.sbuf_tensor("c_xsq", [128, KD * PW], F32,
                                          side="right"))
    scr = sb("c_scr", [128, 512], F32)
    rowbuf = sb("c_rowbuf", [1, 3 * PW], F32)
    stats_ps = ps("c_stats", [128, 512], F32)
    mb_ps = ps("c_mb_ps", [128, 512], F32)
    rb_ps = ps("c_rb_ps", [128, 512], F32)
    mm = [ps("c_mmA", [128, 512], F32), ps("c_mmB", [128, 512], F32)]
    sm_ps = ps("c_sm_ps", [128, 512], F32)
    sm_tp = ps("c_sm_tp", [128, 64], F32)

    p.op("gpsimd", lambda e: e.dma_start(out=conv_s[:], in_=convs),
         incs=[(ld_w, 16)], dma=True)
    X.bump("ld_w")
    for kcx in range(KD):
        p.op("gpsimd", lambda e, kcx=kcx: e.dma_start(
                out=wsm_s[:, kcx * 96:(kcx + 1) * 96],
                in_=wsm[kcx * 128:(kcx + 1) * 128, :]),
             incs=[(ld_w, 16)], dma=True)
        X.bump("ld_w")
    WRDY0 = X.val("ld_w")

    ln_sig = None
    for g in range(NPG):
        t0 = g * PW
        p.op("sync", lambda e, t0=t0:
             e.dma_start(out=xg.rearrange("p (c t) -> p c t", c=KD),
                         in_=x[:, t0:t0 + PW]
                             .rearrange("(c p) t -> c p t", p=128)
                             .rearrange("c p t -> p c t")),
             waits=[ln_sig], incs=[(ldx, 16)])
        nldx = X.bump("ldx")
        ln_sig = emit_ln_fm(p, c, xg, xsq, xln_ext, stats_ps, mb_ps, rb_ps,
                            rowbuf, scr, s_mm, mmc, PW,
                            pre_waits=[(ldx, nldx)],
                            xln_stride=EXT3, xln_off=t0)

    smfree = None
    smev = None
    for g4 in range(4):
        t0 = 3 + g4 * 512
        for kcx in range(KD):
            wts = []
            if kcx == 0:
                wts = [ln_sig, (ld_w, WRDY0), smfree]
            p.op("tensor", lambda e, kcx=kcx, t0=t0: e.matmul(
                    sm_ps[0:96, 0:512],
                    wsm_s[:, kcx * 96:(kcx + 1) * 96],
                    xln_ext[:, kcx * EXT3 + t0:kcx * EXT3 + t0 + 512],
                    start=(kcx == 0), stop=(kcx == KD - 1)),
                 waits=wts, incs=([(s_mm, 1)] if kcx == KD - 1 else []))
        mmc["n"] += 1
        p.op("scalar", lambda e: e.activation(
                smq_b[0:8, 0:512], sm_ps[0:8, 0:512], AF.Sigmoid),
             waits=[(s_mm, mmc["n"])])
        p.op("scalar", lambda e: e.copy(smq_g[0:4, 0:512],
                                        sm_ps[32:36, 0:512]))
        smfree = p.op("scalar", lambda e: e.copy(smq_h[0:4, 0:512],
                                                 sm_ps[64:68, 0:512]))
        nt = 0
        for tt in range(4):
            for (src, nq, qo) in ((smq_b, 8, 0), (smq_g, 4, 8), (smq_h, 4, 12)):
                nt += 1
                p.op("tensor", lambda e, tt=tt, src=src, nq=nq, qo=qo:
                     e.transpose(
                        sm_tp[:, tt * 16 + qo:tt * 16 + qo + nq],
                        src[0:nq, tt * 128:(tt + 1) * 128],
                        c.ident_f32[0:nq, 0:nq]),
                     waits=([smfree, smev] if nt == 1 else ()),
                     incs=([(s_tq, 1)] if nt == 12 else []))
        tqc["n"] += 1
        smev = p.op("vector", lambda e: e.tensor_copy(smtm[:, 0:64],
                                                      sm_tp[:, 0:64]),
                    waits=[(s_tq, tqc["n"]), (s_st, 16 * nstp[0])])
        p.op("sync", lambda e, g4=g4: e.dma_start(
                out=ccs2_in[g4 * 512:(g4 + 1) * 512, :]
                    .rearrange("(tt pp) q -> pp tt q", pp=128),
                in_=smtm[:, 0:64].rearrange("p (tt q) -> p tt q", q=16)),
             waits=[smev], incs=[(s_st, 16)])
        nstp[0] += 1

    pmm = 0
    ev_sigs = {}
    praw_free = {}
    sl_last = {}
    for ti, whbm in enumerate(wqkv):
        wguard = ev_sigs.get("wlast")
        for kcx in range(KD):
            p.op("gpsimd", lambda e, kcx=kcx, whbm=whbm: e.dma_start(
                    out=w_sb[:, kcx * D:(kcx + 1) * D],
                    in_=whbm[kcx * 128:(kcx + 1) * 128, :]),
                 waits=([wguard] if (kcx == 0 and wguard is not None) else []),
                 incs=[(ld_w2, 16)], dma=True)
            X.bump("ld_w2")
        wld = X.val("ld_w2")
        for mc in range(KD):
            par = pmm % 2
            for g in range(NPG):
                for kcx in range(KD):
                    wts = []
                    if kcx == 0:
                        wts = [(ld_w2, wld), ln_sig, ev_sigs.get(par)]
                    p.op("tensor", lambda e, par=par, mc=mc, g=g, kcx=kcx:
                         e.matmul(mm[par][:, 0:PW],
                                  w_sb.rearrange("p (kc f) -> p kc f", f=D)
                                      [:, kcx, mc * 128:(mc + 1) * 128],
                                  xln_ext[:, kcx * EXT3 + g * PW:
                                          kcx * EXT3 + (g + 1) * PW],
                                  start=(kcx == 0), stop=(kcx == KD - 1)),
                         waits=wts,
                         incs=([(s_mm, 1)] if kcx == KD - 1 else []))
                mmc["n"] += 1
                ppar = mc % 2
                ev_sigs[par] = p.op("scalar",
                     lambda e, par=par, ppar=ppar, g=g:
                     e.activation(praw[:, ppar * EXT3 + g * PW:
                                       ppar * EXT3 + (g + 1) * PW],
                                  mm[par][:, 0:PW], AF.Copy),
                     waits=[(s_mm, mmc["n"])] +
                           ([praw_free.get(mc % 2)] if g == 0 else []))
                pmm += 1
                par = pmm % 2
            ev_sigs["wlast"] = ev_sigs[(pmm - 1) % 2]
            cacc = cacc_v
            ppar = mc % 2
            pr_sig = ev_sigs[(pmm - 1) % 2]
            p.op("vector", lambda e, ppar=ppar, mc=mc, ti=ti, cacc=cacc:
                 e.tensor_scalar(
                    cacc[:], praw[:, ppar * EXT3 + 0:ppar * EXT3 + TOK],
                    conv_s[:, mc * 12 + ti * 4:mc * 12 + ti * 4 + 1],
                    None, ALU.mult),
                 waits=[pr_sig, sl_last.get("vector")])
            csig = None
            for j in (1, 2, 3):
                csig = p.op("vector", lambda e, ppar=ppar, mc=mc, ti=ti, j=j,
                            cacc=cacc:
                     e.scalar_tensor_tensor(
                         cacc[:], praw[:, ppar * EXT3 + j:ppar * EXT3 + j + TOK],
                         conv_s[:, mc * 12 + ti * 4 + j:
                                mc * 12 + ti * 4 + j + 1],
                         cacc[:], ALU.mult, ALU.add))
            stguard = (s_st, 16 * (cbase + 4 + ti * 8)) if ti > 0 else None
            sl_sig = p.op("scalar", lambda e, mc=mc, cacc=cacc:
                 e.activation(
                    pout1[:, mc * TOK:(mc + 1) * TOK], cacc[:], AF.Silu),
                 waits=[csig, stguard])
            sl_last["vector"] = sl_sig
            praw_free[mc % 2] = csig
            p.op("sync", lambda e, mc=mc, ti=ti: e.dma_start(
                    out=ccq_in[ti * 1024 + mc * 128:
                               ti * 1024 + (mc + 1) * 128, :],
                    in_=pout1[:, mc * TOK:(mc + 1) * TOK]),
                 waits=[sl_sig], incs=[(s_st, 16)])
            nstp[0] += 1


def emit_l1comb(X, st, cc2_out, ccq_in, ccs2_in, bgh, nw_in, wo, x, x_off, y):
    nc, p, c = X.nc, X.p, X.c
    ld_w = X.S("ld_w")
    ld_wh = X.S("ld_wh")
    ld_t = X.S("ld_t")
    ldr = X.S("ldr")
    s_tq = X.S("s_tq")
    s_mm = X.S("s_mm")
    s_cc = X.S("s_cc")
    s_st = [X.S("s_st0"), X.S("s_st1")]
    nstp = X.nstp
    mmc, tqc = X.mmc, X.tqc
    mfl = c.extra["mflag"]

    sb = lambda name, shp, dt: st.enter_context(nc.sbuf_tensor(name, shp, dt))
    ps = lambda name, shp, dt: st.enter_context(nc.psum_tensor(name, shp, dt))
    wo_s = sb("e_wo_s", [128, KD * D], BF16)
    nw_s = sb("e_nw_s", [128, 1024], F32)
    bgh_s = sb("e_bgh_s", [128, 8], F32)
    otA = sb("e_otA", [128, 1024], BF16)
    otB = sb("e_otB", [128, 1024], BF16)
    ot = sb("e_ot", [128, 1024], F32)
    ot2 = sb("e_ot2", [128, 1024], F32)
    vstg = sb("e_vstg", [128, 1024], BF16)
    vt = sb("e_vt", [128, 1024], BF16)
    ght = sb("e_ght", [128, 8], F32)
    wrk = sb("e_wrk", [128, 24], F32)
    o2 = sb("e_o2", [128, 1024], F32)
    ssq = sb("e_ssq", [128, 8], F32)
    o3 = sb("e_o3", [128, 1024], BF16)
    ofm = sb("e_ofm", [128, KD * TOK], BF16)
    yt = sb("e_yt", [128, 2 * 512], F32)
    xr = sb("e_xr", [128, 2 * 512], F32)
    mm = [ps("e_mmA", [128, 512], F32), ps("e_mmB", [128, 512], F32)]
    pq = ps("e_pqA", [128, 1024], BF16)

    LW0 = X.val("ld_w")
    for kcx in range(KD):
        p.op("gpsimd", lambda e, kcx=kcx: e.dma_start(
                out=wo_s[:, kcx * D:(kcx + 1) * D],
                in_=wo[kcx * 128:(kcx + 1) * 128, :]),
             incs=[(ld_w, 16)], dma=True)
        X.bump("ld_w")
    WRDY = X.val("ld_w")
    LWH = X.val("ld_wh")
    p.op("sync", lambda e: e.dma_start(out=nw_s[:], in_=nw_in),
         incs=[(ld_wh, 16)])
    p.op("sync", lambda e: e.dma_start(out=bgh_s[:], in_=bgh),
         incs=[(ld_wh, 16)])
    X.bump("ld_wh", 32)
    LWH += 32

    ev_sig = None
    tile_sigs = {}
    for i in range(16):
        tguard = tile_sigs.get(i - 1)
        first = (i == 0)
        for r in range(2):
            for h2 in range(2):
                for (buf, goff) in ((otA, 0), (otB, 16)):
                    wts = [tguard] if (r == 0 and h2 == 0 and goff == 0) else []
                    if first and r == 0 and h2 == 0 and goff == 0:
                        wts.append((s_cc, 4))
                    p.op("sync", lambda e, r=r, h2=h2, buf=buf, goff=goff, i=i:
                         e.dma_start(
                            out=buf[:, (2 * r + h2) * 256:
                                    (2 * r + h2 + 1) * 256],
                            in_=cc2_out[r * 8192 + h2 * 4096 + (goff + i) * 128:
                                        r * 8192 + h2 * 4096 + (goff + i + 1) * 128,
                                        :]),
                         waits=wts, incs=[(ld_t, 16)])
                    X.bump("ld_t")
        p.op("sync", lambda e, i=i: e.dma_start(
                out=vstg.rearrange("p (cx t) -> p cx t", cx=KD),
                in_=ccq_in[2048:3072, i * 128:(i + 1) * 128]
                    .rearrange("(cx p) t -> cx p t", p=128)
                    .rearrange("cx p t -> p cx t")),
             incs=[(ld_t, 16)])
        X.bump("ld_t")
        p.op("sync", lambda e, i=i: e.dma_start(
                out=ght[:],
                in_=ccs2_in[i * 128:(i + 1) * 128, 8:16]),
             incs=[(ld_t, 16)])
        X.bump("ld_t")
        nldt = X.val("ld_t")
        ldw = [(ld_t, nldt), (ld_w, WRDY), (ld_wh, LWH)]
        p.op("vector", lambda e: e.tensor_scalar(ot[:], otB[:], mfl[:, 1:2],
                                                 None, ALU.mult), waits=ldw)
        blend_sig = p.op("vector", lambda e: e.scalar_tensor_tensor(
                ot[:], otA[:], mfl[:, 0:1], ot[:], ALU.mult, ALU.add))
        for jj in range(8):
            wts = []
            if jj == 0:
                wts = [ev_sig, (ld_t, nldt)]
            p.op("tensor", lambda e, jj=jj: e.transpose(
                    pq[:, jj * 128:(jj + 1) * 128],
                    vstg[:, jj * 128:(jj + 1) * 128], c.ident_bf),
                 waits=wts, incs=([(s_tq, 1)] if jj == 7 else []))
        tqc["n"] += 1
        vt_sig = p.op("scalar", lambda e: e.activation(
                vt[:], pq[:, 0:1024], AF.Copy),
             waits=[(s_tq, tqc["n"])])
        p.op("vector", lambda e: e.tensor_add(wrk[:, 0:4], ght[:, 0:4],
                                              bgh_s[:, 0:4]), waits=ldw)
        p.op("vector", lambda e: e.tensor_add(wrk[:, 4:8], ght[:, 4:8],
                                              bgh_s[:, 4:8]))
        p.op("vector", lambda e: e.tensor_max(wrk[:, 8:12], wrk[:, 0:4],
                                              wrk[:, 4:8]))
        p.op("vector", lambda e: e.tensor_scalar(wrk[:, 8:12], wrk[:, 8:12],
                                                 0.0, None, ALU.max))
        p.op("vector", lambda e: e.tensor_sub(wrk[:, 0:4], wrk[:, 0:4],
                                              wrk[:, 8:12]))
        vsb = p.op("vector", lambda e: e.tensor_sub(wrk[:, 4:8], wrk[:, 4:8],
                                                    wrk[:, 8:12]))
        p.op("scalar", lambda e: e.activation(wrk[:, 12:16], wrk[:, 0:4],
                                              AF.Exp), waits=[vsb])
        p.op("scalar", lambda e: e.activation(wrk[:, 16:20], wrk[:, 4:8],
                                              AF.Exp))
        es = p.op("scalar", lambda e: e.activation(wrk[:, 20:24], wrk[:, 8:12],
                                                   AF.Exp, scale=-1.0))
        p.op("vector", lambda e: e.tensor_add(wrk[:, 12:16], wrk[:, 12:16],
                                              wrk[:, 16:20]), waits=[es])
        p.op("vector", lambda e: e.tensor_add(wrk[:, 12:16], wrk[:, 12:16],
                                              wrk[:, 20:24]))
        rcp = p.op("vector", lambda e: e.reciprocal(wrk[:, 16:20],
                                                    wrk[:, 12:16]))
        wr = p.op("vector", lambda e: e.tensor_mul(wrk[:, 20:24],
                                                   wrk[:, 20:24],
                                                   wrk[:, 16:20]),
                  waits=[rcp])
        o2s = None
        for h in range(4):
            o2s = p.op("vector", lambda e, h=h: e.scalar_tensor_tensor(
                    o2[:, h * 256:(h + 1) * 256],
                    vt[:, h * 256:(h + 1) * 256],
                    wrk[:, 20 + h:21 + h],
                    ot[:, h * 256:(h + 1) * 256], ALU.mult, ALU.add),
                 waits=([wr, blend_sig, vt_sig] if h == 0 else ()))
        sqs = p.op("vector", lambda e: e.tensor_mul(yt[:, 0:1024],
                                                    o2[:, 0:1024],
                                                    o2[:, 0:1024]),
                   waits=[o2s])
        p.op("vector", lambda e: e.tensor_reduce(
                ssq[:, 0:4], yt[:, 0:1024].rearrange("p (a t) -> p a t", a=4),
                AX.X, ALU.add))
        tss = p.op("vector", lambda e: e.tensor_scalar(
                ssq[:, 0:4], ssq[:, 0:4], 1.0 / 256.0,
                EPS, ALU.mult, ALU.add))
        lns = p.op("scalar", lambda e: e.activation(ssq[:, 4:8], ssq[:, 0:4],
                                                    AF.Ln), waits=[tss])
        rst = p.op("scalar", lambda e: e.activation(ssq[:, 4:8], ssq[:, 4:8],
                                                    AF.Exp, scale=-0.5),
                   waits=[lns])
        for h in range(4):
            p.op("vector", lambda e, h=h: e.tensor_scalar(
                    o2[:, h * 256:(h + 1) * 256], o2[:, h * 256:(h + 1) * 256],
                    ssq[:, 4 + h:5 + h], None, ALU.mult),
                 waits=([rst] if h == 0 else ()))
        o3s = p.op("vector", lambda e: e.tensor_mul(o3[:], o2[:, 0:1024],
                                                    nw_s[:]))
        for quad in range(2):
            for jj in range(4):
                cx = quad * 4 + jj
                wts = []
                if jj == 0:
                    wts = [o3s, ev_sig, vt_sig]
                p.op("tensor", lambda e, cx=cx, jj=jj: e.transpose(
                        pq[:, jj * 128:(jj + 1) * 128],
                        o3[:, cx * 128:(cx + 1) * 128], c.ident_bf),
                     waits=wts, incs=([(s_tq, 1)] if jj == 3 else []))
            tqc["n"] += 1
            ev_sig = p.op("vector", lambda e, quad=quad, i=i: e.tensor_copy(
                    ofm.rearrange("p (cx t) -> p cx t", cx=KD)
                       [:, quad * 4:(quad + 1) * 4, i * 128:(i + 1) * 128],
                    pq[:, 0:512].rearrange("p (a f) -> p a f", a=4)),
                 waits=[(s_tq, tqc["n"])])
        tile_sigs[i] = ev_sig

    EW = 512
    nst = 0
    est = {}
    pmm = 0
    for g in range(4):
        for mc in range(KD):
            par = pmm % 2
            p.op("sync", lambda e, mc=mc, g=g, par=par:
                 e.dma_start(out=xr[:, par * EW:(par + 1) * EW],
                             in_=x[mc * 128:(mc + 1) * 128,
                                   x_off + g * EW:x_off + (g + 1) * EW]),
                 waits=[est.get(par)], incs=[(ldr, 16)])
            nldr = X.bump("ldr")
            for kcx in range(KD):
                wts = []
                if kcx == 0:
                    wts = [ev_sig, est.get(par), (ld_w, WRDY), (ld_wh, LWH)]
                p.op("tensor", lambda e, par=par, mc=mc, g=g, kcx=kcx:
                     e.matmul(mm[par][:, 0:EW],
                              wo_s.rearrange("p (kc f) -> p kc f", f=D)
                                  [:, kcx, mc * 128:(mc + 1) * 128],
                              ofm[:, kcx * TOK + g * EW:
                                  kcx * TOK + (g + 1) * EW],
                              start=(kcx == 0), stop=(kcx == KD - 1)),
                     waits=wts,
                     incs=([(s_mm, 1)] if kcx == KD - 1 else []))
            mmc["n"] += 1
            epar = nst % 2
            esig = p.op("vector", lambda e, par=par, epar=epar: e.tensor_add(
                    yt[:, epar * EW:(epar + 1) * EW], mm[par][:, 0:EW],
                    xr[:, par * EW:(par + 1) * EW]),
                 waits=[(s_mm, mmc["n"]), (ldr, nldr),
                        (s_st[epar], 16 * nstp[epar])])
            est[par] = esig
            pmm += 1
            p.op("sync", lambda e, epar=epar, mc=mc, g=g: e.dma_start(
                    out=y[mc * 128:(mc + 1) * 128, g * EW:(g + 1) * EW],
                    in_=yt[:, epar * EW:(epar + 1) * EW]),
                 waits=[esig], incs=[(s_st[epar], 16)])
            nstp[epar] += 1
            nst += 1


def emit_l1scan(X, st, ccq_out, ccs_out, msk_d, bgh_d, cc2_in, scr1, scr2):
    nc, p, c = X.nc, X.p, X.c
    L4L = 4096
    NG = 32
    CCq = 3072
    ld_in = X.S("ld_in")
    ld_sm = X.S("ld_sm")
    ld_sh = X.S("ld_sh")
    s_tq = X.S("s_tq")
    s_mm = X.S("s_mm")
    s_cc = X.S("s_cc")
    s_st = [X.S("s_st0"), X.S("s_st1")]
    nstp = X.nstp
    mmc, tqc = X.mmc, X.tqc
    mflc = c.extra["mflag"]
    selm_sb = c.extra["selm"]

    sb = lambda name, shp, dt: st.enter_context(nc.sbuf_tensor(name, shp, dt))
    qtm = sb("d_qtm", [128, NG * 256], BF16)
    ktm = sb("d_ktm", [128, NG * 256], BF16)
    vtm = sb("d_vtm", [128, NG * 256], BF16)
    q_fm = sb("d_q_fm", [128, 2 * L4L], BF16)
    k_fm = sb("d_k_fm", [128, 2 * L4L], BF16)
    TB = [sb("d_TB_l", [128, NG * 128], BF16), sb("d_TB_g", [128, NG * 128], BF16)]
    T0F = [sb("d_T0F_l", [128, 4096], F32), sb("d_T0F_g", [128, 4096], F32)]
    T_vec = sb("d_T_vec", [128, 1024], F32)
    T_vt = sb("d_T_vt", [128, 1024], F32)
    T_bfs = sb("d_T_bfs", [128, 1024], BF16)
    kkm_sb = sb("d_kkm_sb", [128, 128], F32)
    acc32 = sb("d_acc32", [128, 32], F32)
    blg = [sb("d_bl_s", [128, NG], F32), sb("d_bg_s", [128, NG], F32)]
    nblg = [sb("d_nbl_s", [128, NG], F32), sb("d_nbg_s", [128, NG], F32)]
    ggs = sb("d_ggs", [128, NG], F32)
    hhs = sb("d_hhs", [128, NG], F32)
    wlg = [sb("d_wl_s", [128, NG], F32), sb("d_wg_s", [128, NG], F32)]
    gwk = sb("d_gwk", [128, 8 * NG], F32)
    bghs = sb("d_bghs", [128, 4], F32)
    msk = sb("d_msk_s", [128, 416], F32)
    nrm = sb("d_nrm", [128, 2 * NG], F32)
    vs_sb = sb("d_vs_sb", [128, 512], BF16)
    kb_sb = sb("d_kb_sb", [128, 512], BF16)
    wm_sb = sb("d_wm_sb", [128, 512], BF16)
    wkm_sb = [sb("d_wkm_l", [128, 128], BF16), sb("d_wkm_g", [128, 128], BF16)]
    wkm3 = [sb("d_wkm3_l", [128, 128], BF16), sb("d_wkm3_g", [128, 128], BF16)]
    grm_sb = sb("d_grm_sb", [128, 128], BF16)
    Usb = [sb("d_U_l", [128, 256], BF16), sb("d_U_g", [128, 256], BF16)]
    S_bf = [sb("d_Sbf_l", [128, 512], BF16), sb("d_Sbf_g", [128, 512], BF16)]
    S_sb = [sb("d_Ssb_l", [128, 512], F32), sb("d_Ssb_g", [128, 512], F32)]
    o_stage = sb("d_o_stage", [128, 2 * 256], BF16)
    stgA = sb("d_stgA", [128, 2 * TOK], BF16)
    stgB = sb("d_stgB", [128, 2 * TOK], BF16)
    smfull = sb("d_smfull", [128, 512], F32)
    smT = sb("d_smT", [128, 32], F32)
    nld = {"n": 0}
    nsh = {"n": 0}

    MS_BD = msk[:, 32:160]
    MS_BL = msk[:, 160:288]
    MS_CA = msk[:, 288:416]
    nsm = {"n": 0}
    p.op("sync", lambda e: e.dma_start(out=msk[:], in_=msk_d),
         incs=[(ld_sm, 16)])
    nsm["n"] += 16
    p.op("sync", lambda e: e.dma_start(out=bghs[:], in_=bgh_d),
         incs=[(ld_sm, 16)])
    nsm["n"] += 16
    mz_sig = None
    for t in (Usb[0], Usb[1], S_bf[0], S_bf[1], wkm3[0], wkm3[1],
              TB[0], TB[1]):
        mz_sig = p.op("vector", lambda e, t=t: e.memset(t[:], 0.0))

    for r in range(2):
        for gl in range(16):
            wts = []
            if r == 0 and gl == 0:
                wts = [(s_cc, 3)]
            p.op("sync", lambda e, r=r, gl=gl: e.dma_start(
                    out=smfull[:, (r * 16 + gl) * 16:(r * 16 + gl + 1) * 16],
                    in_=ccs_out[r * TOK + gl * 128:r * TOK + (gl + 1) * 128,
                                0:16]),
                 waits=wts, incs=[(ld_sm, 16)])
            nsm["n"] += 16
    LDSM = nsm["n"]
    smv = smfull.rearrange("p (g q) -> p g q", q=16)

    head_free = None
    shuf_done = None
    conv_ev = None
    with nc.psum_tensor("d_pqc", [128, 1024], F32) as pqc:
        for hh in range(2):
            stg_free = None
            for ti, dst in enumerate((qtm, ktm, vtm)):
                for cc2 in range(2):
                    for (stg, pbase) in ((stgA, 0), (stgB, 512)):
                        for r in range(2):
                            row0 = (r * CCq + ti * 1024 + pbase + hh * 256 +
                                    cc2 * 128)
                            wts = [stg_free]
                            if (hh == 0 and ti == 0 and cc2 == 0 and
                                    pbase == 0 and r == 0):
                                wts.append((s_cc, 3))
                            p.op("sync", lambda e, stg=stg, r=r, row0=row0:
                                 e.dma_start(
                                    out=stg[:, r * TOK:(r + 1) * TOK],
                                    in_=ccq_out[row0:row0 + 128, :]),
                                 waits=wts, incs=[(ld_in, 16)])
                            nld["n"] += 16
                    LDS = nld["n"]
                    for gb in range(4):
                        for gi in range(8):
                            g = gb * 8 + gi
                            r = g // 16
                            t0 = (g % 16) * 128
                            wts = []
                            if gi == 0:
                                wts = [conv_ev]
                                if gb == 0:
                                    wts.append((ld_in, LDS))
                            p.op("tensor", lambda e, gi=gi, r=r, t0=t0:
                                 e.matmul(pqc[:, gi * 128:(gi + 1) * 128],
                                          stgA[:, r * TOK + t0:
                                               r * TOK + t0 + 128],
                                          selm_sb[:, 0:128],
                                          start=True, stop=False),
                                 waits=wts)
                            p.op("tensor", lambda e, gi=gi, r=r, t0=t0:
                                 e.matmul(pqc[:, gi * 128:(gi + 1) * 128],
                                          stgB[:, r * TOK + t0:
                                               r * TOK + t0 + 128],
                                          selm_sb[:, 128:256],
                                          start=False, stop=True),
                                 incs=([(s_mm, 1)] if gi == 7 else []))
                        mmc["n"] += 1
                        cwts = [(s_mm, mmc["n"])]
                        if gb == 0 and cc2 == 0 and ti == 0:
                            cwts.append(head_free)
                        conv_ev = p.op("scalar",
                             lambda e, dst=dst, gb=gb, cc2=cc2:
                             e.copy(
                                dst.rearrange("p (g cx) -> p g cx", cx=256)
                                   [:, gb * 8:(gb + 1) * 8,
                                    cc2 * 128:(cc2 + 1) * 128],
                                pqc[:, 0:1024]
                                   .rearrange("p (g cx) -> p g cx", cx=128)),
                             waits=cwts)
                    stg_free = (s_mm, mmc["n"])
            sm_sig = None
            t3 = smT[:].rearrange("p (g o) -> p g o", o=1)
            for (dstb, base) in ((blg[0], 0), (blg[1], 4), (ggs, 8),
                                 (hhs, 12)):
                d3 = dstb[:].rearrange("p (g o) -> p g o", o=1)
                a3 = smv[:, :, base + hh:base + hh + 1]
                b3 = smv[:, :, base + 2 + hh:base + 2 + hh + 1]
                p.op("gpsimd", lambda e, d3=d3, a3=a3: e.tensor_scalar(
                        d3, a3, mflc[:, 0:1], None, ALU.mult),
                     waits=[(ld_sm, LDSM)] + ([head_free] if base == 0 else []))
                p.op("gpsimd", lambda e, b3=b3: e.tensor_scalar(
                        t3, b3, mflc[:, 1:2], None, ALU.mult))
                sm_sig = p.op("gpsimd", lambda e, d3=d3: e.tensor_add(
                        d3, d3, t3))

            for (ti, tsr) in ((0, qtm), (1, ktm)):
                for g in range(NG):
                    p.op("vector", lambda e, tsr=tsr, g=g: e.tensor_mul(
                            T_vec[:, 0:256], tsr[:, 256 * g:256 * (g + 1)],
                            tsr[:, 256 * g:256 * (g + 1)]),
                         waits=([sm_sig, conv_ev, mz_sig, (ld_sm, LDSM),
                                 (s_st[0], 16 * nstp[0]),
                                 (s_st[1], 16 * nstp[1])]
                                if (ti == 0 and g == 0) else []))
                    p.op("vector", lambda e, ti=ti, g=g: e.tensor_reduce(
                            nrm[:, ti * NG + g:ti * NG + g + 1],
                            T_vec[:, 0:256], AX.X, ALU.add))
            ns = p.op("vector", lambda e: e.tensor_scalar(
                    nrm[:], nrm[:], 1e-12, None, ALU.add))
            p.op("scalar", lambda e: e.activation(nrm[:], nrm[:], AF.Ln),
                 waits=[ns])
            rs = p.op("scalar", lambda e: e.activation(nrm[:], nrm[:], AF.Exp,
                                                       scale=-0.5))
            sc_sig = None
            for (ti, tsr) in ((0, qtm), (1, ktm)):
                for g in range(NG):
                    sc_sig = p.op("vector", lambda e, ti=ti, tsr=tsr, g=g:
                         e.tensor_scalar(
                            tsr[:, 256 * g:256 * (g + 1)],
                            tsr[:, 256 * g:256 * (g + 1)],
                            nrm[:, ti * NG + g:ti * NG + g + 1], None, ALU.mult),
                         waits=([rs] if (ti == 0 and g == 0) else ()))
            p.op("vector", lambda e: e.tensor_scalar(nblg[0][:], blg[0][:],
                                                     -1.0, None, ALU.mult),
                 waits=[sm_sig])
            nb_sig = p.op("vector", lambda e: e.tensor_scalar(
                    nblg[1][:], blg[1][:], -1.0, None, ALU.mult))
            W0 = gwk
            p.op("vector", lambda e, hh=hh: e.tensor_scalar(
                    W0[:, 0:NG], ggs[:], bghs[:, 2 * hh:2 * hh + 1], None,
                    ALU.add),
                 waits=[sm_sig])
            p.op("vector", lambda e, hh=hh: e.tensor_scalar(
                    W0[:, NG:2 * NG], hhs[:], bghs[:, 2 * hh + 1:2 * hh + 2],
                    None, ALU.add))
            p.op("vector", lambda e: e.tensor_max(W0[:, 2 * NG:3 * NG],
                                                  W0[:, 0:NG],
                                                  W0[:, NG:2 * NG]))
            p.op("vector", lambda e: e.tensor_scalar(
                    W0[:, 2 * NG:3 * NG], W0[:, 2 * NG:3 * NG], 0.0, None,
                    ALU.max))
            p.op("vector", lambda e: e.tensor_sub(W0[:, 0:NG], W0[:, 0:NG],
                                                  W0[:, 2 * NG:3 * NG]))
            sb_sig = p.op("vector", lambda e: e.tensor_sub(
                    W0[:, NG:2 * NG], W0[:, NG:2 * NG], W0[:, 2 * NG:3 * NG]))
            p.op("scalar", lambda e: e.activation(W0[:, 3 * NG:4 * NG],
                                                  W0[:, 0:NG], AF.Exp),
                 waits=[sb_sig])
            p.op("scalar", lambda e: e.activation(W0[:, 4 * NG:5 * NG],
                                                  W0[:, NG:2 * NG], AF.Exp))
            ex_sig = p.op("scalar", lambda e: e.activation(
                    W0[:, 5 * NG:6 * NG], W0[:, 2 * NG:3 * NG], AF.Exp,
                    scale=-1.0))
            p.op("vector", lambda e: e.tensor_add(W0[:, 6 * NG:7 * NG],
                                                  W0[:, 3 * NG:4 * NG],
                                                  W0[:, 4 * NG:5 * NG]),
                 waits=[ex_sig])
            p.op("vector", lambda e: e.tensor_add(W0[:, 6 * NG:7 * NG],
                                                  W0[:, 6 * NG:7 * NG],
                                                  W0[:, 5 * NG:6 * NG]))
            p.op("vector", lambda e: e.reciprocal(W0[:, 7 * NG:8 * NG],
                                                  W0[:, 6 * NG:7 * NG]))
            p.op("vector", lambda e: e.tensor_mul(wlg[0][:],
                                                  W0[:, 3 * NG:4 * NG],
                                                  W0[:, 7 * NG:8 * NG]))
            gate_sig = p.op("vector", lambda e: e.tensor_mul(
                    wlg[1][:], W0[:, 4 * NG:5 * NG], W0[:, 7 * NG:8 * NG]))

            with nc.psum_tensor(f"d_pq{hh}", [128, 1024], BF16) as pq, \
                 nc.psum_tensor(f"d_kk_ps{hh}", [128, 512], F32) as kk_ps:
                ev_sig = None
                for (tsr, fmt) in ((qtm, q_fm), (ktm, k_fm)):
                    for g in range(NG):
                        par = g % 2
                        for cc in range(2):
                            wts = []
                            if cc == 0:
                                wts = [sc_sig, ev_sig]
                            p.op("tensor", lambda e, tsr=tsr, g=g, cc=cc,
                                 par=par:
                                 e.transpose(pq[:, par * 256 + cc * 128:
                                                par * 256 + (cc + 1) * 128],
                                             tsr[:, 256 * g + 128 * cc:
                                                 256 * g + 128 * (cc + 1)],
                                             c.ident_bf),
                                 waits=wts,
                                 incs=([(s_tq, 1)] if cc == 1 else []))
                        tqc["n"] += 1
                        for cc in range(2):
                            ev_sig = p.op("vector",
                                 lambda e, fmt=fmt, g=g, cc=cc, par=par:
                                 e.tensor_copy(
                                     fmt[:, cc * L4L + 128 * g:
                                         cc * L4L + 128 * (g + 1)],
                                     pq[:, par * 256 + cc * 128:
                                        par * 256 + (cc + 1) * 128]),
                                 waits=[(s_tq, tqc["n"])])
                t0_sig = None
                for g in range(NG):
                    for cc in range(2):
                        wts = []
                        if cc == 0:
                            wts = [ev_sig, nb_sig, t0_sig]
                        p.op("tensor", lambda e, g=g, cc=cc: e.matmul(
                                kk_ps[:, 0:128],
                                k_fm[:, cc * L4L + 128 * g:
                                     cc * L4L + 128 * (g + 1)],
                                k_fm[:, cc * L4L + 128 * g:
                                     cc * L4L + 128 * (g + 1)],
                                start=(cc == 0), stop=(cc == 1)),
                             waits=wts,
                             incs=([(s_mm, 1)] if cc == 1 else []))
                    mmc["n"] += 1
                    p.op("vector", lambda e: e.tensor_mul(
                            kkm_sb[:], kk_ps[:, 0:128], MS_BD),
                         waits=[(s_mm, mmc["n"])])
                    for s in range(2):
                        t0_sig = p.op("vector", lambda e, g=g, s=s:
                             e.tensor_scalar(
                                T0F[s][:, 128 * g:128 * (g + 1)], kkm_sb[:],
                                nblg[s][:, g:g + 1], None, ALU.mult))
                prev_tv_done = None
                for s in range(2):
                    p.op("sync", lambda e, s=s: e.dma_start(out=scr1,
                                                            in_=T0F[s][:]),
                         waits=[t0_sig, shuf_done], incs=[(ld_sh, 16)])
                    nsh["n"] += 16
                    WSC = nsh["n"]
                    for a in range(4):
                        p.op("sync", lambda e, a=a: e.dma_start(
                                out=T_vec[32 * a:32 * (a + 1), :]
                                    .rearrange("p (r cc) -> p r cc", r=32),
                                in_=scr1.rearrange(
                                    "(A r) (g blk cc) -> A blk g r cc",
                                    A=4, blk=4, cc=32)[a, a]),
                             waits=([prev_tv_done, (ld_sh, WSC)] if a == 0
                                    else [(ld_sh, WSC)]),
                             incs=[(ld_sh, 16)])
                        nsh["n"] += 16
                    rec_sig = None
                    first_rec = [(ld_sh, nsh["n"])]
                    for i in range(1, 32):
                        for k2 in range(i):
                            if i == 1:
                                src_out = T_vec[:, 32:64]
                                in1 = T_vec[:, 32:64]
                            elif k2 == 0:
                                src_out = acc32[:]
                                in1 = T_vec[:, 32 * i:32 * (i + 1)]
                            elif k2 == i - 1:
                                src_out = T_vec[:, 32 * i:32 * (i + 1)]
                                in1 = acc32[:]
                            else:
                                src_out = acc32[:]
                                in1 = acc32[:]
                            rec_sig = p.op("vector",
                                 lambda e, i=i, k2=k2, src_out=src_out, in1=in1:
                                 e.scalar_tensor_tensor(
                                     src_out, T_vec[:, 32 * k2:32 * (k2 + 1)],
                                     T_vec[:, 32 * i + k2:32 * i + k2 + 1],
                                     in1, ALU.mult, ALU.add),
                                 waits=(first_rec if (i == 1 and k2 == 0)
                                        else ()))
                            first_rec = []
                    p.op("vector", lambda e: e.tensor_scalar(
                            T_vec[:, 0:1024:33], T_vec[:, 0:1024:33], 1.0,
                            None, ALU.add))
                    prev_tv_done = p.op("vector", lambda e: e.tensor_copy(
                            T_vt[:].rearrange("p (cc r) -> p cc r", cc=32),
                            T_vec[:].rearrange("p (r cc) -> p r cc", r=32)
                                    .rearrange("p r cc -> p cc r")))
                    tb_sig = p.op("vector", lambda e: e.tensor_copy(
                            T_bfs[:], T_vt[:]))
                    p.op("sync", lambda e: e.dma_start(out=scr2, in_=T_bfs[:]),
                         waits=[tb_sig], incs=[(ld_sh, 16)])
                    nsh["n"] += 16
                    WSC2 = nsh["n"]
                    for a in range(4):
                        p.op("sync", lambda e, s=s, a=a: e.dma_start(
                                out=TB[s][32 * a:32 * (a + 1), :]
                                    .rearrange("k (g blk r) -> k g blk r",
                                               blk=4, r=32)[:, :, a, :],
                                in_=scr2.rearrange("(A g) (k r) -> A g k r",
                                                   A=4, r=32)[a]
                                        .rearrange("g k r -> k g r")),
                             waits=[(ld_sh, WSC2), mz_sig],
                             incs=[(ld_sh, 16)])
                        nsh["n"] += 16
                    shuf_done = (ld_sh, nsh["n"])
            SHUF = (ld_sh, nsh["n"])

            p.op("vector", lambda e: e.memset(S_bf[0][:], 0.0),
                 waits=[head_free])
            p.op("vector", lambda e: e.memset(S_bf[1][:], 0.0))
            p.op("vector", lambda e: e.memset(S_sb[0][:], 0.0))
            sbz = p.op("vector", lambda e: e.memset(S_sb[1][:], 0.0))
            with nc.psum_tensor(f"d_sd_l{hh}", [128, 512], F32) as sd_l, \
                 nc.psum_tensor(f"d_sd_g{hh}", [128, 512], F32) as sd_g, \
                 nc.psum_tensor(f"d_u_l{hh}", [128, 512], F32) as u_l, \
                 nc.psum_tensor(f"d_u_g{hh}", [128, 512], F32) as u_g, \
                 nc.psum_tensor(f"d_o_ps{hh}", [128, 512], F32) as o_ps, \
                 nc.psum_tensor(f"d_gram{hh}", [128, 512], F32) as gram:
                sd_ps = [sd_l, sd_g]
                u_ps = [u_l, u_g]
                vk_free = None
                wm_free = None
                gram_free = None
                sbf_sig = {0: sbz, 1: sbz}
                ext_sigs = {}
                o_ev = {0: None, 1: None}
                for g in range(NG):
                    p.op("vector", lambda e, g=g: e.tensor_scalar(
                            vs_sb[:, 0:256], vtm[:, 256 * g:256 * (g + 1)],
                            blg[0][:, g:g + 1], None, ALU.mult),
                         waits=[vk_free, gate_sig, SHUF])
                    p.op("vector", lambda e, g=g: e.tensor_scalar(
                            vs_sb[:, 256:512], vtm[:, 256 * g:256 * (g + 1)],
                            blg[1][:, g:g + 1], None, ALU.mult))
                    p.op("vector", lambda e, g=g: e.tensor_scalar(
                            kb_sb[:, 0:256], ktm[:, 256 * g:256 * (g + 1)],
                            nblg[0][:, g:g + 1], None, ALU.mult))
                    vk_sig = p.op("vector", lambda e, g=g: e.tensor_scalar(
                            kb_sb[:, 256:512], ktm[:, 256 * g:256 * (g + 1)],
                            nblg[1][:, g:g + 1], None, ALU.mult))
                    for s in range(2):
                        p.op("tensor", lambda e, s=s, g=g: e.matmul(
                                u_ps[s][:, 0:256], TB[s][:, 128 * g:128 * (g + 1)],
                                vs_sb[:, s * 256:(s + 1) * 256],
                                start=True, stop=False),
                             waits=([vk_sig, SHUF, sbf_sig[0], sbf_sig[1]]
                                    if s == 0 else []))
                    for s in range(2):
                        for cc in range(2):
                            p.op("tensor", lambda e, s=s, cc=cc, g=g: e.matmul(
                                    gram[:, s * 256 + cc * 128:
                                         s * 256 + (cc + 1) * 128],
                                    kb_sb[:, s * 256 + cc * 128:
                                          s * 256 + (cc + 1) * 128],
                                    TB[s][:, 128 * g:128 * (g + 1)],
                                    start=(s == 0 and cc == 0),
                                    stop=(s == 1 and cc == 1)),
                                 waits=([gram_free] if (s == 0 and cc == 0)
                                        else []),
                                 incs=([(s_mm, 1)] if (s == 1 and cc == 1)
                                       else []))
                    mmc["n"] += 1
                    wm_sig = p.op("vector", lambda e: e.tensor_copy(
                            wm_sb[:], gram[:, 0:512]),
                         waits=[(s_mm, mmc["n"])])
                    for s in range(2):
                        for cc in range(2):
                            p.op("tensor", lambda e, s=s, cc=cc: e.matmul(
                                    u_ps[s][:, 0:256],
                                    wm_sb[:, s * 256 + cc * 128:
                                          s * 256 + (cc + 1) * 128],
                                    S_bf[s][:, cc * 256:(cc + 1) * 256],
                                    start=False, stop=(cc == 1)),
                                 waits=([wm_sig] if (s == 0 and cc == 0)
                                        else []))
                    for s in range(2):
                        for cc in range(2):
                            p.op("tensor", lambda e, s=s, cc=cc, g=g: e.matmul(
                                    gram[:, s * 128:(s + 1) * 128],
                                    k_fm[:, cc * L4L + 128 * g:
                                         cc * L4L + 128 * (g + 1)],
                                    wm_sb[:, s * 256 + cc * 128:
                                          s * 256 + (cc + 1) * 128],
                                    start=(s == 0 and cc == 0),
                                    stop=(s == 1 and cc == 1)),
                                 waits=([wm_sig] if (s == 0 and cc == 0)
                                        else []),
                                 incs=([(s_mm, 1)] if (s == 1 and cc == 1)
                                       else []))
                    mmc["n"] += 1
                    wk3 = None
                    for s in range(2):
                        p.op("vector", lambda e, s=s: e.tensor_mul(
                                wkm_sb[s][:], gram[:, s * 128:(s + 1) * 128],
                                MS_BL),
                             waits=([(s_mm, mmc["n"])] if s == 0 else ()))
                        wk3 = p.op("vector", lambda e, s=s: e.tensor_copy(
                                wkm3[s][:, 96:128], wkm_sb[s][:, 96:128]))
                    wm_free = wk3
                    EXT_SLICES = [(0, 32), (32, 64), (64, 96), (64, 128)]
                    for s in range(2):
                        ext_sigs[(0, s)] = p.op("vector",
                             lambda e, s=s, g=g: e.tensor_scalar(
                                Usb[s][0:32, :], u_ps[s][0:32, 0:256],
                                wlg[s][0:32, g:g + 1], None, ALU.mult),
                             waits=[wk3] if s == 0 else [])
                    for a in range(1, 4):
                        for s in range(2):
                            if a < 3:
                                p.op("tensor", lambda e, s=s, a=a: e.matmul(
                                        u_ps[s][32 * a:32 * (a + 1), 0:256],
                                        wkm_sb[s][:, 32 * a:32 * (a + 1)],
                                        Usb[s][:],
                                        start=False, stop=True,
                                        skip_group_check=True),
                                     waits=[ext_sigs[(a - 1, s)]],
                                     incs=([(s_mm, 1)] if s == 1 else []))
                            else:
                                p.op("tensor", lambda e, s=s: e.matmul(
                                        u_ps[s][:, 0:256], wkm3[s][:],
                                        Usb[s][:],
                                        start=False, stop=True,
                                        skip_group_check=True),
                                     waits=[ext_sigs[(2, s)]],
                                     incs=([(s_mm, 1)] if s == 1 else []))
                        mmc["n"] += 1
                        lo, hi = EXT_SLICES[a]
                        for s in range(2):
                            ext_sigs[(a, s)] = p.op("vector",
                                 lambda e, s=s, g=g, lo=lo, hi=hi:
                                 e.tensor_scalar(
                                    Usb[s][lo:hi, :], u_ps[s][lo:hi, 0:256],
                                    wlg[s][lo:hi, g:g + 1], None, ALU.mult),
                                 waits=([(s_mm, mmc["n"])] if s == 0 else []))
                    uprime_done = ext_sigs[(3, 1)]
                    for s in range(2):
                        for cc in range(2):
                            p.op("tensor", lambda e, s=s, cc=cc, g=g: e.matmul(
                                    o_ps[:, 0:256],
                                    q_fm[:, cc * L4L + 128 * g:
                                         cc * L4L + 128 * (g + 1)],
                                    S_bf[s][:, cc * 256:(cc + 1) * 256],
                                    start=(s == 0 and cc == 0), stop=False),
                                 waits=([o_ev[0], o_ev[1]]
                                        if (s == 0 and cc == 0) else []))
                    for cc in range(2):
                        p.op("tensor", lambda e, cc=cc, g=g: e.matmul(
                                gram[:, 0:128],
                                k_fm[:, cc * L4L + 128 * g:
                                     cc * L4L + 128 * (g + 1)],
                                q_fm[:, cc * L4L + 128 * g:
                                     cc * L4L + 128 * (g + 1)],
                                start=(cc == 0), stop=(cc == 1)),
                             waits=([wm_free] if cc == 0 else []),
                             incs=([(s_mm, 1)] if cc == 1 else []))
                    mmc["n"] += 1
                    gr_sig = p.op("vector", lambda e: e.tensor_mul(
                            grm_sb[:], gram[:, 0:128], MS_CA),
                         waits=[(s_mm, mmc["n"])])
                    gram_free = gr_sig
                    for s in range(2):
                        p.op("tensor", lambda e, s=s: e.matmul(
                                o_ps[:, 0:256], grm_sb[:], Usb[s][:],
                                start=False, stop=(s == 1)),
                             waits=([gr_sig, uprime_done] if s == 0 else []),
                             incs=([(s_mm, 1)] if s == 1 else []))
                    mmc["n"] += 1
                    OMM = mmc["n"]
                    for s in range(2):
                        for cc in range(2):
                            p.op("tensor", lambda e, s=s, cc=cc, g=g: e.matmul(
                                    sd_ps[s][:, cc * 256:(cc + 1) * 256],
                                    ktm[:, 256 * g + 128 * cc:
                                        256 * g + 128 * (cc + 1)],
                                    Usb[s][:],
                                    start=(cc == 0), stop=(cc == 1)),
                                 waits=([sbf_sig[s]] if cc == 0 else []),
                                 incs=([(s_mm, 1)] if (s == 1 and cc == 1)
                                       else []))
                    mmc["n"] += 1
                    SMM = mmc["n"]
                    par = g % 2
                    oev = p.op("vector", lambda e, par=par: e.tensor_copy(
                            o_stage[:, par * 256:(par + 1) * 256],
                            o_ps[:, 0:256]),
                         waits=[(s_mm, OMM), (s_st[par], 16 * nstp[par])])
                    o_ev[par] = oev
                    p.op("sync", lambda e, hh=hh, g=g, par=par: e.dma_start(
                            out=cc2_in[(hh * NG + g) * 128:
                                       (hh * NG + g + 1) * 128, :],
                            in_=o_stage[:, par * 256:(par + 1) * 256]),
                         waits=[oev], incs=[(s_st[par], 16)])
                    nstp[par] += 1
                    for s in range(2):
                        p.op("vector", lambda e, s=s: e.tensor_add(
                                S_sb[s][:], S_sb[s][:], sd_ps[s][:, 0:512]),
                             waits=([(s_mm, SMM)] if s == 0 else ()))
                        sbf_sig[s] = p.op("vector", lambda e, s=s: e.tensor_copy(
                                S_bf[s][:], S_sb[s][:]))
                    vk_free = (s_mm, SMM)
                head_free = sbf_sig[1]


def build_fused():
    nc = bass.Bass("TRN2", target_bir_lowering=False, debug=False)
    import ml_dtypes
    dri = lambda n, shp, dt=F32: nc.dram_tensor(n, shp, dt,
                                                kind="ExternalInput").ap()
    a_x = dri("a_x", [D, EXTL])
    a_wq = dri("a_wq", [D, D])
    a_wk = dri("a_wk", [D, D])
    a_wv = dri("a_wv", [D, D])
    a_wo = dri("a_wo", [D, D])
    a_kinit = dri("a_kinit", [128, 8 * 63])
    a_vinit = dri("a_vinit", [128, 8 * 63])
    a_maskc = dri("a_maskc", [128, 256])
    b_w1 = dri("b_w1", [D, 4096])
    b_w2 = dri("b_w2", [4096, D])
    b_b1t = dri("b_b1t", [128, 32])
    b_b2t = dri("b_b2t", [128, 8])
    b_a1h = dri("b_a1h", [128, 8 * 24])
    b_b1c = dri("b_b1c", [24, 4096])
    b_r1h = dri("b_r1h", [128, 8 * 3])
    b_a2h = dri("b_a2h", [128, 32 * 24])
    b_b2c = dri("b_b2c", [24, D])
    b_r2h = dri("b_r2h", [128, 32 * 3])
    c_wq = dri("c_wq", [D, D])
    c_wk = dri("c_wk", [D, D])
    c_wv = dri("c_wv", [D, D])
    c_convs = dri("c_convs", [128, 8 * 12])
    c_wsm = dri("c_wsm", [D, 96])
    d_msk = dri("d_msk", [128, 416])
    d_bgh = dri("d_bgh", [128, 4])
    e_bgh = dri("e_bgh", [128, 8])
    e_nw = dri("e_nw", [128, 1024])
    e_wo = dri("e_wo", [D, D])
    f_w1 = dri("f_w1", [4096, D])
    f_b1t = dri("f_b1t", [128, 32])
    f_w2 = dri("f_w2", [D, 4096])
    f_b2t = dri("f_b2t", [128, 8])
    mflag = dri("mflag", [128, 2])
    selm = dri("selm", [128, 256], BF16)
    y = nc.dram_tensor("y", [D, TOK], F32, kind="ExternalOutput").ap()
    drt = lambda n, shp, dt: nc.dram_tensor(n, shp, dt, kind="Internal").ap()
    xb1 = drt("xb1", [D, TOK], F32)
    xb2e = drt("xb2e", [D, EXT3], F32)
    xb3 = drt("xb3", [D, TOK], F32)
    sv_in = drt("sv_in", [D, 3], F32)
    sv_out = drt("sv_out", [2 * D, 3], F32)
    ccq_in = drt("ccq_in", [3 * D, TOK], BF16)
    ccq_out = drt("ccq_out", [6 * D, TOK], BF16)
    ccs_in = drt("ccs_in", [TOK, 16], F32)
    ccs_out = drt("ccs_out", [2 * TOK, 16], F32)
    cc2_in = drt("cc2_in", [8192, 256], BF16)
    cc2_out = drt("cc2_out", [16384, 256], BF16)
    scr1 = drt("d_scr1", [128, 4096], F32)
    scr2 = drt("d_scr2", [128, 1024], BF16)

    c = Consts(nc)
    e24_np = np.zeros((3, 24), np.float32)
    for e_ in range(3):
        e24_np[e_, e_ * 8:(e_ + 1) * 8] = 1.0
    e24_dram = nc.inline_tensor(e24_np.astype(ml_dtypes.bfloat16),
                                name="e24").ap()

    with ExitStack() as st:
        p = Prog(nc)
        c.materialize(p, st, extra_dram=[
            ("e24", e24_dram, [3, 24], BF16),
            ("mflag", mflag, [128, 2], F32),
            ("selm", selm, [128, 256], BF16)])
        X = Ctx(nc, p, c)
        s_cc = X.S("s_cc")
        s_z = X.S("s_z")
        ld_m = X.S("ld_m")
        svb = st.enter_context(nc.sbuf_tensor("svb", [128, 24], F32))
        zb = st.enter_context(nc.sbuf_tensor("zb", [128, 256], F32))
        p.op("gpsimd", lambda e: e.memset(zb[:], 0.0), setup=True, dma=True,
             incs=[(s_z, 1)])
        for mc in range(KD):
            p.op("sync", lambda e, mc=mc: e.dma_start(
                    out=xb2e[mc * 128:(mc + 1) * 128, 2051:2304],
                    in_=zb[:, 0:253]),
                 waits=[(s_z, 1)], incs=[(c.s_const, 16)], setup=True)
            c.ready_val += 16

        with ExitStack() as stA:
            emit_l0attn(X, stA, a_x, a_wq, a_wk, a_wv, a_wo, a_kinit, a_vinit,
                        a_maskc, xb1)
        X.join()
        with ExitStack() as stB:
            emit_l0ffn(X, stB, xb1, b_w1, b_w2, b_b1t, b_b2t, b_a1h, b_b1c,
                       b_r1h, b_a2h, b_b2c, b_r2h, xb2e, 3)
        X.join()
        p.op("sync", lambda e: e.dma_start(
                out=svb.rearrange("p (cx t) -> p cx t", cx=KD)[:, :, 0:3],
                in_=xb2e[:, 2048:2051]
                    .rearrange("(cx p) t -> cx p t", p=128)
                    .rearrange("cx p t -> p cx t")),
             incs=[(ld_m, 16)])
        X.bump("ld_m")
        p.op("sync", lambda e: e.dma_start(
                out=sv_in.rearrange("(cx p) t -> cx p t", p=128)
                         .rearrange("cx p t -> p cx t"),
                in_=svb.rearrange("p (cx t) -> p cx t", cx=KD)[:, :, 0:3]),
             waits=[(ld_m, X.val("ld_m"))], incs=[(ld_m, 16)])
        X.bump("ld_m")
        p.op("gpsimd", lambda e: e.collective_compute(
                "AllGather", mybir.AluOpType.bypass,
                replica_groups=PAIR_GROUPS,
                ins=[sv_in.opt()], outs=[sv_out.opt()]),
             waits=[(ld_m, X.val("ld_m"))], incs=[(s_cc, 1)], dma=True)
        p.op("sync", lambda e: e.dma_start(
                out=svb.rearrange("p (cx t) -> p cx t", cx=KD)[:, :, 0:3],
                in_=sv_out[0:D, :]
                    .rearrange("(cx p) t -> cx p t", p=128)
                    .rearrange("cx p t -> p cx t")),
             waits=[(s_cc, 1)], incs=[(ld_m, 16)])
        X.bump("ld_m")
        hb = p.op("vector", lambda e: e.tensor_scalar(
                svb[:, 0:24], svb[:, 0:24], c.extra["mflag"][:, 1:2],
                None, ALU.mult),
             waits=[(ld_m, X.val("ld_m"))])
        p.op("sync", lambda e: e.dma_start(
                out=xb2e[:, 0:3]
                    .rearrange("(cx p) t -> cx p t", p=128)
                    .rearrange("cx p t -> p cx t"),
                in_=svb.rearrange("p (cx t) -> p cx t", cx=KD)[:, :, 0:3]),
             waits=[hb], incs=[(ld_m, 16)])
        X.bump("ld_m")
        HALO = (ld_m, X.val("ld_m"))
        with ExitStack() as stC:
            p.op("sync", lambda e: e.nop(), waits=[HALO])
            emit_l1front(X, stC, xb2e, [c_wq, c_wk, c_wv], c_convs, c_wsm,
                         ccq_in, ccs_in)
        jC = X.join()
        p.op("gpsimd", lambda e: e.collective_compute(
                "AllGather", mybir.AluOpType.bypass,
                replica_groups=PAIR_GROUPS,
                ins=[ccq_in.opt()], outs=[ccq_out.opt()]),
             waits=[jC], incs=[(s_cc, 1)], dma=True)
        p.op("gpsimd", lambda e: e.collective_compute(
                "AllGather", mybir.AluOpType.bypass,
                replica_groups=PAIR_GROUPS,
                ins=[ccs_in.opt()], outs=[ccs_out.opt()]),
             incs=[(s_cc, 1)], dma=True)
        with ExitStack() as stD:
            emit_l1scan(X, stD, ccq_out, ccs_out, d_msk, d_bgh, cc2_in,
                        scr1, scr2)
        jD = X.join()
        p.op("gpsimd", lambda e: e.collective_compute(
                "AllGather", mybir.AluOpType.bypass,
                replica_groups=PAIR_GROUPS,
                ins=[cc2_in.opt()], outs=[cc2_out.opt()]),
             waits=[jD], incs=[(s_cc, 1)], dma=True)
        with ExitStack() as stE:
            emit_l1comb(X, stE, cc2_out, ccq_in, ccs_in, e_bgh, e_nw, e_wo,
                        xb2e, 3, xb3)
        X.join()
        with ExitStack() as stF:
            emit_l1ffn(X, stF, xb3, f_w1, f_b1t, f_w2, f_b2t, y)
        X.join()
        p.emit(c.s_const, c.ready_val)
    return nc


def kernel(L0_attn_x, L0_attn_Wq, L0_attn_Wk, L0_attn_Wv, L0_attn_Wo,
           L0_attn_last_k_init, L0_attn_last_v_init,
           L0_ffn_W1, L0_ffn_b1, L0_ffn_A1, L0_ffn_B1, L0_ffn_R1,
           L0_ffn_W2, L0_ffn_b2, L0_ffn_A2, L0_ffn_B2, L0_ffn_R2,
           L1_attn_Wq, L1_attn_Wk, L1_attn_Wv, L1_attn_conv_q, L1_attn_conv_k,
           L1_attn_conv_v, L1_attn_Wb, L1_attn_Wg, L1_attn_bg, L1_attn_Wh,
           L1_attn_bh, L1_attn_norm_w, L1_attn_Wo,
           L1_ffn_w1, L1_ffn_b1, L1_ffn_w2, L1_ffn_b2):
    import ml_dtypes
    f32 = lambda a: np.asarray(a, np.float32)
    nc = build_fused()

    x0 = f32(L0_attn_x)
    xin = np.zeros((NCORES, EXTL, D), np.float32)
    for i in range(NCORES):
        j, s = i // 2, i % 2
        xin[i, 63:63 + TOK] = x0[j, s * TOK:(s + 1) * TOK]
        if s == 1:
            xin[i, 0:63] = x0[j, TOK - 63:TOK]
    xin_t = np.ascontiguousarray(xin.transpose(0, 2, 1))
    ki = np.ascontiguousarray(f32(L0_attn_last_k_init).reshape(63, D).T
                              .reshape(8, 128, 63).transpose(1, 0, 2)
                              .reshape(128, 504))
    vi = np.ascontiguousarray(f32(L0_attn_last_v_init).reshape(63, D).T
                              .reshape(8, 128, 63).transpose(1, 0, 2)
                              .reshape(128, 504))
    maskc = np.full((128, 256), -30000.0, np.float32)
    for r in range(64):
        for qq in range(2):
            maskc[r, qq * 128 + r:qq * 128 + r + 64] = 0.0
            maskc[64 + r, qq * 128 + r:qq * 128 + r + 64] = 0.0
    a1hh = np.ascontiguousarray(
        f32(L0_ffn_A1).transpose(1, 0, 2).reshape(8, 128, 24).transpose(1, 0, 2)
        .reshape(128, 8 * 24))
    a2hh = np.ascontiguousarray(
        f32(L0_ffn_A2).transpose(1, 0, 2).reshape(32, 128, 24).transpose(1, 0, 2)
        .reshape(128, 32 * 24))
    b1cc = np.ascontiguousarray(f32(L0_ffn_B1).reshape(24, 4096))
    b2cc = np.ascontiguousarray(f32(L0_ffn_B2).reshape(24, D))
    r1hh = np.ascontiguousarray(
        f32(L0_ffn_R1).reshape(8, 128, 3).transpose(1, 0, 2).reshape(128, 24))
    r2hh = np.ascontiguousarray(
        f32(L0_ffn_R2).reshape(32, 128, 3).transpose(1, 0, 2).reshape(128, 96))
    b1tt = np.ascontiguousarray(f32(L0_ffn_b1).reshape(32, 128).T)
    b2tt = np.ascontiguousarray(f32(L0_ffn_b2).reshape(8, 128).T)
    cv = np.zeros((128, 8 * 12), np.float32)
    for kcx in range(8):
        for ti, cw in enumerate((L1_attn_conv_q, L1_attn_conv_k,
                                 L1_attn_conv_v)):
            cv[:, kcx * 12 + ti * 4:kcx * 12 + (ti + 1) * 4] = \
                f32(cw)[kcx * 128:(kcx + 1) * 128, :]
    wsm_np = np.zeros((D, 96), np.float32)
    wsm_np[:, 0:8] = f32(L1_attn_Wb)
    wsm_np[:, 32:36] = f32(L1_attn_Wg)
    wsm_np[:, 64:68] = f32(L1_attn_Wh)
    msk = np.zeros((128, 416), np.float32)
    rr = np.arange(128)
    blk = rr // 32
    for i in range(128):
        for jj in range(128):
            if blk[i] == blk[jj] and (i % 32) > (jj % 32):
                msk[i, 32 + jj] = 1.0
            if blk[i] < blk[jj]:
                msk[i, 160 + jj] = 1.0
            if blk[i] < blk[jj] or (blk[i] == blk[jj] and i <= jj):
                msk[i, 288 + jj] = 1.0
    bghv8 = np.zeros((128, 8), np.float32)
    bghv8[:, 0:4] = f32(L1_attn_bg)[None, :]
    bghv8[:, 4:8] = f32(L1_attn_bh)[None, :]
    nwv = np.broadcast_to(f32(L1_attn_norm_w)[None, :], (128, 256))
    nwt = np.ascontiguousarray(np.tile(nwv, (1, 4)))
    fb1tt = np.ascontiguousarray(f32(L1_ffn_b1).reshape(32, 128).T)
    fb2tt = np.ascontiguousarray(f32(L1_ffn_b2).reshape(8, 128).T)

    eye = np.eye(128, dtype=np.float32)
    in_maps = []
    for i in range(NCORES):
        j, s = i // 2, i % 2
        mfl = np.zeros((128, 2), np.float32)
        mfl[:, 0] = 1.0 if s == 0 else 0.0
        mfl[:, 1] = 0.0 if s == 0 else 1.0
        selm = np.zeros((128, 256), np.float32)
        selm[:, 0:128] = eye * (1.0 if s == 0 else 0.0)
        selm[:, 128:256] = eye * (0.0 if s == 0 else 1.0)
        bghv4 = np.zeros((128, 4), np.float32)
        for hh in range(2):
            h = 2 * s + hh
            bghv4[:, 2 * hh] = float(f32(L1_attn_bg)[h])
            bghv4[:, 2 * hh + 1] = float(f32(L1_attn_bh)[h])
        in_maps.append({
            "a_x": xin_t[i], "a_wq": f32(L0_attn_Wq), "a_wk": f32(L0_attn_Wk),
            "a_wv": f32(L0_attn_Wv), "a_wo": f32(L0_attn_Wo),
            "a_kinit": ki, "a_vinit": vi, "a_maskc": maskc,
            "b_w1": f32(L0_ffn_W1), "b_w2": f32(L0_ffn_W2),
            "b_b1t": b1tt, "b_b2t": b2tt, "b_a1h": a1hh, "b_b1c": b1cc,
            "b_r1h": r1hh, "b_a2h": a2hh, "b_b2c": b2cc, "b_r2h": r2hh,
            "c_wq": f32(L1_attn_Wq), "c_wk": f32(L1_attn_Wk),
            "c_wv": f32(L1_attn_Wv), "c_convs": cv, "c_wsm": wsm_np,
            "d_msk": msk, "d_bgh": bghv4,
            "e_bgh": bghv8, "e_nw": nwt, "e_wo": f32(L1_attn_Wo),
            "f_w1": f32(L1_ffn_w1), "f_b1t": fb1tt,
            "f_w2": f32(L1_ffn_w2), "f_b2t": fb2tt,
            "mflag": mfl,
            "selm": selm.astype(ml_dtypes.bfloat16),
        })
    res = run_bass_kernel_spmd(nc, in_maps, core_ids=list(range(NCORES)))
    yv = np.stack([r["y"] for r in res.results])
    return np.ascontiguousarray(yv.transpose(0, 2, 1)).reshape(4, 4096, D)


# version 10
# speedup vs baseline: 1.0331x; 1.0288x over previous
import numpy as np
from contextlib import ExitStack
import concourse.bass as bass
import concourse.mybir as mybir
from concourse.bass_utils import run_bass_kernel_spmd

F32 = mybir.dt.float32
BF16 = mybir.dt.bfloat16
AF = mybir.ActivationFunctionType
ALU = mybir.AluOpType
AX = mybir.AxisListType
NCORES = 8
D = 1024
KD = 8
TOK = 2048
EPS = 1e-5
CHAINED = ("vector", "gpsimd", "scalar")
PAIR_GROUPS = [[0, 1], [2, 3], [4, 5], [6, 7]]


class Prog:

    def __init__(self, nc):
        self.nc = nc
        self.ops = []
        self.self_sems = {e: nc.alloc_semaphore("self_" + e) for e in CHAINED}
        self.k = {e: 0 for e in CHAINED}
        self.all_sems = list(self.self_sems.values())
        self.pending_pre = {}

    def sem(self, name):
        h = self.nc.alloc_semaphore(name)
        self.all_sems.append(h)
        return h

    def set_phase_pre(self, sig):
        for e in ("vector", "gpsimd", "scalar", "tensor", "sync"):
            self.pending_pre[e] = sig

    def op(self, eng, fn, waits=(), incs=(), setup=False, dma=False):
        waits = [w for w in waits if w is not None]
        if not setup and eng in self.pending_pre:
            waits.append(self.pending_pre.pop(eng))
        sig = None
        chain_wait = None
        if eng in CHAINED and not dma:
            assert not incs, "chained ops cannot carry explicit incs"
            if self.k[eng] > 0:
                chain_wait = (self.self_sems[eng], self.k[eng])
            self.k[eng] += 1
            sig = (self.self_sems[eng], self.k[eng])
            incs = [(self.self_sems[eng], 1)]
        else:
            assert len(incs) <= 1
        self.ops.append((eng, fn, tuple(waits), tuple(incs), setup, chain_wait))
        return sig

    def emit(self, setup_sem=None, setup_val=0):
        nc = self.nc
        with nc.Block() as block:
            for name, meth in (("sync", block.sync), ("gpsimd", block.gpsimd),
                               ("scalar", block.scalar), ("vector", block.vector),
                               ("tensor", block.tensor)):
                def body(eng, name=name):
                    first = True
                    for e, fn, waits, incs, setup, chain_wait in self.ops:
                        if e != name:
                            continue
                        if first and not setup and setup_sem is not None:
                            eng.wait_ge(setup_sem, setup_val)
                        if not setup:
                            first = False
                        if chain_wait is not None:
                            eng.wait_ge(*chain_wait)
                        for (h, v) in waits:
                            if v > 0:
                                eng.wait_ge(h, v)
                        ins = fn(eng)
                        for (h, kk) in incs:
                            ins = ins.then_inc(h, kk)
                meth(body)
        nc.all_engine_barrier()
        nc.clear_and_free_semaphores(self.all_sems)
        nc.all_engine_barrier()


class Consts:

    def __init__(self, nc):
        import ml_dtypes
        self._ident_dram = nc.inline_tensor(
            np.eye(128, dtype=np.float32).astype(ml_dtypes.bfloat16),
            name="ident_bf").ap()
        self.nc = nc

    def materialize(self, p, st, extra_dram=()):
        nc = self.nc
        sbt = lambda name, shp, dt: st.enter_context(nc.sbuf_tensor(name, shp, dt))
        self.ident_bf = sbt("c_ident", [128, 128], BF16)[:]
        self.ident_f32 = sbt("c_identf", [128, 128], F32)[:]
        self.ones_col = sbt("c_ones_col", [128, 1], F32)[:]
        self.ones_row = sbt("c_ones_row", [1, 128], F32)[:]
        self.s_const = p.sem("s_const")
        s_id = p.sem("s_id")
        n = 0
        p.op("sync", lambda e: e.dma_start(out=self.ident_bf, in_=self._ident_dram),
             incs=[(s_id, 16)], setup=True)
        p.op("scalar", lambda e: e.copy(self.ident_f32, self.ident_bf),
             incs=[(self.s_const, 17)], setup=True, dma=True,
             waits=[(s_id, 16)])
        n += 17
        p.op("gpsimd", lambda e: e.memset(self.ones_col, 1.0), setup=True, dma=True,
             incs=[(self.s_const, 1)])
        n += 1
        p.op("gpsimd", lambda e: e.memset(self.ones_row, 1.0), setup=True, dma=True,
             incs=[(self.s_const, 1)])
        n += 1
        self.extra = {}
        for (name, dram_ap, shp, dt) in extra_dram:
            t = sbt("c_" + name, shp, dt)
            self.extra[name] = t
            p.op("sync", lambda e, t=t, dram_ap=dram_ap:
                 e.dma_start(out=t[:], in_=dram_ap),
                 incs=[(self.s_const, 16)], setup=True)
            n += 16
        self.ready_val = n


class Ctx:

    def __init__(self, nc, p, c):
        self.nc, self.p, self.c = nc, p, c
        self.sems = {}
        self.cnt = {}
        self.mmc = {"n": 0}
        self.tqc = {"n": 0}
        self.ldc = {"n": 0}
        self.nstp = [0, 0]
        self.nph = 0

    def S(self, name):
        if name not in self.sems:
            self.sems[name] = self.p.sem(name)
        return self.sems[name]

    def bump(self, name, v=16):
        self.cnt[name] = self.cnt.get(name, 0) + v
        return self.cnt[name]

    def val(self, name):
        return self.cnt.get(name, 0)

    def join(self):
        p = self.p
        w = [(self.S("s_st0"), 16 * self.nstp[0]),
             (self.S("s_st1"), 16 * self.nstp[1]),
             (self.S("s_mm"), self.mmc["n"]),
             (self.S("s_tq"), self.tqc["n"])]
        for e in CHAINED:
            w.append((p.self_sems[e], p.k[e]))
        self.nph += 1
        p.op("sync", lambda e: e.nop(), waits=w, incs=[(self.S("s_ph"), 1)])
        sig = (self.S("s_ph"), self.nph)
        p.set_phase_pre(sig)
        return sig


def emit_ln_fm(p, c, xg, xsq, xln, stats_ps, mb_ps, rb_ps, rowbuf, scr,
               s_mm, mmc, ncols, nch=KD, eps=EPS, pre_waits=(), xln_stride=None,
               xln_off=0):
    W = ncols
    mrow = rowbuf[0:1, 0:W]
    vrow = rowbuf[0:1, W:2 * W]
    rrow = rowbuf[0:1, 2 * W:3 * W]
    sq_sig = p.op("scalar", lambda e: e.activation(xsq[:, 0:nch * W],
                                                   xg[:, 0:nch * W], AF.Square),
                  waits=pre_waits)
    for cc in range(nch):
        p.op("tensor",
             lambda e, cc=cc: e.matmul(stats_ps[0:1, 0:W], c.ones_col,
                                       xg[:, cc * W:(cc + 1) * W],
                                       start=(cc == 0), stop=(cc == nch - 1)),
             waits=(pre_waits if cc == 0 else ()))
    for cc in range(nch):
        p.op("tensor",
             lambda e, cc=cc: e.matmul(stats_ps[32:33, 0:W], c.ones_col,
                                       xsq[:, cc * W:(cc + 1) * W],
                                       start=(cc == 0), stop=(cc == nch - 1)),
             waits=([sq_sig] if cc == 0 else ()),
             incs=([(s_mm, 1)] if cc == nch - 1 else ()))
    mmc["n"] += 1
    stat_w = [(s_mm, mmc["n"])]
    p.op("vector", lambda e: e.tensor_scalar(mrow, stats_ps[0:1, 0:W],
                                             1.0 / (nch * 128.0), None, ALU.mult),
         waits=stat_w)
    p.op("vector", lambda e: e.tensor_mul(vrow, mrow, mrow))
    p.op("vector", lambda e: e.scalar_tensor_tensor(vrow, stats_ps[32:33, 0:W],
                                                    1.0 / (nch * 128.0), vrow,
                                                    ALU.mult, ALU.subtract))
    veps = p.op("vector", lambda e: e.tensor_scalar(vrow, vrow, eps, None, ALU.add))
    p.op("scalar", lambda e: e.activation(rrow, vrow, AF.Ln), waits=[veps])
    rsig = p.op("scalar", lambda e: e.activation(rrow, rrow, AF.Exp, scale=-0.5))
    p.op("tensor", lambda e: e.matmul(mb_ps[:, 0:W], c.ones_row, mrow,
                                      start=True, stop=True),
         waits=[rsig])
    p.op("tensor", lambda e: e.matmul(rb_ps[:, 0:W], c.ones_row, rrow,
                                      start=True, stop=True),
         incs=[(s_mm, 1)])
    mmc["n"] += 1
    bc_w = [(s_mm, mmc["n"])]
    xst = W if xln_stride is None else xln_stride
    last = None
    for cc in range(nch):
        p.op("vector", lambda e, cc=cc: e.tensor_sub(
                scr[:, 0:W], xg[:, cc * W:(cc + 1) * W], mb_ps[:, 0:W]),
             waits=(bc_w if cc == 0 else ()))
        last = p.op("vector", lambda e, cc=cc: e.tensor_mul(
                xln[:, cc * xst + xln_off:cc * xst + xln_off + W], scr[:, 0:W],
                rb_ps[:, 0:W]))
    return last


def emit_wprep_transposed(p, c, specs, stag, sgn, pq, ld_slab, ldc, s_tq, tqc,
                          first_evac_waits=()):
    red_sig = None
    sgn_sig = None
    evac_sig = None
    alpha_sigs = []
    first = True
    for (whbm, nslab, scols, nq4, fdim, wT, alp, use_sign) in specs:
        nqq = nq4 // 4
        for m in range(nslab):
            dw = [red_sig, sgn_sig]
            if first:
                dw += list(first_evac_waits)
            p.op("sync", lambda e, whbm=whbm, m=m, scols=scols:
                 e.dma_start(out=stag[:, 0:scols],
                             in_=whbm[m * 128:(m + 1) * 128, :]),
                 waits=dw, incs=[(ld_slab, 16)])
            ldc["n"] += 16
            ldw = [(ld_slab, ldc["n"])]
            if alp is not None:
                red_sig = p.op("vector", lambda e, m=m, scols=scols, alp=alp:
                               e.tensor_reduce(alp[:, m:m + 1], stag[:, 0:scols],
                                               AX.X, ALU.add,
                                               apply_absolute_value=True),
                               waits=ldw)
                alpha_sigs.append(red_sig)
            tq_w = [(s_tq, tqc["n"])]
            if use_sign:
                sgn_sig = p.op("scalar", lambda e, scols=scols:
                               e.sign(sgn[:, 0:scols], stag[:, 0:scols]),
                               waits=ldw + tq_w)
            else:
                sgn_sig = p.op("scalar", lambda e, scols=scols:
                               e.copy(sgn[:, 0:scols], stag[:, 0:scols]),
                               waits=ldw + tq_w)
            for q in range(nqq):
                for j in range(4):
                    kc = q * 4 + j
                    wts = []
                    if j == 0:
                        wts = [sgn_sig, evac_sig]
                        if first:
                            wts += list(first_evac_waits)
                    p.op("tensor", lambda e, j=j, kc=kc:
                         e.transpose(pq[:, j * 128:(j + 1) * 128],
                                     sgn[:, kc * 128:(kc + 1) * 128], c.ident_bf),
                         waits=wts,
                         incs=([(s_tq, 1)] if j == 3 else []))
                tqc["n"] += 1
                first = False
                evac_sig = p.op("vector", lambda e, q=q, m=m, wT=wT, fdim=fdim:
                                e.tensor_copy(
                                    wT.rearrange("p (kc f) -> p kc f", f=fdim)
                                      [:, q * 4:q * 4 + 4, m * 128:(m + 1) * 128],
                                    pq[:, 0:512].rearrange("p (a f) -> p a f", a=4)),
                                waits=[(s_tq, tqc["n"])])
    return evac_sig, alpha_sigs


EXTL = 2304


def emit_l0attn(X, st, x, wq, wk, wv, wo, kinit, vinit, maskc, y):
    nc, p, c = X.nc, X.p, X.c
    PW = 384
    NPG = 6
    ld_slab = X.S("ld_slab")
    ld_m = X.S("ld_m")
    ld_msw = X.S("ld_msw")
    ldx = X.S("ldx")
    ldr = X.S("ldr")
    s_tq = X.S("s_tq")
    s_mm = X.S("s_mm")
    s_st = [X.S("s_st0"), X.S("s_st1")]
    nstp = X.nstp
    mmc, tqc, ldc = X.mmc, X.tqc, X.ldc
    mfl = c.extra["mflag"]

    sb = lambda name, shp, dt: st.enter_context(nc.sbuf_tensor(name, shp, dt))
    ps = lambda name, shp, dt: st.enter_context(nc.psum_tensor(name, shp, dt))
    xln_ext = sb("a_xln_ext", [128, KD * EXTL], BF16)
    wqT = sb("a_wqT", [128, KD * D], BF16)
    wkT = sb("a_wkT", [128, KD * D], BF16)
    wvT = sb("a_wvT", [128, KD * D], BF16)
    stag = sb("a_stag", [128, 1024], F32)
    sgn = sb("a_sgn", [128, 1024], BF16)
    kin_s = sb("a_kin_s", [128, 8 * 63], BF16)
    vin_s = sb("a_vin_s", [128, 8 * 63], BF16)
    mk = sb("a_mk", [128, 256], F32)
    scr = sb("a_scr", [128, 768], F32)
    rowbuf = sb("a_rowbuf", [1, 3 * PW], F32)
    q_g = sb("a_q_g", [128, 2 * EXTL], BF16)
    k_g = sb("a_k_g", [128, 2 * EXTL], BF16)
    v_g = sb("a_v_g", [128, 2 * EXTL], BF16)
    v_tm = sb("a_v_tm", [128, 17 * 256], BF16)
    scf = sb("a_scf", [128, 512], F32)
    attn_n = sb("a_attn_n", [128, 512], BF16)
    attnT = sb("a_attnT", [128, 512], BF16)
    nmx = sb("a_nmx", [128, 4], F32)
    smr = sb("a_smr", [128, 8], F32)
    yt = sb("a_yt", [128, 2 * 512], F32)
    xr = sb("a_xr", [128, 2 * 512], F32)
    ao_fm = sb("a_ao_fm", [128, KD * TOK], BF16)
    stats_ps = ps("a_stats", [128, 512], F32)
    mb_ps = ps("a_mb_ps", [128, 512], F32)
    rb_ps = ps("a_rb_ps", [128, 512], F32)
    mm = [ps("a_mmA", [128, 512], F32), ps("a_mmB", [128, 512], F32)]
    sc_ps = ps("a_scA", [128, 512], F32)
    pq = ps("a_pqA", [128, 512], BF16)
    pq2 = [pq, ps("a_pqB", [128, 512], BF16)]

    LDMSW = X.val("ld_msw")
    LDM = X.val("ld_m")
    for (eng, sem, nm, dst, src) in (("gpsimd", ld_msw, "ld_msw", kin_s, kinit),
                                     ("gpsimd", ld_msw, "ld_msw", vin_s, vinit),
                                     ("sync", ld_m, "ld_m", mk, maskc)):
        p.op(eng, lambda e, dst=dst, src=src: e.dma_start(out=dst[:], in_=src),
             incs=[(sem, 16)], dma=True)
        X.bump(nm)
    LDMSW += 32
    LDM += 16

    nldx = X.val("ldx")
    ln_sig = None
    with nc.sbuf_tensor("a_xg0", [128, KD * PW], F32, side="right") as xg, \
         nc.sbuf_tensor("a_xsq0", [128, KD * PW], F32, side="right") as xsq:
        for g in range(NPG):
            t0 = g * PW
            p.op("sync", lambda e, t0=t0:
                 e.dma_start(out=xg.rearrange("p (c t) -> p c t", c=KD),
                             in_=x[:, t0:t0 + PW]
                                 .rearrange("(c p) t -> c p t", p=128)
                                 .rearrange("c p t -> p c t")),
                 waits=[ln_sig], incs=[(ldx, 16)])
            nldx = X.bump("ldx")
            ln_sig = emit_ln_fm(p, c, xg, xsq, xln_ext, stats_ps, mb_ps, rb_ps,
                                rowbuf, scr, s_mm, mmc, PW,
                                pre_waits=[(ldx, nldx)],
                                xln_stride=EXTL, xln_off=t0)

    wp_sig, _ = emit_wprep_transposed(
        p, c, [(wq, 8, D, 8, D, wqT, None, False),
               (wk, 8, D, 8, D, wkT, None, False),
               (wv, 8, D, 8, D, wvT, None, False)],
        stag, sgn, pq, ld_slab, ldc, s_tq, tqc)

    SCALE = float(1.0 / np.sqrt(64.0))
    ev_sigs = {}
    aolast = None
    pmm = 0
    for hg in range(4):
        mc0 = 2 * hg
        proj_last = None
        for (wTb, dest) in ((wqT, q_g), (wkT, k_g), (wvT, v_g)):
            for mcl in range(2):
                mc = mc0 + mcl
                for g in range(NPG):
                    par = pmm % 2
                    for kc in range(KD):
                        wts = []
                        if kc == 0:
                            wts = [wp_sig, ln_sig, ev_sigs.get(par)]
                        p.op("tensor", lambda e, par=par, mc=mc, g=g, kc=kc,
                             wTb=wTb:
                             e.matmul(mm[par][:, 0:PW],
                                      wTb.rearrange("p (kc f) -> p kc f", f=D)
                                         [:, kc, mc * 128:(mc + 1) * 128],
                                      xln_ext[:, kc * EXTL + g * PW:
                                              kc * EXTL + (g + 1) * PW],
                                      start=(kc == 0), stop=(kc == KD - 1)),
                             waits=wts,
                             incs=([(s_mm, 1)] if kc == KD - 1 else []))
                    mmc["n"] += 1
                    pmm += 1
                    ev_sigs[par] = p.op(
                        "scalar", lambda e, par=par, mcl=mcl, g=g, dest=dest:
                        e.activation(dest[:, mcl * EXTL + g * PW:
                                          mcl * EXTL + (g + 1) * PW],
                                     mm[par][:, 0:PW], AF.Copy),
                        waits=[(s_mm, mmc["n"])])
                    proj_last = ev_sigs[par]
        bs = None
        for (dst, ini) in ((k_g, kin_s), (v_g, vin_s)):
            for mcl in range(2):
                mc = mc0 + mcl
                p.op("vector", lambda e, ini=ini, mc=mc: e.tensor_scalar(
                        scr[:, 0:63], ini[:, mc * 63:(mc + 1) * 63],
                        mfl[:, 0:1], None, ALU.mult),
                     waits=([proj_last, (ld_msw, LDMSW), (ld_m, LDM)]
                            if bs is None else []))
                bs = p.op("vector", lambda e, dst=dst, mcl=mcl:
                          e.scalar_tensor_tensor(
                              dst[:, mcl * EXTL:mcl * EXTL + 63],
                              dst[:, mcl * EXTL:mcl * EXTL + 63],
                              mfl[:, 1:2], scr[:, 0:63], ALU.mult, ALU.add))
        blend_sig = bs

        vq_sig = None
        for t in range(17):
            for mcl in range(2):
                wts = []
                if mcl == 0:
                    wts = [blend_sig, vq_sig]
                p.op("tensor", lambda e, t=t, mcl=mcl:
                     e.transpose(pq[:, mcl * 128:(mcl + 1) * 128],
                                 v_g[:, mcl * EXTL + 128 * t:
                                      mcl * EXTL + 128 * (t + 1)],
                                 c.ident_bf),
                     waits=wts,
                     incs=([(s_tq, 1)] if mcl == 1 else []))
            tqc["n"] += 1
            vq_sig = p.op("vector", lambda e, t=t:
                          e.tensor_copy(v_tm[:, 256 * t:256 * (t + 1)],
                                        pq[:, 0:256]),
                          waits=[(s_tq, tqc["n"])])

        sc_free = {0: None, 1: None}
        at_free = {0: vq_sig, 1: vq_sig}
        o_free = {0: [None, None], 1: [None, None]}
        sc_mm = {}

        def emit_score(ht):
            odd = ht % 2
            pp2 = ht % 2
            P = 256 * pp2
            t = ht // 2
            kwin = 128 * t + 64 * odd
            q0 = 63 + 128 * t + 64 * odd
            for hl in range(4):
                par = hl % 2
                qq = hl // 2
                mcl = hl // 2
                wts = []
                if hl == 0:
                    wts = [sc_free[pp2], blend_sig]
                p.op("tensor", lambda e, par=par, qq=qq, mcl=mcl,
                     kwin=kwin, q0=q0, P=P:
                     e.matmul(sc_ps[64 * par:64 * par + 64,
                                    P + qq * 128:P + (qq + 1) * 128],
                              q_g[64 * par:64 * par + 64,
                                  mcl * EXTL + q0:mcl * EXTL + q0 + 64],
                              k_g[64 * par:64 * par + 64,
                                  mcl * EXTL + kwin:mcl * EXTL + kwin + 128],
                              start=True, stop=True,
                              tile_position=(64 * par, 64 * par)),
                     waits=wts,
                     incs=([(s_mm, 1)] if hl == 3 else []))
            mmc["n"] += 1
            sc_mm[ht] = mmc["n"]

        pet_tq = {}

        def emit_soft(ht):
            odd = ht % 2
            pp2 = ht % 2
            P = 256 * pp2
            t = ht // 2
            sc_free[pp2] = p.op("vector", lambda e, P=P:
                 e.scalar_tensor_tensor(
                    scf[:, P:P + 256], sc_ps[:, P:P + 256], SCALE, mk[:],
                    ALU.mult, ALU.add),
                 waits=[(s_mm, sc_mm[ht])])
            nm_sig = p.op("vector", lambda e, P=P, pp2=pp2: e.tensor_reduce(
                    nmx[:, 2 * pp2:2 * pp2 + 1],
                    scf[:, P:P + 256],
                    AX.X, ALU.max, negate=True))
            ex_sig = p.op("scalar", lambda e, P=P, pp2=pp2:
                 e.activation(
                    scf[:, P:P + 256],
                    scf[:, P:P + 256],
                    AF.Exp, bias=nmx[:, 2 * pp2:2 * pp2 + 1]),
                 waits=[nm_sig])
            p.op("vector", lambda e, P=P, pp2=pp2: e.tensor_reduce(
                    smr[:, 4 * pp2:4 * pp2 + 2],
                    scf[:, P:P + 256].rearrange("p (a t) -> p a t", a=2),
                    AX.X, ALU.add),
                 waits=[ex_sig])
            p.op("vector", lambda e, pp2=pp2: e.reciprocal(
                    smr[:, 4 * pp2 + 2:4 * pp2 + 4],
                    smr[:, 4 * pp2:4 * pp2 + 2]))
            nops = []
            for qq in range(2):
                if odd == 0:
                    nops.append((qq * 128, qq * 128, 128, qq))
                else:
                    nops.append((qq * 128, qq * 128 + 64, 64, qq))
                    nops.append((qq * 128 + 64, qq * 128, 64, qq))
            an_sig = None
            first_an = [(s_tq, pet_tq.get(ht - 2, 0))]
            for (so, do, wd, qq) in nops:
                an_sig = p.op("vector",
                     lambda e, so=so, do=do, wd=wd, qq=qq, P=P, pp2=pp2:
                     e.tensor_scalar(attn_n[:, P + do:P + do + wd],
                                     scf[:, P + so:P + so + wd],
                                     smr[:, 4 * pp2 + 2 + qq:
                                         4 * pp2 + 3 + qq], None, ALU.mult),
                     waits=first_an)
                first_an = []
            pqb = pq2[pp2]
            for qq in range(2):
                p.op("tensor", lambda e, qq=qq, P=P, pqb=pqb: e.transpose(
                        pqb[:, qq * 128:(qq + 1) * 128],
                        attn_n[:, P + qq * 128:P + (qq + 1) * 128],
                        c.ident_bf),
                     waits=([an_sig, at_free[pp2]] if qq == 0 else ()),
                     incs=([(s_tq, 1)] if qq == 1 else []))
            tqc["n"] += 1
            pet_tq[ht] = tqc["n"]

        def emit_out(ht):
            nonlocal aolast
            odd = ht % 2
            pp2 = ht % 2
            P = 256 * pp2
            AVP = 128 * pp2
            PS = 256 + 256 * pp2
            t = ht // 2
            ra0 = 128 * t + 64 * odd
            pqb = pq2[pp2]
            at_free[pp2] = p.op("vector", lambda e, P=P, ht=ht, pqb=pqb:
                 e.tensor_copy(
                    attnT[:, P:P + 256], pqb[:, 0:256]),
                 waits=[(s_tq, pet_tq[ht])])
            for hl in range(4):
                par = hl % 2
                qq = hl // 2
                wts = []
                if hl == 0:
                    wts = [at_free[pp2], o_free[pp2][0], o_free[pp2][1]]
                if odd == 0:
                    p.op("tensor", lambda e, hl=hl, par=par, qq=qq, t=t,
                         P=P, AVP=AVP:
                         e.matmul(mm[0][64 * par:64 * par + 64,
                                        AVP + qq * 64:AVP + (qq + 1) * 64],
                                  v_tm[:, 256 * t + 64 * hl:
                                       256 * t + 64 * (hl + 1)],
                                  attnT[:, P + qq * 128 + 64 * par:
                                        P + qq * 128 + 64 * (par + 1)],
                                  start=True, stop=True),
                         waits=wts,
                         incs=([(s_mm, 1)] if hl == 3 else []))
                else:
                    p.op("tensor", lambda e, hl=hl, par=par, qq=qq, t=t,
                         P=P, AVP=AVP:
                         e.matmul(mm[0][64 * par:64 * par + 64,
                                        AVP + qq * 64:AVP + (qq + 1) * 64],
                                  v_tm[64:128, 256 * t + 64 * hl:
                                       256 * t + 64 * (hl + 1)],
                                  attnT[64:128, P + qq * 128 + 64 * par:
                                        P + qq * 128 + 64 * (par + 1)],
                                  start=True, stop=True),
                         waits=wts)
                    p.op("tensor", lambda e, hl=hl, par=par, qq=qq, t=t,
                         P=P, AVP=AVP:
                         e.matmul(mm[1][64 * par:64 * par + 64,
                                        AVP + qq * 64:AVP + (qq + 1) * 64],
                                  v_tm[0:64, 256 * (t + 1) + 64 * hl:
                                       256 * (t + 1) + 64 * (hl + 1)],
                                  attnT[0:64, P + qq * 128 + 64 * par:
                                        P + qq * 128 + 64 * (par + 1)],
                                  start=True, stop=True),
                         incs=([(s_mm, 1)] if hl == 3 else []))
            mmc["n"] += 1
            if odd == 1:
                cpsig = p.op("vector", lambda e, PS=PS, AVP=AVP: e.tensor_copy(
                        scr[:, PS:PS + 256], mm[0][:, AVP:AVP + 256]),
                     waits=[(s_mm, mmc["n"])])
            if odd == 0:
                ev = p.op("vector",
                     lambda e, ra0=ra0, mc0=mc0, AVP=AVP:
                     e.tensor_copy(
                         ao_fm.rearrange("p (cc t) -> p cc t", cc=KD)
                              [:, mc0:mc0 + 2, ra0:ra0 + 64],
                         mm[0][:, AVP:AVP + 128]
                         .rearrange("p (a t) -> p a t", a=2)),
                     waits=[(s_mm, mmc["n"])])
            else:
                ev = p.op("vector",
                     lambda e, ra0=ra0, mc0=mc0, PS=PS, AVP=AVP:
                     e.tensor_add(
                         ao_fm.rearrange("p (cc t) -> p cc t", cc=KD)
                              [:, mc0:mc0 + 2, ra0:ra0 + 64],
                         scr[:, PS:PS + 128]
                         .rearrange("p (a t) -> p a t", a=2),
                         mm[1][:, AVP:AVP + 128]
                         .rearrange("p (a t) -> p a t", a=2)),
                     waits=[(s_mm, mmc["n"])])
            o_free[pp2][0] = ev
            o_free[pp2][1] = ev
            aolast = ev

        for k in range(34):
            if k < 32:
                emit_score(k)
            if 1 <= k <= 32:
                emit_soft(k - 1)
            if k >= 2:
                emit_out(k - 2)
        ev_sigs = {0: aolast, 1: aolast}

    wo_sig, _ = emit_wprep_transposed(
        p, c, [(wo, 8, D, 8, D, wqT, None, False)], stag, sgn, pq,
        ld_slab, ldc, s_tq, tqc,
        first_evac_waits=[(s_mm, mmc["n"]), at_free[0], at_free[1]])
    EW = 512
    nst = 0
    est = {}
    for g in range(4):
        for mc in range(KD):
            par = pmm % 2
            p.op("sync", lambda e, mc=mc, g=g, par=par:
                 e.dma_start(out=xr[:, par * EW:(par + 1) * EW],
                             in_=x[mc * 128:(mc + 1) * 128,
                                   63 + g * EW:63 + (g + 1) * EW]),
                 waits=[est.get(par)], incs=[(ldr, 16)])
            nldr = X.bump("ldr")
            for kc in range(KD):
                wts = []
                if kc == 0:
                    wts = [wo_sig, aolast, est.get(par)]
                p.op("tensor", lambda e, par=par, mc=mc, g=g, kc=kc:
                     e.matmul(mm[par][:, 0:EW],
                              wqT.rearrange("p (kc f) -> p kc f", f=D)
                                 [:, kc, mc * 128:(mc + 1) * 128],
                              ao_fm[:, kc * TOK + g * EW:
                                    kc * TOK + (g + 1) * EW],
                              start=(kc == 0), stop=(kc == KD - 1)),
                     waits=wts,
                     incs=([(s_mm, 1)] if kc == KD - 1 else []))
            mmc["n"] += 1
            epar = nst % 2
            esig = p.op("vector", lambda e, par=par, epar=epar: e.tensor_add(
                    yt[:, epar * EW:(epar + 1) * EW], mm[par][:, 0:EW],
                    xr[:, par * EW:(par + 1) * EW]),
                 waits=[(s_mm, mmc["n"]), (ldr, nldr),
                        (s_st[epar], 16 * nstp[epar])])
            est[par] = esig
            pmm += 1
            p.op("sync", lambda e, epar=epar, mc=mc, g=g: e.dma_start(
                    out=y[mc * 128:(mc + 1) * 128, g * EW:(g + 1) * EW],
                    in_=yt[:, epar * EW:(epar + 1) * EW]),
                 waits=[esig], incs=[(s_st[epar], 16)])
            nstp[epar] += 1
            nst += 1


def emit_l0ffn(X, st, x, w1, w2, b1t, b2t, a1h, b1c, r1h, a2h, b2c, r2h,
               y, y_off):
    nc, p, c = X.nc, X.p, X.c
    H = 4096
    MH = 32
    NG2 = 8
    W = 256
    ld_w = X.S("ld_w")
    ldx = X.S("ldx")
    s_mm = X.S("s_mm")
    s_st = [X.S("s_st0"), X.S("s_st1")]
    nstp = X.nstp
    mmc = X.mmc
    e24sb = c.extra["e24"]

    sb = lambda name, shp, dt: st.enter_context(nc.sbuf_tensor(name, shp, dt))
    ps = lambda name, shp, dt: st.enter_context(nc.psum_tensor(name, shp, dt))
    w1s = sb("b_w1s", [128, KD * H], BF16)
    w2s = sb("b_w2s", [128, MH * D], BF16)
    a1s = sb("b_a1s", [128, 8 * 24], BF16)
    b1cs = sb("b_b1cs", [24, H], BF16)
    r1s = sb("b_r1s", [128, 8 * 3], BF16)
    a2s = sb("b_a2s", [128, 32 * 24], BF16)
    b2cs = sb("b_b2cs", [24, D], BF16)
    r2s = sb("b_r2s", [128, 32 * 3], BF16)
    b1s = sb("b_b1s", [128, 32], F32)
    b2s = sb("b_b2s", [128, 8], F32)
    xg = sb("b_xg", [128, KD * W], F32)
    xsq = sb("b_xsq", [128, KD * W], F32)
    xln = sb("b_xln", [128, KD * W], BF16)
    hg = sb("b_hg", [128, MH * W], BF16)
    z_sb = sb("b_z_sb", [24, W], BF16)
    zg_sb = sb("b_zg_sb", [24, W], BF16)
    gex = sb("b_gex", [128, 8], F32)
    gtm = sb("b_gtm", [128, 16], BF16)
    gfm = sb("b_gfm", [3, W], BF16)
    negm = sb("b_negm", [128, 2], F32)
    srec = sb("b_srec", [128, 4], F32)
    scr = sb("b_scr", [128, W], F32)
    tep = sb("b_tep", [128, 2 * W], F32)
    yt = sb("b_yt", [128, 2 * W], F32)
    rowbuf = sb("b_rowbuf", [1, 3 * W], F32)
    stats_ps = ps("b_stats", [128, 512], F32)
    mbrb = ps("b_mbrb", [128, 512], F32)
    mm = [ps("b_mmA", [128, 512], F32), ps("b_mmB", [128, 512], F32)]
    z_ps = ps("b_z_ps", [128, 512], F32)
    g_ps = ps("b_g_ps", [128, 512], F32)
    gt_ps = ps("b_gt_ps", [128, 1024], BF16)
    ge_ps = ps("b_ge_ps", [128, 512], F32)

    gz_sig = p.op("vector", lambda e: e.memset(gtm[:], 0.0))

    for (dst, src, nkc, fdim) in ((w1s, w1, KD, H), (w2s, w2, MH, D)):
        for kc in range(nkc):
            p.op("gpsimd", lambda e, dst=dst, src=src, kc=kc, fdim=fdim:
                 e.dma_start(out=dst[:, kc * fdim:(kc + 1) * fdim],
                             in_=src[kc * 128:(kc + 1) * 128, :]),
                 incs=[(ld_w, 16)], dma=True)
            X.bump("ld_w")
    for (dst, src) in ((a1s, a1h), (b1cs, b1c), (r1s, r1h), (a2s, a2h),
                       (b2cs, b2c), (r2s, r2h), (b1s, b1t), (b2s, b2t)):
        p.op("gpsimd", lambda e, dst=dst, src=src:
             e.dma_start(out=dst[:], in_=src),
             incs=[(ld_w, 16)], dma=True)
        X.bump("ld_w")
    WRDY = X.val("ld_w")

    state = {"pmm": 0, "nst": 0}
    ev_sigs = {}
    add_sigs = []

    def mole_layer(tg, src_ready, src_sb, nkc, acat, rcat, bcat, wmain, fdim,
                   nm, bias_sb, is_last, hg_free):
        for kc in range(nkc):
            wts = []
            if kc == 0:
                wts = [src_ready]
                if tg == 0 and src_sb is xln:
                    wts.append((ld_w, WRDY))
            p.op("tensor", lambda e, kc=kc: e.matmul(
                    z_ps[0:24, 0:W], acat[:, kc * 24:(kc + 1) * 24],
                    src_sb[:, kc * W:(kc + 1) * W],
                    start=(kc == 0), stop=(kc == nkc - 1)),
                 waits=wts)
        for i in range(2):
            for kc in range(nkc):
                p.op("tensor", lambda e, i=i, kc=kc: e.matmul(
                        g_ps[:, i * 4:i * 4 + 3],
                        src_sb[:, kc * W + i * 128:kc * W + (i + 1) * 128],
                        rcat[:, kc * 3:(kc + 1) * 3],
                        start=(kc == 0), stop=(kc == nkc - 1)),
                     incs=([(s_mm, 1)] if (i == 1 and kc == nkc - 1) else []))
        mmc["n"] += 1
        zg_w = [(s_mm, mmc["n"])]
        zev = p.op("vector", lambda e: e.tensor_copy(z_sb[:], z_ps[0:24, 0:W]),
                   waits=zg_w)
        p.op("vector", lambda e: e.tensor_reduce(negm[:, 0:1], g_ps[:, 0:3],
                                                 AX.X, ALU.max, negate=True))
        nm_sig = p.op("vector", lambda e: e.tensor_reduce(
                negm[:, 1:2], g_ps[:, 4:7], AX.X, ALU.max, negate=True))
        p.op("scalar", lambda e: e.activation(gex[:, 0:3], g_ps[:, 0:3],
                                              AF.Exp, bias=negm[:, 0:1]),
             waits=[nm_sig])
        ex_sig = p.op("scalar", lambda e: e.activation(
                gex[:, 4:7], g_ps[:, 4:7], AF.Exp, bias=negm[:, 1:2]))
        p.op("vector", lambda e: e.tensor_reduce(
                srec[:, 0:1], gex[:, 0:3], AX.X, ALU.add), waits=[ex_sig])
        p.op("vector", lambda e: e.tensor_reduce(
                srec[:, 1:2], gex[:, 4:7], AX.X, ALU.add))
        p.op("vector", lambda e: e.reciprocal(srec[:, 2:4], srec[:, 0:2]))
        gt_sig = None
        for i in range(2):
            gt_sig = p.op("vector", lambda e, i=i: e.tensor_scalar(
                    gtm[:, i * 8:i * 8 + 3], gex[:, i * 4:i * 4 + 3],
                    srec[:, 2 + i:3 + i], None, ALU.mult),
                 waits=([gz_sig] if (tg == 0 and i == 0) else ()))
        for i in range(2):
            p.op("tensor", lambda e, i=i: e.transpose(
                    gt_ps[0:8, i * 128:(i + 1) * 128],
                    gtm[:, i * 8:(i + 1) * 8], c.ident_bf),
                 waits=([gt_sig] if i == 0 else ()),
                 incs=([(s_mm, 1)] if i == 1 else []))
        mmc["n"] += 1
        gf_sig = p.op("vector", lambda e: e.tensor_copy(
                gfm[:], gt_ps[0:3, 0:W]), waits=[(s_mm, mmc["n"])])
        p.op("tensor", lambda e: e.matmul(ge_ps[0:24, 0:W], e24sb[:], gfm[:],
                                          start=True, stop=True),
             waits=[gf_sig], incs=[(s_mm, 1)])
        mmc["n"] += 1
        zg_sig = p.op("vector", lambda e: e.scalar_tensor_tensor(
                zg_sb[:], ge_ps[0:24, 0:W], 0.125, z_sb[:],
                ALU.mult, ALU.mult),
             waits=[(s_mm, mmc["n"]), zev])
        last_ep = None
        for m in range(nm):
            par = state["pmm"] % 2
            for kc in range(nkc):
                wts = []
                if kc == 0:
                    wts = [zg_sig, ev_sigs.get(par)]
                    if m == 0:
                        wts.append(hg_free)
                p.op("tensor", lambda e, par=par, m=m, kc=kc: e.matmul(
                        mm[par][:, 0:W],
                        wmain.rearrange("p (kc f) -> p kc f", f=fdim)
                             [:, kc, m * 128:(m + 1) * 128],
                        src_sb[:, kc * W:(kc + 1) * W],
                        start=(kc == 0), stop=False),
                     waits=wts)
            p.op("tensor", lambda e, par=par, m=m: e.matmul(
                    mm[par][:, 0:W], bcat[:, m * 128:(m + 1) * 128], zg_sb[:],
                    start=False, stop=True),
                 incs=[(s_mm, 1)])
            mmc["n"] += 1
            state["pmm"] += 1
            if not is_last:
                ev_sigs[par] = p.op("scalar", lambda e, par=par, m=m:
                     e.activation(hg[:, m * W:(m + 1) * W], mm[par][:, 0:W],
                                  AF.Relu, bias=bias_sb[:, m:m + 1]),
                     waits=[(s_mm, mmc["n"])])
                last_ep = ev_sigs[par]
            else:
                nst = state["nst"]
                epar = nst % 2
                tep_guard = [add_sigs[nst - 2]] if nst >= 2 else []
                ev_sigs[par] = p.op("scalar", lambda e, par=par, epar=epar,
                                    m=m:
                     e.activation(tep[:, epar * W:(epar + 1) * W],
                                  mm[par][:, 0:W], AF.Identity,
                                  bias=bias_sb[:, m:m + 1]),
                     waits=[(s_mm, mmc["n"])] + tep_guard)
                asig = p.op("vector", lambda e, epar=epar, m=m: e.tensor_add(
                        yt[:, epar * W:(epar + 1) * W],
                        tep[:, epar * W:(epar + 1) * W],
                        xg[:, m * W:(m + 1) * W]),
                     waits=[ev_sigs[par], (s_st[epar], 16 * nstp[epar])])
                add_sigs.append(asig)
                p.op("sync", lambda e, epar=epar, m=m, tg=tg: e.dma_start(
                        out=y[m * 128:(m + 1) * 128,
                              y_off + tg * W:y_off + (tg + 1) * W],
                        in_=yt[:, epar * W:(epar + 1) * W]),
                     waits=[asig], incs=[(s_st[epar], 16)])
                nstp[epar] += 1
                state["nst"] += 1
                last_ep = asig
        return last_ep

    ln_sig = None
    hg_free = None
    for tg in range(NG2):
        t0 = tg * W
        xg_guard = add_sigs[-1] if add_sigs else None
        p.op("sync", lambda e, t0=t0:
             e.dma_start(out=xg.rearrange("p (c t) -> p c t", c=KD),
                         in_=x[:, t0:t0 + W]
                             .rearrange("(c p) t -> c p t", p=128)
                             .rearrange("c p t -> p c t")),
             waits=[xg_guard, ln_sig], incs=[(ldx, 16)])
        nldx = X.bump("ldx")
        ln_sig = emit_ln_fm(p, c, xg, xsq, xln, stats_ps, mbrb[:, 0:W],
                            mbrb[:, W:2 * W], rowbuf, scr, s_mm, mmc, W,
                            pre_waits=[(ldx, nldx)])
        evA = mole_layer(tg, ln_sig, xln, KD, a1s, r1s, b1cs, w1s, H, MH,
                         b1s, False, hg_free)
        hg_free = mole_layer(tg, evA, hg, MH, a2s, r2s, b2cs, w2s, D, KD,
                             b2s, True, None)


def emit_l1ffn(X, st, x, w1, b1t, w2, b2t, y):
    nc, p, c = X.nc, X.p, X.c
    H = 4096
    MH = 32
    NG2 = 8
    W = 256
    ld_slab = X.S("ld_slab")
    ld_b = X.S("ld_b")
    ldx = X.S("ldx")
    s_tq = X.S("s_tq")
    s_mm = X.S("s_mm")
    s_st = [X.S("s_st0"), X.S("s_st1")]
    nstp = X.nstp
    mmc, tqc, ldc = X.mmc, X.tqc, X.ldc

    sb = lambda name, shp, dt: st.enter_context(nc.sbuf_tensor(name, shp, dt))
    ps = lambda name, shp, dt: st.enter_context(nc.psum_tensor(name, shp, dt))
    w1T = sb("f_w1T", [128, KD * H], BF16)
    w2T = sb("f_w2T", [128, MH * D], BF16)
    stag = sb("f_stag", [128, 4096], F32)
    sgn = sb("f_sgn", [128, 4096], BF16)
    alpha1 = sb("f_alpha1", [128, 32], F32)
    alpha2 = sb("f_alpha2", [128, 8], F32)
    b1s = sb("f_b1s", [128, 32], F32)
    b2s = sb("f_b2s", [128, 8], F32)
    xg = sb("f_xg", [128, KD * W], F32)
    xsq = sb("f_xsq", [128, KD * W], F32)
    xln = sb("f_xln", [128, KD * W], BF16)
    hg = sb("f_hg", [128, MH * W], BF16)
    scr = sb("f_scr", [128, W], F32)
    tep = sb("f_tep", [128, 2 * W], F32)
    yt = sb("f_yt", [128, 2 * W], F32)
    rowbuf = sb("f_rowbuf", [1, 3 * W], F32)
    pq = ps("f_pq0", [128, 1024], BF16)
    stats_ps = ps("f_stats", [128, 512], F32)
    mbrb = ps("f_mbrb", [128, 512], F32)
    mm = [ps("f_mmA", [128, 512], F32), ps("f_mmB", [128, 512], F32)]

    LDB = X.val("ld_b")
    p.op("sync", lambda e: e.dma_start(out=b1s[:], in_=b1t), incs=[(ld_b, 16)])
    p.op("sync", lambda e: e.dma_start(out=b2s[:], in_=b2t), incs=[(ld_b, 16)])
    X.bump("ld_b", 32)
    LDB += 32

    specs = [(w1, 32, D, 8, H, w1T, alpha1, True),
             (w2, 8, 4096, 32, D, w2T, alpha2, True)]
    wp_sig, alpha_sigs = emit_wprep_transposed(
        p, c, specs, stag, sgn, pq, ld_slab, ldc, s_tq, tqc)
    p.op("vector", lambda e: e.tensor_scalar(alpha1[:], alpha1[:], 1.0 / D,
                                             None, ALU.mult),
         waits=[(ld_b, LDB), alpha_sigs[-1]])
    al_sig = p.op("vector", lambda e: e.tensor_scalar(
            alpha2[:], alpha2[:], 1.0 / 4096.0, None, ALU.mult))

    ev_sigs = {}
    add_sigs = []
    ln_sig = None
    hg_free = None
    pmm = 0
    nst = 0
    for tg in range(NG2):
        t0 = tg * W
        xg_guard = add_sigs[-1] if add_sigs else None
        p.op("sync", lambda e, t0=t0:
             e.dma_start(out=xg.rearrange("p (c t) -> p c t", c=KD),
                         in_=x[:, t0:t0 + W]
                             .rearrange("(c p) t -> c p t", p=128)
                             .rearrange("c p t -> p c t")),
             waits=[xg_guard, ln_sig], incs=[(ldx, 16)])
        nldx = X.bump("ldx")
        ln_sig = emit_ln_fm(p, c, xg, xsq, xln, stats_ps, mbrb[:, 0:W],
                            mbrb[:, W:2 * W], rowbuf, scr, s_mm, mmc, W,
                            pre_waits=[(ldx, nldx)])
        last_gelu = None
        for m in range(MH):
            par = pmm % 2
            for kc in range(KD):
                wts = []
                if kc == 0:
                    wts = [ln_sig, ev_sigs.get(par)]
                    if tg == 0 and m == 0:
                        wts.append(wp_sig)
                    if m == 0:
                        wts.append(hg_free)
                p.op("tensor", lambda e, par=par, m=m, kc=kc:
                     e.matmul(mm[par][:, 0:W],
                              w1T.rearrange("p (kc f) -> p kc f", f=H)
                                 [:, kc, m * 128:(m + 1) * 128],
                              xln[:, kc * W:(kc + 1) * W],
                              start=(kc == 0), stop=(kc == KD - 1)),
                     waits=wts, incs=([(s_mm, 1)] if kc == KD - 1 else []))
            mmc["n"] += 1
            pmm += 1
            ev_sigs[par] = p.op("scalar", lambda e, par=par, m=m:
                 e.activation(hg[:, m * W:(m + 1) * W], mm[par][:, 0:W],
                              AF.Gelu, bias=b1s[:, m:m + 1],
                              scale=alpha1[:, m:m + 1]),
                 waits=[(s_mm, mmc["n"])] +
                       ([al_sig] if (tg == 0 and m == 0) else []))
            last_gelu = ev_sigs[par]
        for md in range(KD):
            par = pmm % 2
            for kc in range(MH):
                wts = []
                if kc == 0:
                    wts = [ev_sigs.get(par)]
                    if md == 0:
                        wts.append(last_gelu)
                p.op("tensor", lambda e, par=par, md=md, kc=kc:
                     e.matmul(mm[par][:, 0:W],
                              w2T.rearrange("p (kc f) -> p kc f", f=D)
                                 [:, kc, md * 128:(md + 1) * 128],
                              hg[:, kc * W:(kc + 1) * W],
                              start=(kc == 0), stop=(kc == MH - 1)),
                     waits=wts, incs=([(s_mm, 1)] if kc == MH - 1 else []))
            mmc["n"] += 1
            pmm += 1
            epar = nst % 2
            tep_guard = [add_sigs[nst - 2]] if nst >= 2 else []
            ev_sigs[par] = p.op("scalar", lambda e, par=par, epar=epar, md=md:
                 e.activation(tep[:, epar * W:(epar + 1) * W],
                              mm[par][:, 0:W], AF.Identity,
                              bias=b2s[:, md:md + 1],
                              scale=alpha2[:, md:md + 1]),
                 waits=[(s_mm, mmc["n"])] + tep_guard)
            asig = p.op("vector", lambda e, epar=epar, md=md: e.tensor_add(
                    yt[:, epar * W:(epar + 1) * W],
                    tep[:, epar * W:(epar + 1) * W],
                    xg[:, md * W:(md + 1) * W]),
                 waits=[ev_sigs[par], (s_st[epar], 16 * nstp[epar])])
            add_sigs.append(asig)
            p.op("sync", lambda e, epar=epar, md=md, t0=t0: e.dma_start(
                    out=y[md * 128:(md + 1) * 128, t0:t0 + W],
                    in_=yt[:, epar * W:(epar + 1) * W]),
                 waits=[asig], incs=[(s_st[epar], 16)])
            nstp[epar] += 1
            nst += 1
            hg_free = asig


EXT3 = 2304


def emit_l1front(X, st, x, wqkv, convs, wsm, ccq_in, ccs2_in):
    nc, p, c = X.nc, X.p, X.c
    PW = 384
    NPG = 6
    ld_w = X.S("ld_w")
    ld_w2 = X.S("ld_w2")
    ldx = X.S("ldx")
    s_mm = X.S("s_mm")
    s_tq = X.S("s_tq")
    s_st = X.S("s_st0")
    nstp = X.nstp
    mmc, tqc = X.mmc, X.tqc
    cbase = nstp[0]

    sb = lambda name, shp, dt: st.enter_context(nc.sbuf_tensor(name, shp, dt))
    ps = lambda name, shp, dt: st.enter_context(nc.psum_tensor(name, shp, dt))
    xln_ext = sb("c_xln_ext", [128, KD * EXT3], BF16)
    w_sb = sb("c_w_sb", [128, KD * D], BF16)
    conv_s = sb("c_conv_s", [128, 8 * 12], F32)
    wsm_s = sb("c_wsm_s", [128, 8 * 96], BF16)
    praw = sb("c_praw", [128, 2 * EXT3], BF16)
    cacc_v = sb("c_cacc_v", [128, TOK], F32)
    pout1 = sb("c_pout1", [128, KD * TOK], BF16)
    smq_b = sb("c_smq_b", [8, 512], F32)
    smq_g = sb("c_smq_g", [4, 512], F32)
    smq_h = sb("c_smq_h", [4, 512], F32)
    smtm = sb("c_smtm", [128, 64], F32)
    xg = st.enter_context(nc.sbuf_tensor("c_xg", [128, KD * PW], F32,
                                         side="right"))
    xsq = st.enter_context(nc.sbuf_tensor("c_xsq", [128, KD * PW], F32,
                                          side="right"))
    scr = sb("c_scr", [128, 512], F32)
    rowbuf = sb("c_rowbuf", [1, 3 * PW], F32)
    stats_ps = ps("c_stats", [128, 512], F32)
    mb_ps = ps("c_mb_ps", [128, 512], F32)
    rb_ps = ps("c_rb_ps", [128, 512], F32)
    mm = [ps("c_mmA", [128, 512], F32), ps("c_mmB", [128, 512], F32)]
    sm_ps = ps("c_sm_ps", [128, 512], F32)
    sm_tp = ps("c_sm_tp", [128, 64], F32)

    p.op("gpsimd", lambda e: e.dma_start(out=conv_s[:], in_=convs),
         incs=[(ld_w, 16)], dma=True)
    X.bump("ld_w")
    for kcx in range(KD):
        p.op("gpsimd", lambda e, kcx=kcx: e.dma_start(
                out=wsm_s[:, kcx * 96:(kcx + 1) * 96],
                in_=wsm[kcx * 128:(kcx + 1) * 128, :]),
             incs=[(ld_w, 16)], dma=True)
        X.bump("ld_w")
    WRDY0 = X.val("ld_w")

    ln_sig = None
    for g in range(NPG):
        t0 = g * PW
        p.op("sync", lambda e, t0=t0:
             e.dma_start(out=xg.rearrange("p (c t) -> p c t", c=KD),
                         in_=x[:, t0:t0 + PW]
                             .rearrange("(c p) t -> c p t", p=128)
                             .rearrange("c p t -> p c t")),
             waits=[ln_sig], incs=[(ldx, 16)])
        nldx = X.bump("ldx")
        ln_sig = emit_ln_fm(p, c, xg, xsq, xln_ext, stats_ps, mb_ps, rb_ps,
                            rowbuf, scr, s_mm, mmc, PW,
                            pre_waits=[(ldx, nldx)],
                            xln_stride=EXT3, xln_off=t0)

    smfree = None
    smev = None
    for g4 in range(4):
        t0 = 3 + g4 * 512
        for kcx in range(KD):
            wts = []
            if kcx == 0:
                wts = [ln_sig, (ld_w, WRDY0), smfree]
            p.op("tensor", lambda e, kcx=kcx, t0=t0: e.matmul(
                    sm_ps[0:96, 0:512],
                    wsm_s[:, kcx * 96:(kcx + 1) * 96],
                    xln_ext[:, kcx * EXT3 + t0:kcx * EXT3 + t0 + 512],
                    start=(kcx == 0), stop=(kcx == KD - 1)),
                 waits=wts, incs=([(s_mm, 1)] if kcx == KD - 1 else []))
        mmc["n"] += 1
        p.op("scalar", lambda e: e.activation(
                smq_b[0:8, 0:512], sm_ps[0:8, 0:512], AF.Sigmoid),
             waits=[(s_mm, mmc["n"])])
        p.op("scalar", lambda e: e.copy(smq_g[0:4, 0:512],
                                        sm_ps[32:36, 0:512]))
        smfree = p.op("scalar", lambda e: e.copy(smq_h[0:4, 0:512],
                                                 sm_ps[64:68, 0:512]))
        nt = 0
        for tt in range(4):
            for (src, nq, qo) in ((smq_b, 8, 0), (smq_g, 4, 8), (smq_h, 4, 12)):
                nt += 1
                p.op("tensor", lambda e, tt=tt, src=src, nq=nq, qo=qo:
                     e.transpose(
                        sm_tp[:, tt * 16 + qo:tt * 16 + qo + nq],
                        src[0:nq, tt * 128:(tt + 1) * 128],
                        c.ident_f32[0:nq, 0:nq]),
                     waits=([smfree, smev] if nt == 1 else ()),
                     incs=([(s_tq, 1)] if nt == 12 else []))
        tqc["n"] += 1
        smev = p.op("vector", lambda e: e.tensor_copy(smtm[:, 0:64],
                                                      sm_tp[:, 0:64]),
                    waits=[(s_tq, tqc["n"]), (s_st, 16 * nstp[0])])
        p.op("sync", lambda e, g4=g4: e.dma_start(
                out=ccs2_in[g4 * 512:(g4 + 1) * 512, :]
                    .rearrange("(tt pp) q -> pp tt q", pp=128),
                in_=smtm[:, 0:64].rearrange("p (tt q) -> p tt q", q=16)),
             waits=[smev], incs=[(s_st, 16)])
        nstp[0] += 1

    pmm = 0
    ev_sigs = {}
    praw_free = {}
    sl_last = {}
    for ti, whbm in enumerate(wqkv):
        wguard = ev_sigs.get("wlast")
        for kcx in range(KD):
            p.op("gpsimd", lambda e, kcx=kcx, whbm=whbm: e.dma_start(
                    out=w_sb[:, kcx * D:(kcx + 1) * D],
                    in_=whbm[kcx * 128:(kcx + 1) * 128, :]),
                 waits=([wguard] if (kcx == 0 and wguard is not None) else []),
                 incs=[(ld_w2, 16)], dma=True)
            X.bump("ld_w2")
        wld = X.val("ld_w2")
        for mc in range(KD):
            par = pmm % 2
            for g in range(NPG):
                for kcx in range(KD):
                    wts = []
                    if kcx == 0:
                        wts = [(ld_w2, wld), ln_sig, ev_sigs.get(par)]
                    p.op("tensor", lambda e, par=par, mc=mc, g=g, kcx=kcx:
                         e.matmul(mm[par][:, 0:PW],
                                  w_sb.rearrange("p (kc f) -> p kc f", f=D)
                                      [:, kcx, mc * 128:(mc + 1) * 128],
                                  xln_ext[:, kcx * EXT3 + g * PW:
                                          kcx * EXT3 + (g + 1) * PW],
                                  start=(kcx == 0), stop=(kcx == KD - 1)),
                         waits=wts,
                         incs=([(s_mm, 1)] if kcx == KD - 1 else []))
                mmc["n"] += 1
                ppar = mc % 2
                ev_sigs[par] = p.op("scalar",
                     lambda e, par=par, ppar=ppar, g=g:
                     e.activation(praw[:, ppar * EXT3 + g * PW:
                                       ppar * EXT3 + (g + 1) * PW],
                                  mm[par][:, 0:PW], AF.Copy),
                     waits=[(s_mm, mmc["n"])] +
                           ([praw_free.get(mc % 2)] if g == 0 else []))
                pmm += 1
                par = pmm % 2
            ev_sigs["wlast"] = ev_sigs[(pmm - 1) % 2]
            cacc = cacc_v
            ppar = mc % 2
            pr_sig = ev_sigs[(pmm - 1) % 2]
            p.op("vector", lambda e, ppar=ppar, mc=mc, ti=ti, cacc=cacc:
                 e.tensor_scalar(
                    cacc[:], praw[:, ppar * EXT3 + 0:ppar * EXT3 + TOK],
                    conv_s[:, mc * 12 + ti * 4:mc * 12 + ti * 4 + 1],
                    None, ALU.mult),
                 waits=[pr_sig, sl_last.get("vector")])
            csig = None
            for j in (1, 2, 3):
                csig = p.op("vector", lambda e, ppar=ppar, mc=mc, ti=ti, j=j,
                            cacc=cacc:
                     e.scalar_tensor_tensor(
                         cacc[:], praw[:, ppar * EXT3 + j:ppar * EXT3 + j + TOK],
                         conv_s[:, mc * 12 + ti * 4 + j:
                                mc * 12 + ti * 4 + j + 1],
                         cacc[:], ALU.mult, ALU.add))
            stguard = (s_st, 16 * (cbase + 4 + ti * 8)) if ti > 0 else None
            sl_sig = p.op("scalar", lambda e, mc=mc, cacc=cacc:
                 e.activation(
                    pout1[:, mc * TOK:(mc + 1) * TOK], cacc[:], AF.Silu),
                 waits=[csig, stguard])
            sl_last["vector"] = sl_sig
            praw_free[mc % 2] = csig
            p.op("sync", lambda e, mc=mc, ti=ti: e.dma_start(
                    out=ccq_in[ti * 1024 + mc * 128:
                               ti * 1024 + (mc + 1) * 128, :],
                    in_=pout1[:, mc * TOK:(mc + 1) * TOK]),
                 waits=[sl_sig], incs=[(s_st, 16)])
            nstp[0] += 1


def emit_l1comb(X, st, cc2_out, ccq_in, ccs2_in, bgh, nw_in, wo, x, x_off, y):
    nc, p, c = X.nc, X.p, X.c
    ld_w = X.S("ld_w")
    ld_wh = X.S("ld_wh")
    ld_t2 = [X.S("ld_t0"), X.S("ld_t1")]
    nld_t = [0, 0]
    ldr = X.S("ldr")
    s_tq = X.S("s_tq")
    s_mm = X.S("s_mm")
    s_cc = X.S("s_cc")
    s_st = [X.S("s_st0"), X.S("s_st1")]
    nstp = X.nstp
    mmc, tqc = X.mmc, X.tqc
    mfl = c.extra["mflag"]

    sb = lambda name, shp, dt: st.enter_context(nc.sbuf_tensor(name, shp, dt))
    ps = lambda name, shp, dt: st.enter_context(nc.psum_tensor(name, shp, dt))
    wo_s = sb("e_wo_s", [128, KD * D], BF16)
    nw_s = sb("e_nw_s", [128, 1024], F32)
    bgh_s = sb("e_bgh_s", [128, 8], F32)
    otA = sb("e_otA", [128, 2048], BF16)
    otB = sb("e_otB", [128, 2048], BF16)
    ot = sb("e_ot", [128, 2048], F32)
    vstg = sb("e_vstg", [128, 2048], BF16)
    vt = sb("e_vt", [128, 2048], BF16)
    ght = sb("e_ght", [128, 16], F32)
    wrk = sb("e_wrk", [128, 48], F32)
    o2 = sb("e_o2", [128, 2048], F32)
    ssq = sb("e_ssq", [128, 16], F32)
    o3 = sb("e_o3", [128, 2048], BF16)
    ofm = sb("e_ofm", [128, KD * TOK], BF16)
    yt = sb("e_yt", [128, 2 * 512], F32)
    xr = sb("e_xr", [128, 2 * 512], F32)
    mm = [ps("e_mmA", [128, 512], F32), ps("e_mmB", [128, 512], F32)]
    pq = ps("e_pqA", [128, 1024], BF16)
    pqv = ps("e_pqV", [128, 1024], BF16)

    LW0 = X.val("ld_w")
    for kcx in range(KD):
        p.op("gpsimd", lambda e, kcx=kcx: e.dma_start(
                out=wo_s[:, kcx * D:(kcx + 1) * D],
                in_=wo[kcx * 128:(kcx + 1) * 128, :]),
             incs=[(ld_w, 16)], dma=True)
        X.bump("ld_w")
    WRDY = X.val("ld_w")
    LWH = X.val("ld_wh")
    p.op("sync", lambda e: e.dma_start(out=nw_s[:], in_=nw_in),
         incs=[(ld_wh, 16)])
    p.op("sync", lambda e: e.dma_start(out=bgh_s[:], in_=bgh),
         incs=[(ld_wh, 16)])
    X.bump("ld_wh", 32)
    LWH += 32

    ev_sig = None
    tile_sigs = {}
    vt_sigs = {}
    o2_sigs = {}
    for i in range(16):
        tguard = tile_sigs.get(i - 2)
        first = (i == 0)
        TP = 1024 * (i % 2)
        SP = 24 * (i % 2)
        QP = 8 * (i % 2)
        GP = 8 * (i % 2)
        ld_t = ld_t2[i % 2]
        for r in range(2):
            for h2 in range(2):
                for (buf, goff) in ((otA, 0), (otB, 16)):
                    wts = [tguard] if (r == 0 and h2 == 0 and goff == 0) else []
                    if first and r == 0 and h2 == 0 and goff == 0:
                        wts.append((s_cc, 4))
                    p.op("sync", lambda e, r=r, h2=h2, buf=buf, goff=goff, i=i,
                         TP=TP:
                         e.dma_start(
                            out=buf[:, TP + (2 * r + h2) * 256:
                                    TP + (2 * r + h2 + 1) * 256],
                            in_=cc2_out[r * 8192 + h2 * 4096 + (goff + i) * 128:
                                        r * 8192 + h2 * 4096 + (goff + i + 1) * 128,
                                        :]),
                         waits=wts, incs=[(ld_t, 16)])
                    nld_t[i % 2] += 16
        p.op("sync", lambda e, i=i, TP=TP: e.dma_start(
                out=vstg[:, TP:TP + 1024].rearrange("p (cx t) -> p cx t",
                                                    cx=KD),
                in_=ccq_in[2048:3072, i * 128:(i + 1) * 128]
                    .rearrange("(cx p) t -> cx p t", p=128)
                    .rearrange("cx p t -> p cx t")),
             incs=[(ld_t, 16)])
        nld_t[i % 2] += 16
        p.op("sync", lambda e, i=i, GP=GP: e.dma_start(
                out=ght[:, GP:GP + 8],
                in_=ccs2_in[i * 128:(i + 1) * 128, 8:16]),
             incs=[(ld_t, 16)])
        nld_t[i % 2] += 16
        nldt = nld_t[i % 2]
        ldw = [(ld_t, nldt), (ld_w, WRDY), (ld_wh, LWH)]
        p.op("vector", lambda e, TP=TP: e.tensor_scalar(
                ot[:, TP:TP + 1024], otB[:, TP:TP + 1024], mfl[:, 1:2],
                None, ALU.mult), waits=ldw)
        blend_sig = p.op("vector", lambda e, TP=TP: e.scalar_tensor_tensor(
                ot[:, TP:TP + 1024], otA[:, TP:TP + 1024], mfl[:, 0:1],
                ot[:, TP:TP + 1024], ALU.mult, ALU.add))
        for jj in range(8):
            wts = []
            if jj == 0:
                wts = [vt_sigs.get(i - 1), (ld_t, nldt)]
            p.op("tensor", lambda e, jj=jj, TP=TP: e.transpose(
                    pqv[:, jj * 128:(jj + 1) * 128],
                    vstg[:, TP + jj * 128:TP + (jj + 1) * 128], c.ident_bf),
                 waits=wts, incs=([(s_tq, 1)] if jj == 7 else []))
        tqc["n"] += 1
        vt_sig = p.op("scalar", lambda e, TP=TP: e.activation(
                vt[:, TP:TP + 1024], pqv[:, 0:1024], AF.Copy),
             waits=[(s_tq, tqc["n"]), o2_sigs.get(i - 2)])
        vt_sigs[i] = vt_sig
        p.op("vector", lambda e, SP=SP, GP=GP: e.tensor_add(
                wrk[:, SP:SP + 4], ght[:, GP:GP + 4],
                bgh_s[:, 0:4]), waits=ldw)
        p.op("vector", lambda e, SP=SP, GP=GP: e.tensor_add(
                wrk[:, SP + 4:SP + 8], ght[:, GP + 4:GP + 8],
                bgh_s[:, 4:8]))
        p.op("vector", lambda e, SP=SP: e.tensor_max(
                wrk[:, SP + 8:SP + 12], wrk[:, SP:SP + 4],
                wrk[:, SP + 4:SP + 8]))
        p.op("vector", lambda e, SP=SP: e.tensor_scalar(
                wrk[:, SP + 8:SP + 12], wrk[:, SP + 8:SP + 12],
                0.0, None, ALU.max))
        p.op("vector", lambda e, SP=SP: e.tensor_sub(
                wrk[:, SP:SP + 4], wrk[:, SP:SP + 4],
                wrk[:, SP + 8:SP + 12]))
        vsb = p.op("vector", lambda e, SP=SP: e.tensor_sub(
                wrk[:, SP + 4:SP + 8], wrk[:, SP + 4:SP + 8],
                wrk[:, SP + 8:SP + 12]))
        p.op("scalar", lambda e, SP=SP: e.activation(
                wrk[:, SP + 12:SP + 16], wrk[:, SP:SP + 4],
                AF.Exp), waits=[vsb])
        p.op("scalar", lambda e, SP=SP: e.activation(
                wrk[:, SP + 16:SP + 20], wrk[:, SP + 4:SP + 8],
                AF.Exp))
        es = p.op("scalar", lambda e, SP=SP: e.activation(
                wrk[:, SP + 20:SP + 24], wrk[:, SP + 8:SP + 12],
                AF.Exp, scale=-1.0))
        p.op("vector", lambda e, SP=SP: e.tensor_add(
                wrk[:, SP + 12:SP + 16], wrk[:, SP + 12:SP + 16],
                wrk[:, SP + 16:SP + 20]), waits=[es])
        p.op("vector", lambda e, SP=SP: e.tensor_add(
                wrk[:, SP + 12:SP + 16], wrk[:, SP + 12:SP + 16],
                wrk[:, SP + 20:SP + 24]))
        rcp = p.op("vector", lambda e, SP=SP: e.reciprocal(
                wrk[:, SP + 16:SP + 20], wrk[:, SP + 12:SP + 16]))
        wr = p.op("vector", lambda e, SP=SP: e.tensor_mul(
                wrk[:, SP + 20:SP + 24], wrk[:, SP + 20:SP + 24],
                wrk[:, SP + 16:SP + 20]),
                  waits=[rcp])
        o2s = None
        for h in range(4):
            o2s = p.op("vector", lambda e, h=h, TP=TP, SP=SP:
                 e.scalar_tensor_tensor(
                    o2[:, TP + h * 256:TP + (h + 1) * 256],
                    vt[:, TP + h * 256:TP + (h + 1) * 256],
                    wrk[:, SP + 20 + h:SP + 21 + h],
                    ot[:, TP + h * 256:TP + (h + 1) * 256],
                    ALU.mult, ALU.add),
                 waits=([wr, blend_sig, vt_sig] if h == 0 else ()))
        sqs = p.op("vector", lambda e, TP=TP: e.tensor_mul(
                yt[:, 0:1024], o2[:, TP:TP + 1024], o2[:, TP:TP + 1024]),
                   waits=[o2s, (s_st[0], 16 * nstp[0]),
                          (s_st[1], 16 * nstp[1])] if i == 0 else [o2s])
        p.op("vector", lambda e, QP=QP: e.tensor_reduce(
                ssq[:, QP:QP + 4],
                yt[:, 0:1024].rearrange("p (a t) -> p a t", a=4),
                AX.X, ALU.add))
        tss = p.op("vector", lambda e, QP=QP: e.tensor_scalar(
                ssq[:, QP:QP + 4], ssq[:, QP:QP + 4], 1.0 / 256.0,
                EPS, ALU.mult, ALU.add))
        lns = p.op("scalar", lambda e, QP=QP: e.activation(
                ssq[:, QP + 4:QP + 8], ssq[:, QP:QP + 4],
                AF.Ln), waits=[tss])
        rst = p.op("scalar", lambda e, QP=QP: e.activation(
                ssq[:, QP + 4:QP + 8], ssq[:, QP + 4:QP + 8],
                AF.Exp, scale=-0.5),
                   waits=[lns])
        for h in range(4):
            p.op("vector", lambda e, h=h, TP=TP, QP=QP: e.tensor_scalar(
                    o2[:, TP + h * 256:TP + (h + 1) * 256],
                    o2[:, TP + h * 256:TP + (h + 1) * 256],
                    ssq[:, QP + 4 + h:QP + 5 + h], None, ALU.mult),
                 waits=([rst] if h == 0 else ()))
        o3s = p.op("vector", lambda e, TP=TP: e.tensor_mul(
                o3[:, TP:TP + 1024], o2[:, TP:TP + 1024], nw_s[:]))
        o2_sigs[i] = o3s
        for quad in range(2):
            for jj in range(4):
                cx = quad * 4 + jj
                wts = []
                if jj == 0:
                    wts = [o3s, ev_sig]
                p.op("tensor", lambda e, cx=cx, jj=jj, TP=TP: e.transpose(
                        pq[:, jj * 128:(jj + 1) * 128],
                        o3[:, TP + cx * 128:TP + (cx + 1) * 128], c.ident_bf),
                     waits=wts, incs=([(s_tq, 1)] if jj == 3 else []))
            tqc["n"] += 1
            ev_sig = p.op("vector", lambda e, quad=quad, i=i: e.tensor_copy(
                    ofm.rearrange("p (cx t) -> p cx t", cx=KD)
                       [:, quad * 4:(quad + 1) * 4, i * 128:(i + 1) * 128],
                    pq[:, 0:512].rearrange("p (a f) -> p a f", a=4)),
                 waits=[(s_tq, tqc["n"])])
        tile_sigs[i] = ev_sig

    EW = 512
    nst = 0
    est = {}
    pmm = 0
    for g in range(4):
        for mc in range(KD):
            par = pmm % 2
            p.op("sync", lambda e, mc=mc, g=g, par=par:
                 e.dma_start(out=xr[:, par * EW:(par + 1) * EW],
                             in_=x[mc * 128:(mc + 1) * 128,
                                   x_off + g * EW:x_off + (g + 1) * EW]),
                 waits=[est.get(par)], incs=[(ldr, 16)])
            nldr = X.bump("ldr")
            for kcx in range(KD):
                wts = []
                if kcx == 0:
                    wts = [ev_sig, est.get(par), (ld_w, WRDY), (ld_wh, LWH)]
                p.op("tensor", lambda e, par=par, mc=mc, g=g, kcx=kcx:
                     e.matmul(mm[par][:, 0:EW],
                              wo_s.rearrange("p (kc f) -> p kc f", f=D)
                                  [:, kcx, mc * 128:(mc + 1) * 128],
                              ofm[:, kcx * TOK + g * EW:
                                  kcx * TOK + (g + 1) * EW],
                              start=(kcx == 0), stop=(kcx == KD - 1)),
                     waits=wts,
                     incs=([(s_mm, 1)] if kcx == KD - 1 else []))
            mmc["n"] += 1
            epar = nst % 2
            esig = p.op("vector", lambda e, par=par, epar=epar: e.tensor_add(
                    yt[:, epar * EW:(epar + 1) * EW], mm[par][:, 0:EW],
                    xr[:, par * EW:(par + 1) * EW]),
                 waits=[(s_mm, mmc["n"]), (ldr, nldr),
                        (s_st[epar], 16 * nstp[epar])])
            est[par] = esig
            pmm += 1
            p.op("sync", lambda e, epar=epar, mc=mc, g=g: e.dma_start(
                    out=y[mc * 128:(mc + 1) * 128, g * EW:(g + 1) * EW],
                    in_=yt[:, epar * EW:(epar + 1) * EW]),
                 waits=[esig], incs=[(s_st[epar], 16)])
            nstp[epar] += 1
            nst += 1


def emit_l1scan(X, st, ccq_out, ccs_out, msk_d, bgh_d, cc2_in, scr1, scr2):
    nc, p, c = X.nc, X.p, X.c
    L4L = 4096
    NG = 32
    CCq = 3072
    ld_in = X.S("ld_in")
    ld_sm = X.S("ld_sm")
    ld_sh = X.S("ld_sh")
    s_tq = X.S("s_tq")
    s_mm = X.S("s_mm")
    s_cc = X.S("s_cc")
    s_st = [X.S("s_st0"), X.S("s_st1")]
    nstp = X.nstp
    mmc, tqc = X.mmc, X.tqc
    mflc = c.extra["mflag"]
    selm_sb = c.extra["selm"]

    sb = lambda name, shp, dt: st.enter_context(nc.sbuf_tensor(name, shp, dt))
    qtm = sb("d_qtm", [128, NG * 256], BF16)
    ktm = sb("d_ktm", [128, NG * 256], BF16)
    vtm = sb("d_vtm", [128, NG * 256], BF16)
    q_fm = sb("d_q_fm", [128, 2 * L4L], BF16)
    k_fm = sb("d_k_fm", [128, 2 * L4L], BF16)
    TB = [sb("d_TB_l", [128, NG * 128], BF16), sb("d_TB_g", [128, NG * 128], BF16)]
    T0F = [sb("d_T0F_l", [128, 4096], F32), sb("d_T0F_g", [128, 4096], F32)]
    T_vec = sb("d_T_vec", [128, 1024], F32)
    T_vt = sb("d_T_vt", [128, 1024], F32)
    T_bfs = sb("d_T_bfs", [128, 1024], BF16)
    kkm_sb = sb("d_kkm_sb", [128, 128], F32)
    acc32 = sb("d_acc32", [128, 32], F32)
    blg = [sb("d_bl_s", [128, NG], F32), sb("d_bg_s", [128, NG], F32)]
    nblg = [sb("d_nbl_s", [128, NG], F32), sb("d_nbg_s", [128, NG], F32)]
    ggs = sb("d_ggs", [128, NG], F32)
    hhs = sb("d_hhs", [128, NG], F32)
    wlg = [sb("d_wl_s", [128, NG], F32), sb("d_wg_s", [128, NG], F32)]
    gwk = sb("d_gwk", [128, 8 * NG], F32)
    bghs = sb("d_bghs", [128, 4], F32)
    msk = sb("d_msk_s", [128, 416], F32)
    nrm = sb("d_nrm", [128, 2 * NG], F32)
    vs_sb = sb("d_vs_sb", [128, 512], BF16)
    kb_sb = sb("d_kb_sb", [128, 512], BF16)
    wm_sb = sb("d_wm_sb", [128, 512], BF16)
    wkm_sb = [sb("d_wkm_l", [128, 128], BF16), sb("d_wkm_g", [128, 128], BF16)]
    wkm3 = [sb("d_wkm3_l", [128, 128], BF16), sb("d_wkm3_g", [128, 128], BF16)]
    grm_sb = sb("d_grm_sb", [128, 128], BF16)
    Usb = [sb("d_U_l", [128, 256], BF16), sb("d_U_g", [128, 256], BF16)]
    S_bf = [sb("d_Sbf_l", [128, 512], BF16), sb("d_Sbf_g", [128, 512], BF16)]
    S_sb = [sb("d_Ssb_l", [128, 512], F32), sb("d_Ssb_g", [128, 512], F32)]
    o_stage = sb("d_o_stage", [128, 2 * 256], BF16)
    stgA = sb("d_stgA", [128, 2 * TOK], BF16)
    stgB = sb("d_stgB", [128, 2 * TOK], BF16)
    smfull = sb("d_smfull", [128, 512], F32)
    smT = sb("d_smT", [128, 32], F32)
    nld = {"n": 0}
    nsh = {"n": 0}

    MS_BD = msk[:, 32:160]
    MS_BL = msk[:, 160:288]
    MS_CA = msk[:, 288:416]
    nsm = {"n": 0}
    p.op("sync", lambda e: e.dma_start(out=msk[:], in_=msk_d),
         incs=[(ld_sm, 16)])
    nsm["n"] += 16
    p.op("sync", lambda e: e.dma_start(out=bghs[:], in_=bgh_d),
         incs=[(ld_sm, 16)])
    nsm["n"] += 16
    mz_sig = None
    for t in (Usb[0], Usb[1], S_bf[0], S_bf[1], wkm3[0], wkm3[1],
              TB[0], TB[1]):
        mz_sig = p.op("vector", lambda e, t=t: e.memset(t[:], 0.0))

    for r in range(2):
        for gl in range(16):
            wts = []
            if r == 0 and gl == 0:
                wts = [(s_cc, 3)]
            p.op("sync", lambda e, r=r, gl=gl: e.dma_start(
                    out=smfull[:, (r * 16 + gl) * 16:(r * 16 + gl + 1) * 16],
                    in_=ccs_out[r * TOK + gl * 128:r * TOK + (gl + 1) * 128,
                                0:16]),
                 waits=wts, incs=[(ld_sm, 16)])
            nsm["n"] += 16
    LDSM = nsm["n"]
    smv = smfull.rearrange("p (g q) -> p g q", q=16)

    head_free = None
    shuf_done = None
    conv_ev = None
    with nc.psum_tensor("d_pqc", [128, 1024], F32) as pqc:
        for hh in range(2):
            stg_free = None
            for ti, dst in enumerate((qtm, ktm, vtm)):
                for cc2 in range(2):
                    for (stg, pbase) in ((stgA, 0), (stgB, 512)):
                        for r in range(2):
                            row0 = (r * CCq + ti * 1024 + pbase + hh * 256 +
                                    cc2 * 128)
                            wts = [stg_free]
                            if (hh == 0 and ti == 0 and cc2 == 0 and
                                    pbase == 0 and r == 0):
                                wts.append((s_cc, 3))
                            p.op("sync", lambda e, stg=stg, r=r, row0=row0:
                                 e.dma_start(
                                    out=stg[:, r * TOK:(r + 1) * TOK],
                                    in_=ccq_out[row0:row0 + 128, :]),
                                 waits=wts, incs=[(ld_in, 16)])
                            nld["n"] += 16
                    LDS = nld["n"]
                    for gb in range(4):
                        for gi in range(8):
                            g = gb * 8 + gi
                            r = g // 16
                            t0 = (g % 16) * 128
                            wts = []
                            if gi == 0:
                                wts = [conv_ev]
                                if gb == 0:
                                    wts.append((ld_in, LDS))
                            p.op("tensor", lambda e, gi=gi, r=r, t0=t0:
                                 e.matmul(pqc[:, gi * 128:(gi + 1) * 128],
                                          stgA[:, r * TOK + t0:
                                               r * TOK + t0 + 128],
                                          selm_sb[:, 0:128],
                                          start=True, stop=False),
                                 waits=wts)
                            p.op("tensor", lambda e, gi=gi, r=r, t0=t0:
                                 e.matmul(pqc[:, gi * 128:(gi + 1) * 128],
                                          stgB[:, r * TOK + t0:
                                               r * TOK + t0 + 128],
                                          selm_sb[:, 128:256],
                                          start=False, stop=True),
                                 incs=([(s_mm, 1)] if gi == 7 else []))
                        mmc["n"] += 1
                        cwts = [(s_mm, mmc["n"])]
                        if gb == 0 and cc2 == 0 and ti == 0:
                            cwts.append(head_free)
                        conv_ev = p.op("scalar",
                             lambda e, dst=dst, gb=gb, cc2=cc2:
                             e.copy(
                                dst.rearrange("p (g cx) -> p g cx", cx=256)
                                   [:, gb * 8:(gb + 1) * 8,
                                    cc2 * 128:(cc2 + 1) * 128],
                                pqc[:, 0:1024]
                                   .rearrange("p (g cx) -> p g cx", cx=128)),
                             waits=cwts)
                    stg_free = (s_mm, mmc["n"])
            sm_sig = None
            t3 = smT[:].rearrange("p (g o) -> p g o", o=1)
            for (dstb, base) in ((blg[0], 0), (blg[1], 4), (ggs, 8),
                                 (hhs, 12)):
                d3 = dstb[:].rearrange("p (g o) -> p g o", o=1)
                a3 = smv[:, :, base + hh:base + hh + 1]
                b3 = smv[:, :, base + 2 + hh:base + 2 + hh + 1]
                p.op("gpsimd", lambda e, d3=d3, a3=a3: e.tensor_scalar(
                        d3, a3, mflc[:, 0:1], None, ALU.mult),
                     waits=[(ld_sm, LDSM)] + ([head_free] if base == 0 else []))
                p.op("gpsimd", lambda e, b3=b3: e.tensor_scalar(
                        t3, b3, mflc[:, 1:2], None, ALU.mult))
                sm_sig = p.op("gpsimd", lambda e, d3=d3: e.tensor_add(
                        d3, d3, t3))

            for (ti, tsr) in ((0, qtm), (1, ktm)):
                for g in range(NG):
                    p.op("vector", lambda e, tsr=tsr, g=g: e.tensor_mul(
                            T_vec[:, 0:256], tsr[:, 256 * g:256 * (g + 1)],
                            tsr[:, 256 * g:256 * (g + 1)]),
                         waits=([sm_sig, conv_ev, mz_sig, (ld_sm, LDSM),
                                 (s_st[0], 16 * nstp[0]),
                                 (s_st[1], 16 * nstp[1])]
                                if (ti == 0 and g == 0) else []))
                    p.op("vector", lambda e, ti=ti, g=g: e.tensor_reduce(
                            nrm[:, ti * NG + g:ti * NG + g + 1],
                            T_vec[:, 0:256], AX.X, ALU.add))
            ns = p.op("vector", lambda e: e.tensor_scalar(
                    nrm[:], nrm[:], 1e-12, None, ALU.add))
            p.op("scalar", lambda e: e.activation(nrm[:], nrm[:], AF.Ln),
                 waits=[ns])
            rs = p.op("scalar", lambda e: e.activation(nrm[:], nrm[:], AF.Exp,
                                                       scale=-0.5))
            sc_sig = None
            for (ti, tsr) in ((0, qtm), (1, ktm)):
                for g in range(NG):
                    sc_sig = p.op("vector", lambda e, ti=ti, tsr=tsr, g=g:
                         e.tensor_scalar(
                            tsr[:, 256 * g:256 * (g + 1)],
                            tsr[:, 256 * g:256 * (g + 1)],
                            nrm[:, ti * NG + g:ti * NG + g + 1], None, ALU.mult),
                         waits=([rs] if (ti == 0 and g == 0) else ()))
            p.op("vector", lambda e: e.tensor_scalar(nblg[0][:], blg[0][:],
                                                     -1.0, None, ALU.mult),
                 waits=[sm_sig])
            nb_sig = p.op("vector", lambda e: e.tensor_scalar(
                    nblg[1][:], blg[1][:], -1.0, None, ALU.mult))
            W0 = gwk
            p.op("vector", lambda e, hh=hh: e.tensor_scalar(
                    W0[:, 0:NG], ggs[:], bghs[:, 2 * hh:2 * hh + 1], None,
                    ALU.add),
                 waits=[sm_sig])
            p.op("vector", lambda e, hh=hh: e.tensor_scalar(
                    W0[:, NG:2 * NG], hhs[:], bghs[:, 2 * hh + 1:2 * hh + 2],
                    None, ALU.add))
            p.op("vector", lambda e: e.tensor_max(W0[:, 2 * NG:3 * NG],
                                                  W0[:, 0:NG],
                                                  W0[:, NG:2 * NG]))
            p.op("vector", lambda e: e.tensor_scalar(
                    W0[:, 2 * NG:3 * NG], W0[:, 2 * NG:3 * NG], 0.0, None,
                    ALU.max))
            p.op("vector", lambda e: e.tensor_sub(W0[:, 0:NG], W0[:, 0:NG],
                                                  W0[:, 2 * NG:3 * NG]))
            sb_sig = p.op("vector", lambda e: e.tensor_sub(
                    W0[:, NG:2 * NG], W0[:, NG:2 * NG], W0[:, 2 * NG:3 * NG]))
            p.op("scalar", lambda e: e.activation(W0[:, 3 * NG:4 * NG],
                                                  W0[:, 0:NG], AF.Exp),
                 waits=[sb_sig])
            p.op("scalar", lambda e: e.activation(W0[:, 4 * NG:5 * NG],
                                                  W0[:, NG:2 * NG], AF.Exp))
            ex_sig = p.op("scalar", lambda e: e.activation(
                    W0[:, 5 * NG:6 * NG], W0[:, 2 * NG:3 * NG], AF.Exp,
                    scale=-1.0))
            p.op("vector", lambda e: e.tensor_add(W0[:, 6 * NG:7 * NG],
                                                  W0[:, 3 * NG:4 * NG],
                                                  W0[:, 4 * NG:5 * NG]),
                 waits=[ex_sig])
            p.op("vector", lambda e: e.tensor_add(W0[:, 6 * NG:7 * NG],
                                                  W0[:, 6 * NG:7 * NG],
                                                  W0[:, 5 * NG:6 * NG]))
            p.op("vector", lambda e: e.reciprocal(W0[:, 7 * NG:8 * NG],
                                                  W0[:, 6 * NG:7 * NG]))
            p.op("vector", lambda e: e.tensor_mul(wlg[0][:],
                                                  W0[:, 3 * NG:4 * NG],
                                                  W0[:, 7 * NG:8 * NG]))
            gate_sig = p.op("vector", lambda e: e.tensor_mul(
                    wlg[1][:], W0[:, 4 * NG:5 * NG], W0[:, 7 * NG:8 * NG]))

            with nc.psum_tensor(f"d_pq{hh}", [128, 1024], BF16) as pq, \
                 nc.psum_tensor(f"d_kk_ps{hh}", [128, 512], F32) as kk_ps:
                ev_sig = None
                for (tsr, fmt) in ((qtm, q_fm), (ktm, k_fm)):
                    for g in range(NG):
                        par = g % 2
                        for cc in range(2):
                            wts = []
                            if cc == 0:
                                wts = [sc_sig, ev_sig]
                            p.op("tensor", lambda e, tsr=tsr, g=g, cc=cc,
                                 par=par:
                                 e.transpose(pq[:, par * 256 + cc * 128:
                                                par * 256 + (cc + 1) * 128],
                                             tsr[:, 256 * g + 128 * cc:
                                                 256 * g + 128 * (cc + 1)],
                                             c.ident_bf),
                                 waits=wts,
                                 incs=([(s_tq, 1)] if cc == 1 else []))
                        tqc["n"] += 1
                        for cc in range(2):
                            ev_sig = p.op("vector",
                                 lambda e, fmt=fmt, g=g, cc=cc, par=par:
                                 e.tensor_copy(
                                     fmt[:, cc * L4L + 128 * g:
                                         cc * L4L + 128 * (g + 1)],
                                     pq[:, par * 256 + cc * 128:
                                        par * 256 + (cc + 1) * 128]),
                                 waits=[(s_tq, tqc["n"])])
                t0_sig = None
                for g in range(NG):
                    for cc in range(2):
                        wts = []
                        if cc == 0:
                            wts = [ev_sig, nb_sig, t0_sig]
                        p.op("tensor", lambda e, g=g, cc=cc: e.matmul(
                                kk_ps[:, 0:128],
                                k_fm[:, cc * L4L + 128 * g:
                                     cc * L4L + 128 * (g + 1)],
                                k_fm[:, cc * L4L + 128 * g:
                                     cc * L4L + 128 * (g + 1)],
                                start=(cc == 0), stop=(cc == 1)),
                             waits=wts,
                             incs=([(s_mm, 1)] if cc == 1 else []))
                    mmc["n"] += 1
                    p.op("vector", lambda e: e.tensor_mul(
                            kkm_sb[:], kk_ps[:, 0:128], MS_BD),
                         waits=[(s_mm, mmc["n"])])
                    for s in range(2):
                        t0_sig = p.op("vector", lambda e, g=g, s=s:
                             e.tensor_scalar(
                                T0F[s][:, 128 * g:128 * (g + 1)], kkm_sb[:],
                                nblg[s][:, g:g + 1], None, ALU.mult))
                prev_tv_done = None
                for s in range(2):
                    p.op("sync", lambda e, s=s: e.dma_start(out=scr1,
                                                            in_=T0F[s][:]),
                         waits=[t0_sig, shuf_done], incs=[(ld_sh, 16)])
                    nsh["n"] += 16
                    WSC = nsh["n"]
                    for a in range(4):
                        p.op("sync", lambda e, a=a: e.dma_start(
                                out=T_vec[32 * a:32 * (a + 1), :]
                                    .rearrange("p (r cc) -> p r cc", r=32),
                                in_=scr1.rearrange(
                                    "(A r) (g blk cc) -> A blk g r cc",
                                    A=4, blk=4, cc=32)[a, a]),
                             waits=([prev_tv_done, (ld_sh, WSC)] if a == 0
                                    else [(ld_sh, WSC)]),
                             incs=[(ld_sh, 16)])
                        nsh["n"] += 16
                    rec_sig = None
                    first_rec = [(ld_sh, nsh["n"])]
                    for i in range(1, 32):
                        for k2 in range(i):
                            if i == 1:
                                src_out = T_vec[:, 32:64]
                                in1 = T_vec[:, 32:64]
                            elif k2 == 0:
                                src_out = acc32[:]
                                in1 = T_vec[:, 32 * i:32 * (i + 1)]
                            elif k2 == i - 1:
                                src_out = T_vec[:, 32 * i:32 * (i + 1)]
                                in1 = acc32[:]
                            else:
                                src_out = acc32[:]
                                in1 = acc32[:]
                            rec_sig = p.op("vector",
                                 lambda e, i=i, k2=k2, src_out=src_out, in1=in1:
                                 e.scalar_tensor_tensor(
                                     src_out, T_vec[:, 32 * k2:32 * (k2 + 1)],
                                     T_vec[:, 32 * i + k2:32 * i + k2 + 1],
                                     in1, ALU.mult, ALU.add),
                                 waits=(first_rec if (i == 1 and k2 == 0)
                                        else ()))
                            first_rec = []
                    p.op("vector", lambda e: e.tensor_scalar(
                            T_vec[:, 0:1024:33], T_vec[:, 0:1024:33], 1.0,
                            None, ALU.add))
                    prev_tv_done = p.op("vector", lambda e: e.tensor_copy(
                            T_vt[:].rearrange("p (cc r) -> p cc r", cc=32),
                            T_vec[:].rearrange("p (r cc) -> p r cc", r=32)
                                    .rearrange("p r cc -> p cc r")))
                    tb_sig = p.op("vector", lambda e: e.tensor_copy(
                            T_bfs[:], T_vt[:]))
                    p.op("sync", lambda e: e.dma_start(out=scr2, in_=T_bfs[:]),
                         waits=[tb_sig], incs=[(ld_sh, 16)])
                    nsh["n"] += 16
                    WSC2 = nsh["n"]
                    for a in range(4):
                        p.op("sync", lambda e, s=s, a=a: e.dma_start(
                                out=TB[s][32 * a:32 * (a + 1), :]
                                    .rearrange("k (g blk r) -> k g blk r",
                                               blk=4, r=32)[:, :, a, :],
                                in_=scr2.rearrange("(A g) (k r) -> A g k r",
                                                   A=4, r=32)[a]
                                        .rearrange("g k r -> k g r")),
                             waits=[(ld_sh, WSC2), mz_sig],
                             incs=[(ld_sh, 16)])
                        nsh["n"] += 16
                    shuf_done = (ld_sh, nsh["n"])
            SHUF = (ld_sh, nsh["n"])

            p.op("vector", lambda e: e.memset(S_bf[0][:], 0.0),
                 waits=[head_free])
            p.op("vector", lambda e: e.memset(S_bf[1][:], 0.0))
            p.op("vector", lambda e: e.memset(S_sb[0][:], 0.0))
            sbz = p.op("vector", lambda e: e.memset(S_sb[1][:], 0.0))
            with nc.psum_tensor(f"d_sd_l{hh}", [128, 512], F32) as sd_l, \
                 nc.psum_tensor(f"d_sd_g{hh}", [128, 512], F32) as sd_g, \
                 nc.psum_tensor(f"d_u_l{hh}", [128, 512], F32) as u_l, \
                 nc.psum_tensor(f"d_u_g{hh}", [128, 512], F32) as u_g, \
                 nc.psum_tensor(f"d_o_ps{hh}", [128, 512], F32) as o_ps, \
                 nc.psum_tensor(f"d_gram{hh}", [128, 512], F32) as gram:
                sd_ps = [sd_l, sd_g]
                u_ps = [u_l, u_g]
                vk_free = None
                wm_free = None
                gram_free = None
                sbf_sig = {0: sbz, 1: sbz}
                ext_sigs = {}
                o_ev = {0: None, 1: None}
                for g in range(NG):
                    p.op("vector", lambda e, g=g: e.tensor_scalar(
                            vs_sb[:, 0:256], vtm[:, 256 * g:256 * (g + 1)],
                            blg[0][:, g:g + 1], None, ALU.mult),
                         waits=[vk_free, gate_sig, SHUF])
                    p.op("vector", lambda e, g=g: e.tensor_scalar(
                            vs_sb[:, 256:512], vtm[:, 256 * g:256 * (g + 1)],
                            blg[1][:, g:g + 1], None, ALU.mult))
                    p.op("vector", lambda e, g=g: e.tensor_scalar(
                            kb_sb[:, 0:256], ktm[:, 256 * g:256 * (g + 1)],
                            nblg[0][:, g:g + 1], None, ALU.mult))
                    vk_sig = p.op("vector", lambda e, g=g: e.tensor_scalar(
                            kb_sb[:, 256:512], ktm[:, 256 * g:256 * (g + 1)],
                            nblg[1][:, g:g + 1], None, ALU.mult))
                    for s in range(2):
                        p.op("tensor", lambda e, s=s, g=g: e.matmul(
                                u_ps[s][:, 0:256], TB[s][:, 128 * g:128 * (g + 1)],
                                vs_sb[:, s * 256:(s + 1) * 256],
                                start=True, stop=False),
                             waits=([vk_sig, SHUF, sbf_sig[0], sbf_sig[1]]
                                    if s == 0 else []))
                    for s in range(2):
                        for cc in range(2):
                            p.op("tensor", lambda e, s=s, cc=cc, g=g: e.matmul(
                                    gram[:, s * 256 + cc * 128:
                                         s * 256 + (cc + 1) * 128],
                                    kb_sb[:, s * 256 + cc * 128:
                                          s * 256 + (cc + 1) * 128],
                                    TB[s][:, 128 * g:128 * (g + 1)],
                                    start=(s == 0 and cc == 0),
                                    stop=(s == 1 and cc == 1)),
                                 waits=([gram_free] if (s == 0 and cc == 0)
                                        else []),
                                 incs=([(s_mm, 1)] if (s == 1 and cc == 1)
                                       else []))
                    mmc["n"] += 1
                    wm_sig = p.op("vector", lambda e: e.tensor_copy(
                            wm_sb[:], gram[:, 0:512]),
                         waits=[(s_mm, mmc["n"])])
                    for s in range(2):
                        for cc in range(2):
                            p.op("tensor", lambda e, s=s, cc=cc: e.matmul(
                                    u_ps[s][:, 0:256],
                                    wm_sb[:, s * 256 + cc * 128:
                                          s * 256 + (cc + 1) * 128],
                                    S_bf[s][:, cc * 256:(cc + 1) * 256],
                                    start=False, stop=(cc == 1)),
                                 waits=([wm_sig] if (s == 0 and cc == 0)
                                        else []))
                    for s in range(2):
                        for cc in range(2):
                            p.op("tensor", lambda e, s=s, cc=cc, g=g: e.matmul(
                                    gram[:, s * 128:(s + 1) * 128],
                                    k_fm[:, cc * L4L + 128 * g:
                                         cc * L4L + 128 * (g + 1)],
                                    wm_sb[:, s * 256 + cc * 128:
                                          s * 256 + (cc + 1) * 128],
                                    start=(s == 0 and cc == 0),
                                    stop=(s == 1 and cc == 1)),
                                 waits=([wm_sig] if (s == 0 and cc == 0)
                                        else []),
                                 incs=([(s_mm, 1)] if (s == 1 and cc == 1)
                                       else []))
                    mmc["n"] += 1
                    wk3 = None
                    for s in range(2):
                        p.op("vector", lambda e, s=s: e.tensor_mul(
                                wkm_sb[s][:], gram[:, s * 128:(s + 1) * 128],
                                MS_BL),
                             waits=([(s_mm, mmc["n"])] if s == 0 else ()))
                        wk3 = p.op("vector", lambda e, s=s: e.tensor_copy(
                                wkm3[s][:, 96:128], wkm_sb[s][:, 96:128]))
                    wm_free = wk3
                    EXT_SLICES = [(0, 32), (32, 64), (64, 96), (64, 128)]
                    for s in range(2):
                        ext_sigs[(0, s)] = p.op("vector",
                             lambda e, s=s, g=g: e.tensor_scalar(
                                Usb[s][0:32, :], u_ps[s][0:32, 0:256],
                                wlg[s][0:32, g:g + 1], None, ALU.mult),
                             waits=[wk3] if s == 0 else [])
                    for a in range(1, 4):
                        for s in range(2):
                            if a < 3:
                                p.op("tensor", lambda e, s=s, a=a: e.matmul(
                                        u_ps[s][32 * a:32 * (a + 1), 0:256],
                                        wkm_sb[s][:, 32 * a:32 * (a + 1)],
                                        Usb[s][:],
                                        start=False, stop=True,
                                        skip_group_check=True),
                                     waits=[ext_sigs[(a - 1, s)]],
                                     incs=([(s_mm, 1)] if s == 1 else []))
                            else:
                                p.op("tensor", lambda e, s=s: e.matmul(
                                        u_ps[s][:, 0:256], wkm3[s][:],
                                        Usb[s][:],
                                        start=False, stop=True,
                                        skip_group_check=True),
                                     waits=[ext_sigs[(2, s)]],
                                     incs=([(s_mm, 1)] if s == 1 else []))
                        mmc["n"] += 1
                        lo, hi = EXT_SLICES[a]
                        for s in range(2):
                            ext_sigs[(a, s)] = p.op("vector",
                                 lambda e, s=s, g=g, lo=lo, hi=hi:
                                 e.tensor_scalar(
                                    Usb[s][lo:hi, :], u_ps[s][lo:hi, 0:256],
                                    wlg[s][lo:hi, g:g + 1], None, ALU.mult),
                                 waits=([(s_mm, mmc["n"])] if s == 0 else []))
                    uprime_done = ext_sigs[(3, 1)]
                    for s in range(2):
                        for cc in range(2):
                            p.op("tensor", lambda e, s=s, cc=cc, g=g: e.matmul(
                                    o_ps[:, 0:256],
                                    q_fm[:, cc * L4L + 128 * g:
                                         cc * L4L + 128 * (g + 1)],
                                    S_bf[s][:, cc * 256:(cc + 1) * 256],
                                    start=(s == 0 and cc == 0), stop=False),
                                 waits=([o_ev[0], o_ev[1]]
                                        if (s == 0 and cc == 0) else []))
                    for cc in range(2):
                        p.op("tensor", lambda e, cc=cc, g=g: e.matmul(
                                gram[:, 0:128],
                                k_fm[:, cc * L4L + 128 * g:
                                     cc * L4L + 128 * (g + 1)],
                                q_fm[:, cc * L4L + 128 * g:
                                     cc * L4L + 128 * (g + 1)],
                                start=(cc == 0), stop=(cc == 1)),
                             waits=([wm_free] if cc == 0 else []),
                             incs=([(s_mm, 1)] if cc == 1 else []))
                    mmc["n"] += 1
                    gr_sig = p.op("vector", lambda e: e.tensor_mul(
                            grm_sb[:], gram[:, 0:128], MS_CA),
                         waits=[(s_mm, mmc["n"])])
                    gram_free = gr_sig
                    for s in range(2):
                        p.op("tensor", lambda e, s=s: e.matmul(
                                o_ps[:, 0:256], grm_sb[:], Usb[s][:],
                                start=False, stop=(s == 1)),
                             waits=([gr_sig, uprime_done] if s == 0 else []),
                             incs=([(s_mm, 1)] if s == 1 else []))
                    mmc["n"] += 1
                    OMM = mmc["n"]
                    for s in range(2):
                        for cc in range(2):
                            p.op("tensor", lambda e, s=s, cc=cc, g=g: e.matmul(
                                    sd_ps[s][:, cc * 256:(cc + 1) * 256],
                                    ktm[:, 256 * g + 128 * cc:
                                        256 * g + 128 * (cc + 1)],
                                    Usb[s][:],
                                    start=(cc == 0), stop=(cc == 1)),
                                 waits=([sbf_sig[s]] if cc == 0 else []),
                                 incs=([(s_mm, 1)] if (s == 1 and cc == 1)
                                       else []))
                    mmc["n"] += 1
                    SMM = mmc["n"]
                    par = g % 2
                    oev = p.op("vector", lambda e, par=par: e.tensor_copy(
                            o_stage[:, par * 256:(par + 1) * 256],
                            o_ps[:, 0:256]),
                         waits=[(s_mm, OMM), (s_st[par], 16 * nstp[par])])
                    o_ev[par] = oev
                    p.op("sync", lambda e, hh=hh, g=g, par=par: e.dma_start(
                            out=cc2_in[(hh * NG + g) * 128:
                                       (hh * NG + g + 1) * 128, :],
                            in_=o_stage[:, par * 256:(par + 1) * 256]),
                         waits=[oev], incs=[(s_st[par], 16)])
                    nstp[par] += 1
                    for s in range(2):
                        p.op("vector", lambda e, s=s: e.tensor_add(
                                S_sb[s][:], S_sb[s][:], sd_ps[s][:, 0:512]),
                             waits=([(s_mm, SMM)] if s == 0 else ()))
                        sbf_sig[s] = p.op("vector", lambda e, s=s: e.tensor_copy(
                                S_bf[s][:], S_sb[s][:]))
                    vk_free = (s_mm, SMM)
                head_free = sbf_sig[1]


def build_fused():
    nc = bass.Bass("TRN2", target_bir_lowering=False, debug=False)
    import ml_dtypes
    dri = lambda n, shp, dt=F32: nc.dram_tensor(n, shp, dt,
                                                kind="ExternalInput").ap()
    a_x = dri("a_x", [D, EXTL])
    a_wq = dri("a_wq", [D, D])
    a_wk = dri("a_wk", [D, D])
    a_wv = dri("a_wv", [D, D])
    a_wo = dri("a_wo", [D, D])
    a_kinit = dri("a_kinit", [128, 8 * 63])
    a_vinit = dri("a_vinit", [128, 8 * 63])
    a_maskc = dri("a_maskc", [128, 256])
    b_w1 = dri("b_w1", [D, 4096])
    b_w2 = dri("b_w2", [4096, D])
    b_b1t = dri("b_b1t", [128, 32])
    b_b2t = dri("b_b2t", [128, 8])
    b_a1h = dri("b_a1h", [128, 8 * 24])
    b_b1c = dri("b_b1c", [24, 4096])
    b_r1h = dri("b_r1h", [128, 8 * 3])
    b_a2h = dri("b_a2h", [128, 32 * 24])
    b_b2c = dri("b_b2c", [24, D])
    b_r2h = dri("b_r2h", [128, 32 * 3])
    c_wq = dri("c_wq", [D, D])
    c_wk = dri("c_wk", [D, D])
    c_wv = dri("c_wv", [D, D])
    c_convs = dri("c_convs", [128, 8 * 12])
    c_wsm = dri("c_wsm", [D, 96])
    d_msk = dri("d_msk", [128, 416])
    d_bgh = dri("d_bgh", [128, 4])
    e_bgh = dri("e_bgh", [128, 8])
    e_nw = dri("e_nw", [128, 1024])
    e_wo = dri("e_wo", [D, D])
    f_w1 = dri("f_w1", [4096, D])
    f_b1t = dri("f_b1t", [128, 32])
    f_w2 = dri("f_w2", [D, 4096])
    f_b2t = dri("f_b2t", [128, 8])
    mflag = dri("mflag", [128, 2])
    selm = dri("selm", [128, 256], BF16)
    y = nc.dram_tensor("y", [D, TOK], F32, kind="ExternalOutput").ap()
    drt = lambda n, shp, dt: nc.dram_tensor(n, shp, dt, kind="Internal").ap()
    xb1 = drt("xb1", [D, TOK], F32)
    xb2e = drt("xb2e", [D, EXT3], F32)
    xb3 = drt("xb3", [D, TOK], F32)
    sv_in = drt("sv_in", [D, 3], F32)
    sv_out = drt("sv_out", [2 * D, 3], F32)
    ccq_in = drt("ccq_in", [3 * D, TOK], BF16)
    ccq_out = drt("ccq_out", [6 * D, TOK], BF16)
    ccs_in = drt("ccs_in", [TOK, 16], F32)
    ccs_out = drt("ccs_out", [2 * TOK, 16], F32)
    cc2_in = drt("cc2_in", [8192, 256], BF16)
    cc2_out = drt("cc2_out", [16384, 256], BF16)
    scr1 = drt("d_scr1", [128, 4096], F32)
    scr2 = drt("d_scr2", [128, 1024], BF16)

    c = Consts(nc)
    e24_np = np.zeros((3, 24), np.float32)
    for e_ in range(3):
        e24_np[e_, e_ * 8:(e_ + 1) * 8] = 1.0
    e24_dram = nc.inline_tensor(e24_np.astype(ml_dtypes.bfloat16),
                                name="e24").ap()

    with ExitStack() as st:
        p = Prog(nc)
        c.materialize(p, st, extra_dram=[
            ("e24", e24_dram, [3, 24], BF16),
            ("mflag", mflag, [128, 2], F32),
            ("selm", selm, [128, 256], BF16)])
        X = Ctx(nc, p, c)
        s_cc = X.S("s_cc")
        s_z = X.S("s_z")
        ld_m = X.S("ld_m")
        svb = st.enter_context(nc.sbuf_tensor("svb", [128, 24], F32))
        zb = st.enter_context(nc.sbuf_tensor("zb", [128, 256], F32))
        p.op("gpsimd", lambda e: e.memset(zb[:], 0.0), setup=True, dma=True,
             incs=[(s_z, 1)])
        for mc in range(KD):
            p.op("sync", lambda e, mc=mc: e.dma_start(
                    out=xb2e[mc * 128:(mc + 1) * 128, 2051:2304],
                    in_=zb[:, 0:253]),
                 waits=[(s_z, 1)], incs=[(c.s_const, 16)], setup=True)
            c.ready_val += 16

        with ExitStack() as stA:
            emit_l0attn(X, stA, a_x, a_wq, a_wk, a_wv, a_wo, a_kinit, a_vinit,
                        a_maskc, xb1)
        X.join()
        with ExitStack() as stB:
            emit_l0ffn(X, stB, xb1, b_w1, b_w2, b_b1t, b_b2t, b_a1h, b_b1c,
                       b_r1h, b_a2h, b_b2c, b_r2h, xb2e, 3)
        X.join()
        p.op("sync", lambda e: e.dma_start(
                out=svb.rearrange("p (cx t) -> p cx t", cx=KD)[:, :, 0:3],
                in_=xb2e[:, 2048:2051]
                    .rearrange("(cx p) t -> cx p t", p=128)
                    .rearrange("cx p t -> p cx t")),
             incs=[(ld_m, 16)])
        X.bump("ld_m")
        p.op("sync", lambda e: e.dma_start(
                out=sv_in.rearrange("(cx p) t -> cx p t", p=128)
                         .rearrange("cx p t -> p cx t"),
                in_=svb.rearrange("p (cx t) -> p cx t", cx=KD)[:, :, 0:3]),
             waits=[(ld_m, X.val("ld_m"))], incs=[(ld_m, 16)])
        X.bump("ld_m")
        p.op("gpsimd", lambda e: e.collective_compute(
                "AllGather", mybir.AluOpType.bypass,
                replica_groups=PAIR_GROUPS,
                ins=[sv_in.opt()], outs=[sv_out.opt()]),
             waits=[(ld_m, X.val("ld_m"))], incs=[(s_cc, 1)], dma=True)
        p.op("sync", lambda e: e.dma_start(
                out=svb.rearrange("p (cx t) -> p cx t", cx=KD)[:, :, 0:3],
                in_=sv_out[0:D, :]
                    .rearrange("(cx p) t -> cx p t", p=128)
                    .rearrange("cx p t -> p cx t")),
             waits=[(s_cc, 1)], incs=[(ld_m, 16)])
        X.bump("ld_m")
        hb = p.op("vector", lambda e: e.tensor_scalar(
                svb[:, 0:24], svb[:, 0:24], c.extra["mflag"][:, 1:2],
                None, ALU.mult),
             waits=[(ld_m, X.val("ld_m"))])
        p.op("sync", lambda e: e.dma_start(
                out=xb2e[:, 0:3]
                    .rearrange("(cx p) t -> cx p t", p=128)
                    .rearrange("cx p t -> p cx t"),
                in_=svb.rearrange("p (cx t) -> p cx t", cx=KD)[:, :, 0:3]),
             waits=[hb], incs=[(ld_m, 16)])
        X.bump("ld_m")
        HALO = (ld_m, X.val("ld_m"))
        with ExitStack() as stC:
            p.op("sync", lambda e: e.nop(), waits=[HALO])
            emit_l1front(X, stC, xb2e, [c_wq, c_wk, c_wv], c_convs, c_wsm,
                         ccq_in, ccs_in)
        jC = X.join()
        p.op("gpsimd", lambda e: e.collective_compute(
                "AllGather", mybir.AluOpType.bypass,
                replica_groups=PAIR_GROUPS,
                ins=[ccq_in.opt()], outs=[ccq_out.opt()]),
             waits=[jC], incs=[(s_cc, 1)], dma=True)
        p.op("gpsimd", lambda e: e.collective_compute(
                "AllGather", mybir.AluOpType.bypass,
                replica_groups=PAIR_GROUPS,
                ins=[ccs_in.opt()], outs=[ccs_out.opt()]),
             incs=[(s_cc, 1)], dma=True)
        with ExitStack() as stD:
            emit_l1scan(X, stD, ccq_out, ccs_out, d_msk, d_bgh, cc2_in,
                        scr1, scr2)
        jD = X.join()
        p.op("gpsimd", lambda e: e.collective_compute(
                "AllGather", mybir.AluOpType.bypass,
                replica_groups=PAIR_GROUPS,
                ins=[cc2_in.opt()], outs=[cc2_out.opt()]),
             waits=[jD], incs=[(s_cc, 1)], dma=True)
        with ExitStack() as stE:
            emit_l1comb(X, stE, cc2_out, ccq_in, ccs_in, e_bgh, e_nw, e_wo,
                        xb2e, 3, xb3)
        X.join()
        with ExitStack() as stF:
            emit_l1ffn(X, stF, xb3, f_w1, f_b1t, f_w2, f_b2t, y)
        X.join()
        p.emit(c.s_const, c.ready_val)
    return nc


def kernel(L0_attn_x, L0_attn_Wq, L0_attn_Wk, L0_attn_Wv, L0_attn_Wo,
           L0_attn_last_k_init, L0_attn_last_v_init,
           L0_ffn_W1, L0_ffn_b1, L0_ffn_A1, L0_ffn_B1, L0_ffn_R1,
           L0_ffn_W2, L0_ffn_b2, L0_ffn_A2, L0_ffn_B2, L0_ffn_R2,
           L1_attn_Wq, L1_attn_Wk, L1_attn_Wv, L1_attn_conv_q, L1_attn_conv_k,
           L1_attn_conv_v, L1_attn_Wb, L1_attn_Wg, L1_attn_bg, L1_attn_Wh,
           L1_attn_bh, L1_attn_norm_w, L1_attn_Wo,
           L1_ffn_w1, L1_ffn_b1, L1_ffn_w2, L1_ffn_b2):
    import ml_dtypes
    f32 = lambda a: np.asarray(a, np.float32)
    nc = build_fused()

    x0 = f32(L0_attn_x)
    xin = np.zeros((NCORES, EXTL, D), np.float32)
    for i in range(NCORES):
        j, s = i // 2, i % 2
        xin[i, 63:63 + TOK] = x0[j, s * TOK:(s + 1) * TOK]
        if s == 1:
            xin[i, 0:63] = x0[j, TOK - 63:TOK]
    xin_t = np.ascontiguousarray(xin.transpose(0, 2, 1))
    ki = np.ascontiguousarray(f32(L0_attn_last_k_init).reshape(63, D).T
                              .reshape(8, 128, 63).transpose(1, 0, 2)
                              .reshape(128, 504))
    vi = np.ascontiguousarray(f32(L0_attn_last_v_init).reshape(63, D).T
                              .reshape(8, 128, 63).transpose(1, 0, 2)
                              .reshape(128, 504))
    maskc = np.full((128, 256), -30000.0, np.float32)
    for r in range(64):
        for qq in range(2):
            maskc[r, qq * 128 + r:qq * 128 + r + 64] = 0.0
            maskc[64 + r, qq * 128 + r:qq * 128 + r + 64] = 0.0
    a1hh = np.ascontiguousarray(
        f32(L0_ffn_A1).transpose(1, 0, 2).reshape(8, 128, 24).transpose(1, 0, 2)
        .reshape(128, 8 * 24))
    a2hh = np.ascontiguousarray(
        f32(L0_ffn_A2).transpose(1, 0, 2).reshape(32, 128, 24).transpose(1, 0, 2)
        .reshape(128, 32 * 24))
    b1cc = np.ascontiguousarray(f32(L0_ffn_B1).reshape(24, 4096))
    b2cc = np.ascontiguousarray(f32(L0_ffn_B2).reshape(24, D))
    r1hh = np.ascontiguousarray(
        f32(L0_ffn_R1).reshape(8, 128, 3).transpose(1, 0, 2).reshape(128, 24))
    r2hh = np.ascontiguousarray(
        f32(L0_ffn_R2).reshape(32, 128, 3).transpose(1, 0, 2).reshape(128, 96))
    b1tt = np.ascontiguousarray(f32(L0_ffn_b1).reshape(32, 128).T)
    b2tt = np.ascontiguousarray(f32(L0_ffn_b2).reshape(8, 128).T)
    cv = np.zeros((128, 8 * 12), np.float32)
    for kcx in range(8):
        for ti, cw in enumerate((L1_attn_conv_q, L1_attn_conv_k,
                                 L1_attn_conv_v)):
            cv[:, kcx * 12 + ti * 4:kcx * 12 + (ti + 1) * 4] = \
                f32(cw)[kcx * 128:(kcx + 1) * 128, :]
    wsm_np = np.zeros((D, 96), np.float32)
    wsm_np[:, 0:8] = f32(L1_attn_Wb)
    wsm_np[:, 32:36] = f32(L1_attn_Wg)
    wsm_np[:, 64:68] = f32(L1_attn_Wh)
    msk = np.zeros((128, 416), np.float32)
    rr = np.arange(128)
    blk = rr // 32
    for i in range(128):
        for jj in range(128):
            if blk[i] == blk[jj] and (i % 32) > (jj % 32):
                msk[i, 32 + jj] = 1.0
            if blk[i] < blk[jj]:
                msk[i, 160 + jj] = 1.0
            if blk[i] < blk[jj] or (blk[i] == blk[jj] and i <= jj):
                msk[i, 288 + jj] = 1.0
    bghv8 = np.zeros((128, 8), np.float32)
    bghv8[:, 0:4] = f32(L1_attn_bg)[None, :]
    bghv8[:, 4:8] = f32(L1_attn_bh)[None, :]
    nwv = np.broadcast_to(f32(L1_attn_norm_w)[None, :], (128, 256))
    nwt = np.ascontiguousarray(np.tile(nwv, (1, 4)))
    fb1tt = np.ascontiguousarray(f32(L1_ffn_b1).reshape(32, 128).T)
    fb2tt = np.ascontiguousarray(f32(L1_ffn_b2).reshape(8, 128).T)

    eye = np.eye(128, dtype=np.float32)
    in_maps = []
    for i in range(NCORES):
        j, s = i // 2, i % 2
        mfl = np.zeros((128, 2), np.float32)
        mfl[:, 0] = 1.0 if s == 0 else 0.0
        mfl[:, 1] = 0.0 if s == 0 else 1.0
        selm = np.zeros((128, 256), np.float32)
        selm[:, 0:128] = eye * (1.0 if s == 0 else 0.0)
        selm[:, 128:256] = eye * (0.0 if s == 0 else 1.0)
        bghv4 = np.zeros((128, 4), np.float32)
        for hh in range(2):
            h = 2 * s + hh
            bghv4[:, 2 * hh] = float(f32(L1_attn_bg)[h])
            bghv4[:, 2 * hh + 1] = float(f32(L1_attn_bh)[h])
        in_maps.append({
            "a_x": xin_t[i], "a_wq": f32(L0_attn_Wq), "a_wk": f32(L0_attn_Wk),
            "a_wv": f32(L0_attn_Wv), "a_wo": f32(L0_attn_Wo),
            "a_kinit": ki, "a_vinit": vi, "a_maskc": maskc,
            "b_w1": f32(L0_ffn_W1), "b_w2": f32(L0_ffn_W2),
            "b_b1t": b1tt, "b_b2t": b2tt, "b_a1h": a1hh, "b_b1c": b1cc,
            "b_r1h": r1hh, "b_a2h": a2hh, "b_b2c": b2cc, "b_r2h": r2hh,
            "c_wq": f32(L1_attn_Wq), "c_wk": f32(L1_attn_Wk),
            "c_wv": f32(L1_attn_Wv), "c_convs": cv, "c_wsm": wsm_np,
            "d_msk": msk, "d_bgh": bghv4,
            "e_bgh": bghv8, "e_nw": nwt, "e_wo": f32(L1_attn_Wo),
            "f_w1": f32(L1_ffn_w1), "f_b1t": fb1tt,
            "f_w2": f32(L1_ffn_w2), "f_b2t": fb2tt,
            "mflag": mfl,
            "selm": selm.astype(ml_dtypes.bfloat16),
        })
    res = run_bass_kernel_spmd(nc, in_maps, core_ids=list(range(NCORES)))
    yv = np.stack([r["y"] for r in res.results])
    return np.ascontiguousarray(yv.transpose(0, 2, 1)).reshape(4, 4096, D)
